# version 01
# baseline (speedup 1.0000x reference)
import numpy as np
import ml_dtypes
from contextlib import ExitStack
import concourse.bass as bass
import concourse.mybir as mybir
from concourse.bass_utils import run_bass_kernel_spmd

F32 = mybir.dt.float32
BF16 = mybir.dt.bfloat16
AF = mybir.ActivationFunctionType
ALU = mybir.AluOpType

NCORES = 8
T = 4096
NST = 8
STW = 512
D = 1024
DC = 8
H = 8
FF = 4096
DEPTH = 6
EPS = 1e-5

ENGS = ("sync", "tensor", "vector", "scalar", "gpsimd")


def ctx_pos(h):
    if h < 6:
        b, j = h // 3, h % 3
        return b, b * 387 + j * 128, b * 387 + 384 + j
    b, j = 2, h - 6
    return 2, 774 + j * 128, 774 + 256 + j


class Emitter:
    def __init__(self):
        self.q = {e: [] for e in ENGS}
        self.cnt = {}
        self.lastwait = {}

    def n(self, sem):
        return self.cnt.get(sem, 0)

    def wait(self, eng, sem, val):
        if val is None or val <= 0:
            return
        key = (eng, sem)
        if self.lastwait.get(key, -1) >= val:
            return
        self.lastwait[key] = val
        self.q[eng].append(("w", sem, val))

    def wait_tok(self, eng, tok):
        if tok is None:
            return
        if isinstance(tok, list):
            for t in tok:
                self.wait_tok(eng, t)
        else:
            self.wait(eng, tok[0], tok[1])

    def op(self, eng, fn, inc=None, amt=1):
        if inc is None:
            self.q[eng].append(("i", fn, None, 0))
            return None
        self.cnt[inc] = self.cnt.get(inc, 0) + amt
        tot = self.cnt[inc]
        self.q[eng].append(("i", fn, inc, amt))
        return (inc, tot)

    def flush(self, block, sems):
        q = {e: list(self.q[e]) for e in ENGS}
        for e in ENGS:
            self.q[e] = []

        def replay(engine, items):
            for item in items:
                if item[0] == "w":
                    engine.wait_ge(sems[item[1]], item[2])
                else:
                    _, fn, inc, amt = item
                    ins = fn(engine)
                    if inc is not None:
                        ins.then_inc(sems[inc], amt)

        if q["sync"]:
            @block.sync
            def _(e, it=q["sync"]):
                replay(e, it)
        if q["tensor"]:
            @block.tensor
            def _(e, it=q["tensor"]):
                replay(e, it)
        if q["vector"]:
            @block.vector
            def _(e, it=q["vector"]):
                replay(e, it)
        if q["scalar"]:
            @block.scalar
            def _(e, it=q["scalar"]):
                replay(e, it)
        if q["gpsimd"]:
            @block.gpsimd
            def _(e, it=q["gpsimd"]):
                replay(e, it)


def build_nc(debug=None):
    nc = bass.Bass("TRN2", target_bir_lowering=False)

    x0 = nc.dram_tensor("x0", [T, D], F32, kind="ExternalInput").ap()
    wq_d = nc.dram_tensor("wq", [DEPTH, D, D], F32, kind="ExternalInput").ap()
    wk_d = nc.dram_tensor("wk", [DEPTH, D, D], F32, kind="ExternalInput").ap()
    wv_d = nc.dram_tensor("wv", [DEPTH, D, D], F32, kind="ExternalInput").ap()
    wo_d = nc.dram_tensor("wo", [DEPTH, D, D], F32, kind="ExternalInput").ap()
    w1_d = nc.dram_tensor("w1", [DEPTH, D, FF], F32, kind="ExternalInput").ap()
    w2_d = nc.dram_tensor("w2", [DEPTH, FF, D], F32, kind="ExternalInput").ap()
    lncp = nc.dram_tensor("lncp", [DEPTH, 128, 40], F32, kind="ExternalInput").ap()
    w1t = nc.dram_tensor("w1t", [D, FF], F32, kind="ExternalInput").ap()
    w2t = nc.dram_tensor("w2t", [D, FF], F32, kind="ExternalInput").ap()
    h1t = nc.dram_tensor("h1t", [FF, FF], F32, kind="ExternalInput").ap()
    h2t = nc.dram_tensor("h2t", [FF, FF], F32, kind="ExternalInput").ap()
    w3t = nc.dram_tensor("w3t", [FF, D], F32, kind="ExternalInput").ap()
    scp = nc.dram_tensor("scp", [128, 160], F32, kind="ExternalInput").ap()
    idf = nc.dram_tensor("idf", [128, 128], F32, kind="ExternalInput").ap()
    cbf = nc.dram_tensor("cbf", [128, 2], BF16, kind="ExternalInput").ap()
    orf = nc.dram_tensor("orf", [1, 128], F32, kind="ExternalInput").ap()

    out = nc.dram_tensor("out", [T, D], F32, kind="ExternalOutput").ap()

    xT = nc.dram_tensor("xT", [NST, DC, 128, STW], F32).ap()
    x0T = nc.dram_tensor("x0T", [NST, DC, 128, STW], F32).ap()
    qT = nc.dram_tensor("qT", [NST, 128, DC * STW], BF16).ap()
    ccin = nc.dram_tensor("ccin", [128, 1032], F32)
    ccout = nc.dram_tensor("ccout", [128, 1032], F32)

    dbg = {}
    if debug is not None:
        dbg["x"] = nc.dram_tensor("dbg_x", [NST, DC, 128, STW], F32, kind="ExternalOutput").ap()
        dbg["q"] = nc.dram_tensor("dbg_q", [NST, 128, DC * STW], BF16, kind="ExternalOutput").ap()
        dbg["ctx"] = nc.dram_tensor("dbg_ctx", [128, 1032], F32, kind="ExternalOutput").ap()

    em = Emitter()

    with ExitStack() as es:
        E = es.enter_context
        semnames = ["dx", "dxo", "dq", "dw", "dcc", "cc", "pe", "act", "dve", "dout", "dmisc"]
        sems = {s: E(nc.semaphore(s)) for s in semnames}

        mreps = E(nc.sbuf_tensor("mreps", [128, 1040], F32))
        tmpA = E(nc.sbuf_tensor("tmpA", [128, 512], F32))
        rows = E(nc.sbuf_tensor("rows", [1, 1024], F32))
        lncols = [E(nc.sbuf_tensor("lncols0", [128, 40], F32)),
                  E(nc.sbuf_tensor("lncols1", [128, 40], F32))]
        id_f = E(nc.sbuf_tensor("id_f", [128, 128], F32))
        ccol = E(nc.sbuf_tensor("ccol", [128, 2], BF16))
        orow = E(nc.sbuf_tensor("orow", [1, 128], F32))
        epsr = E(nc.sbuf_tensor("epsr", [1, 1], F32))

        pc = [E(nc.psum_tensor("pc%d" % i, [128, 512], F32)) for i in range(3)]
        pw = [E(nc.psum_tensor("pw%d" % i, [128, 512], F32)) for i in range(5)]

        block = E(nc.Block())

        inner_es = ExitStack()
        IE = inner_es.enter_context
        xin = [IE(nc.sbuf_tensor("xin0", [128, DC * STW], F32)),
               IE(nc.sbuf_tensor("xin1", [128, DC * STW], F32))]
        qbuf = [IE(nc.sbuf_tensor("qbuf0", [128, DC * STW], BF16)),
                IE(nc.sbuf_tensor("qbuf1", [128, DC * STW], BF16))]
        w12h = IE(nc.sbuf_tensor("w12h", [128, 4 * DC * D], BF16))
        wqkv = IE(nc.sbuf_tensor("wqkv", [128, 3 * DC * D], BF16))
        wosb = IE(nc.sbuf_tensor("wosb", [128, DC * D], BF16))
        xn1T = IE(nc.sbuf_tensor("xn1T", [128, DC * STW], BF16))
        hq = IE(nc.sbuf_tensor("hq", [128, DC * STW], BF16))
        ekb = IE(nc.sbuf_tensor("ekb", [128, D], BF16))
        vbuf = IE(nc.sbuf_tensor("vbuf", [128, D], BF16))
        aoT = hq
        eqs = hq[:, 0:512]
        ctxbf = ekb
        ctxCC = mreps[:, 0:1032]

        def dma(eng, sem_, outap, inap, waits=()):
            for wtok in waits:
                em.wait_tok(eng, wtok)
            return em.op(eng, lambda e, o=outap, i=inap: e.dma_start(out=o, in_=i), inc=sem_, amt=16)

        bank_free = {}

        def pe_use(bank):
            em.wait_tok("tensor", bank_free.get(bank))

        def bank_done(bank, tok):
            bank_free[bank] = tok

        def mm(outap, lhs, rhs, start, stop, inc=None):
            return em.op("tensor",
                         lambda e, o=outap, l=lhs, r=rhs, st=start, sp=stop:
                         e.matmul(o, l, r, start=st, stop=sp, skip_group_check=True),
                         inc=inc)

        def lncol(bi, c, vec):
            return lncols[bi][:, c * 5 + vec: c * 5 + vec + 1]

        xslot = [None] * NST
        qslot = [None] * NST
        xin_free = [None, None]
        qbuf_free = [None, None]

        def ln_stats(xsb, castb, sqb, W=STW):
            ct = em.op("vector", lambda e: e.tensor_copy(castb[:, 0:DC * W], xsb[:, 0:DC * W]), inc="dve")
            em.wait("scalar", "dve", ct[1])
            sq = em.op("scalar", lambda e: e.activation(sqb[:, 0:DC * W], castb[:, 0:DC * W], AF.Square), inc="act")
            em.wait("tensor", "dve", ct[1])
            pe_use("pw0")
            for c in range(DC):
                mm(pw[0][0:1, 0:W], ccol[:, 0:1], castb[:, c * W:(c + 1) * W],
                   start=(c == 0), stop=(c == DC - 1))
            em.wait("tensor", "act", sq[1])
            pe_use("pw1")
            mt = None
            for c in range(DC):
                mt = mm(pw[1][0:1, 0:W], ccol[:, 0:1], sqb[:, c * W:(c + 1) * W],
                        start=(c == 0), stop=(c == DC - 1), inc="pe" if c == DC - 1 else None)
            r_m = pw[0][0:1, 0:W]
            r_s = pw[1][0:1, 0:W]
            em.wait("scalar", "pe", mt[1])
            sqm = em.op("scalar", lambda e: e.activation(rows[0:1, 0:W], r_m, AF.Square), inc="act")
            em.wait("vector", "act", sqm[1])
            vt = em.op("vector", lambda e: e.tensor_tensor(rows[0:1, 512:512 + W], r_s, rows[0:1, 0:W], ALU.subtract), inc="dve")
            em.wait("scalar", "dve", vt[1])
            em.op("scalar", lambda e: e.activation(rows[0:1, 0:W], rows[0:1, 512:512 + W], AF.Ln, bias=epsr[0:1, 0:1], scale=1.0))
            rt = em.op("scalar", lambda e: e.activation(rows[0:1, 512:512 + W], rows[0:1, 0:W], AF.Exp, bias=0.0, scale=-0.5), inc="act")
            em.wait("vector", "act", rt[1])
            mr = em.op("vector", lambda e: e.tensor_tensor(rows[0:1, 0:W], r_m, rows[0:1, 512:512 + W], ALU.mult), inc="dve")
            em.wait("tensor", "dve", mr[1])
            pe_use("pw0")
            pe_use("pw1")
            mm(pw[0][:, 0:W], orow[:, :], rows[0:1, 512:512 + W], start=True, stop=True)
            t2 = mm(pw[1][:, 0:W], orow[:, :], rows[0:1, 0:W], start=True, stop=True, inc="pe")
            em.wait("vector", "pe", t2[1])
            em.op("vector", lambda e: e.tensor_copy(mreps[:, 0:W], pw[0][:, 0:W]))
            c2 = em.op("vector", lambda e: e.tensor_copy(mreps[:, 512:512 + W], pw[1][:, 0:W]), inc="dve")
            bank_done("pw0", ("dve", c2[1]))
            bank_done("pw1", ("dve", c2[1]))
            return c2

        def ln_apply(xsb, c, sink, wcol=None, bcol=None, W=STW):
            xc = xsb[:, c * W:(c + 1) * W]
            em.op("vector", lambda e: e.tensor_tensor(tmpA[:, 0:W], xc, mreps[:, 0:W], ALU.mult))
            em.op("vector", lambda e: e.tensor_tensor(tmpA[:, 0:W], tmpA[:, 0:W], mreps[:, 512:512 + W], ALU.subtract))
            if wcol is not None:
                return em.op("vector", lambda e, w=wcol, b=bcol: e.tensor_scalar(
                    sink, tmpA[:, 0:W], w, b, ALU.mult, ALU.add), inc="dve")
            if callable(sink):
                return sink(tmpA[:, 0:W])
            return em.op("vector", lambda e: e.tensor_copy(sink, tmpA[:, 0:W]), inc="dve")

        def load_wqkv(li):
            gw = ("pe", em.n("pe"))
            dma("gpsimd", "dw", wqkv[:, 0:DC * D].rearrange("p (c w) -> p c w", c=DC),
                wq_d[li].rearrange("(c p) w -> p c w", p=128), waits=[gw])
            dma("gpsimd", "dw", wqkv[:, DC * D:2 * DC * D].rearrange("p (c w) -> p c w", c=DC),
                wk_d[li].rearrange("(c p) w -> p c w", p=128))
            t = dma("gpsimd", "dw", wqkv[:, 2 * DC * D:3 * DC * D].rearrange("p (c w) -> p c w", c=DC),
                    wv_d[li].rearrange("(c p) w -> p c w", p=128))
            return ("dw", t[1])

        def load_wo(li):
            t = dma("gpsimd", "dw", wosb[:, :].rearrange("p (c w) -> p c w", c=DC),
                    wo_d[li].rearrange("(c p) w -> p c w", p=128),
                    waits=[("pe", em.n("pe"))])
            return ("dw", t[1])

        def load_w12h(li, half):
            gw = ("pe", em.n("pe"))
            dma("gpsimd", "dw", w12h[:, 0:2 * DC * D].rearrange("p (c w) -> p c w", c=DC),
                w1_d[li][:, half * 2048:(half + 1) * 2048].rearrange("(c p) w -> p c w", p=128),
                waits=[gw])
            t = dma("gpsimd", "dw", w12h[:, 2 * DC * D:4 * DC * D].rearrange("p (c w) -> p c w", c=16),
                    w2_d[li][half * 2048:(half + 1) * 2048, :].rearrange("(c p) w -> p c w", p=128))
            return ("dw", t[1])

        def load_lncols(li, bi):
            t = dma("sync", "dmisc", lncols[bi][:, :], lncp[li],
                    waits=[("dve", em.n("dve"))])
            return ("dmisc", t[1])

        dma("sync", "dmisc", id_f[:, :], idf)
        dma("sync", "dmisc", ccol[:, :], cbf)
        ct0 = dma("sync", "dmisc", orow[:, :], orf)
        em.op("vector", lambda e: e.memset(epsr[:, :], EPS), inc="dve")
        for eng in ("tensor", "vector", "scalar"):
            em.wait(eng, "dmisc", ct0[1])

        wqkv_rdy = {0: load_wqkv(0)}
        wo_rdy = {0: load_wo(0)}
        w12_rdy = {(0, 0): load_w12h(0, 0)}
        ln_rdy = {0: load_lncols(0, 0)}

        for st in range(NST):
            lt = dma("sync", "dx", xin[0][:, :].rearrange("p (t d) -> p t d", t=4),
                     x0[st * 512:(st + 1) * 512, :].rearrange("(t p) d -> p t d", p=128),
                     waits=[xin_free[0], xin_free[1]])
            em.wait("tensor", "dx", lt[1])
            ctok = None
            for t4 in range(4):
                for c in range(DC):
                    pidx = 2 + ((t4 * DC + c) % 2)
                    pe_use("pw%d" % pidx)
                    tt = em.op("tensor", lambda e, t4=t4, c=c, p=pidx: e.transpose(
                        pw[p][:, 0:128], xin[0][:, t4 * D + c * 128:t4 * D + c * 128 + 128],
                        id_f[:, :]), inc="pe")
                    em.wait("vector", "pe", tt[1])
                    ctok = em.op("vector", lambda e, t4=t4, c=c, p=pidx: e.tensor_copy(
                        xin[1][:, c * 512 + t4 * 128: c * 512 + t4 * 128 + 128], pw[p][:, 0:128]),
                        inc="dve")
                    bank_done("pw%d" % pidx, ("dve", ctok[1]))
            s0 = dma("sync", "dxo", x0T[st].rearrange("c p w -> p c w"),
                     xin[1][:, :].rearrange("p (c w) -> p c w", c=DC),
                     waits=[("dve", ctok[1])])
            ln_stats(xin[1], xn1T, hq)
            atok = None
            for c in range(DC):
                atok = ln_apply(xin[1], c, xin[0][:, c * 512:(c + 1) * 512])
            s1 = dma("sync", "dxo", xT[st].rearrange("c p w -> p c w"),
                     xin[0][:, :].rearrange("p (c w) -> p c w", c=DC),
                     waits=[("dve", atok[1])])
            xslot[st] = ("dxo", s1[1])
            xin_free[0] = [("dxo", s1[1])]
            xin_free[1] = [("dxo", s0[1]), ("dve", atok[1])]

        def sweep_A(li):
            lnb = li % 2
            em.wait_tok("tensor", wqkv_rdy[li])
            em.wait_tok("vector", ln_rdy[li])
            wq_off, wk_off, wv_off = 0, DC * D, 2 * DC * D
            ntq = None
            for st in range(NST):
                xb = st % 2
                qb = st % 2
                lt = dma("sync", "dx", xin[xb][:, :].rearrange("p (c w) -> p c w", c=DC),
                         xT[st].rearrange("c p w -> p c w"),
                         waits=[xin_free[xb], xslot[st]])
                em.wait("vector", "dx", lt[1])
                xsb = xin[xb]
                ln_stats(xsb, xn1T, hq)
                at = None
                for c in range(DC):
                    at = ln_apply(xsb, c, xn1T[:, c * 512:(c + 1) * 512],
                                  wcol=lncol(lnb, c, 0), bcol=lncol(lnb, c, 1))
                em.wait("tensor", "dve", at[1])
                xin_free[xb] = [("dve", at[1])]
                for oc in range(H):
                    pidx = 2 + (oc % 2)
                    pe_use("pw%d" % pidx)
                    qmt = None
                    for c in range(DC):
                        qmt = mm(pw[pidx][:, :],
                                 wqkv[:, wq_off + c * D + oc * 128: wq_off + c * D + oc * 128 + 128],
                                 xn1T[:, c * 512:(c + 1) * 512],
                                 start=(c == 0), stop=(c == DC - 1),
                                 inc="pe" if c == DC - 1 else None)
                    em.wait("scalar", "pe", qmt[1])
                    if ntq is not None:
                        em.wait("scalar", "dve", ntq[1])
                    et = em.op("scalar", lambda e, p=pidx: e.activation(eqs, pw[p][:, :], AF.Exp),
                               inc="act")
                    bank_done("pw%d" % pidx, ("act", et[1]))
                    em.wait("tensor", "act", et[1])
                    pe_use("pw4")
                    zt = mm(pw[4][0:1, 0:512], ccol[:, 1:2], eqs, start=True, stop=True, inc="pe")
                    em.wait("vector", "pe", zt[1])
                    rt = em.op("vector", lambda e: e.reciprocal(rows[0:1, 0:512], pw[4][0:1, 0:512]),
                               inc="dve")
                    bank_done("pw4", ("dve", rt[1]))
                    em.wait("tensor", "dve", rt[1])
                    pe_use("pw4")
                    zr = mm(pw[4][:, :], orow[:, :], rows[0:1, 0:512], start=True, stop=True, inc="pe")
                    em.wait("vector", "pe", zr[1])
                    if oc == 0:
                        em.wait_tok("vector", qbuf_free[qb])
                    ntq = em.op("vector", lambda e, oc=oc, qb=qb: e.tensor_tensor(
                        qbuf[qb][:, oc * 512:(oc + 1) * 512], eqs, pw[4][:, :], ALU.mult),
                        inc="dve")
                    bank_done("pw4", ("dve", ntq[1]))
                qs = dma("sync", "dq", qT[st], qbuf[qb][:, :],
                         waits=[("dve", ntq[1]), qslot[st]])
                qslot[st] = ("dq", qs[1])
                qbuf_free[qb] = [("dq", qs[1])]
                for t4 in range(4):
                    ktoks = []
                    for f in range(2):
                        pe_use("pw%d" % (2 + f))
                        kt = None
                        for c in range(DC):
                            kt = mm(pw[2 + f][:, :],
                                    xn1T[:, c * 512 + t4 * 128: c * 512 + t4 * 128 + 128],
                                    wqkv[:, wk_off + c * D + f * 512: wk_off + c * D + (f + 1) * 512],
                                    start=(c == 0), stop=(c == DC - 1),
                                    inc="pe" if c == DC - 1 else None)
                        ktoks.append(kt)
                    ets = []
                    for f in range(2):
                        em.wait("scalar", "pe", ktoks[f][1])
                        et = em.op("scalar", lambda e, f=f: e.activation(
                            ekb[:, f * 512:(f + 1) * 512], pw[2 + f][:, :], AF.Exp), inc="act")
                        bank_done("pw%d" % (2 + f), ("act", et[1]))
                        ets.append(et)
                    vtoks = []
                    for f in range(2):
                        pe_use("pw%d" % (2 + f))
                        vt = None
                        for c in range(DC):
                            vt = mm(pw[2 + f][:, :],
                                    xn1T[:, c * 512 + t4 * 128: c * 512 + t4 * 128 + 128],
                                    wqkv[:, wv_off + c * D + f * 512: wv_off + c * D + (f + 1) * 512],
                                    start=(c == 0), stop=(c == DC - 1),
                                    inc="pe" if c == DC - 1 else None)
                        vtoks.append(vt)
                    vct = None
                    for f in range(2):
                        em.wait("vector", "pe", vtoks[f][1])
                        vct = em.op("vector", lambda e, f=f: e.tensor_copy(
                            vbuf[:, f * 512:(f + 1) * 512], pw[2 + f][:, :]), inc="dve")
                        bank_done("pw%d" % (2 + f), ("dve", vct[1]))
                    em.wait("tensor", "dve", vct[1])
                    em.wait("tensor", "act", ets[1][1])
                    first = (st == 0 and t4 == 0)
                    last = (st == NST - 1 and t4 == 3)
                    for h in range(H):
                        b, coff, zoff = ctx_pos(h)
                        mm(pc[b][:, coff - b * 387:coff - b * 387 + 128],
                           ekb[:, h * 128:(h + 1) * 128],
                           vbuf[:, h * 128:(h + 1) * 128],
                           start=(first and h in (0, 3, 6)), stop=last)
                        mm(pc[b][:, zoff - b * 387: zoff - b * 387 + 1],
                           ekb[:, h * 128:(h + 1) * 128],
                           ccol[:, 1:2],
                           start=False, stop=last,
                           inc="pe" if (last and h == H - 1) else None)
            em.wait("vector", "pe", em.n("pe"))
            em.op("vector", lambda e: e.tensor_copy(ctxCC[:, 0:387], pc[0][:, 0:387]))
            em.op("vector", lambda e: e.tensor_copy(ctxCC[:, 387:774], pc[1][:, 0:387]))
            ev = em.op("vector", lambda e: e.tensor_copy(ctxCC[:, 774:1032], pc[2][:, 0:258]), inc="dve")
            for b in range(3):
                bank_done("pc%d" % b, ("dve", ev[1]))
            ci = dma("gpsimd", "dcc", ccin[:, :], ctxCC, waits=[("dve", ev[1])])
            em.wait("gpsimd", "dcc", ci[1])
            cct = em.op("gpsimd", lambda e: e.collective_compute(
                "AllReduce", ALU.add,
                replica_groups=[[0, 1, 2, 3], [4, 5, 6, 7]],
                ins=[ccin.ap().opt()], outs=[ccout.ap().opt()]), inc="cc")
            em.wait("gpsimd", "cc", cct[1])
            co = dma("gpsimd", "dcc", ctxCC, ccout[:, :])
            return co

        def sweep_B(li, co):
            lnb = li % 2
            em.wait_tok("tensor", wo_rdy[li])
            em.wait("vector", "dcc", co[1])
            rc = None
            for h in range(H):
                b, coff, zoff = ctx_pos(h)
                rc = em.op("vector", lambda e, z=zoff, h=h: e.reciprocal(
                    tmpA[:, h:h + 1], ctxCC[:, z:z + 1]),
                    inc="dve" if h == H - 1 else None)
            em.wait("vector", "dve", rc[1])
            nt = None
            for h in range(H):
                b, coff, zoff = ctx_pos(h)
                nt = em.op("vector", lambda e, h=h, c0=coff: e.tensor_scalar(
                    ctxbf[:, h * 128:(h + 1) * 128], ctxCC[:, c0:c0 + 128],
                    tmpA[:, h:h + 1], None, ALU.mult), inc="dve")
            em.wait("tensor", "dve", nt[1])
            for st in range(NST):
                xb = st % 2
                qb = st % 2
                lt = dma("sync", "dx", xin[xb][:, :].rearrange("p (c w) -> p c w", c=DC),
                         xT[st].rearrange("c p w -> p c w"),
                         waits=[xin_free[xb], xslot[st]])
                qlt = dma("sync", "dq", qbuf[qb][:, :], qT[st],
                          waits=[qbuf_free[qb], qslot[st]])
                xsb = xin[xb]
                em.wait("tensor", "dq", qlt[1])
                if li == 0 and st == 0:
                    pass
                aot = None
                for h in range(H):
                    bank = "pw4" if h % 2 == 0 else "pc0"
                    pt = pw[4] if h % 2 == 0 else pc[0]
                    pe_use(bank)
                    t = mm(pt[:, :], ctxbf[:, h * 128:(h + 1) * 128],
                           qbuf[qb][:, h * 512:(h + 1) * 512], start=True, stop=True, inc="pe")
                    em.wait("scalar", "pe", t[1])
                    aot = em.op("scalar", lambda e, h=h, pt=pt: e.activation(
                        aoT[:, h * 512:(h + 1) * 512], pt[:, :], AF.Copy), inc="act")
                    bank_done(bank, ("act", aot[1]))
                ao_done = ("pe", em.n("pe"))
                em.wait("tensor", "act", aot[1])
                em.wait("vector", "dx", lt[1])
                for half in range(2):
                    wtoks = []
                    for ds in range(4):
                        d_idx = half * 4 + ds
                        pe_use("pw%d" % ds)
                        t = None
                        for ec in range(DC):
                            t = mm(pw[ds][:, :],
                                   wosb[:, ec * D + d_idx * 128: ec * D + d_idx * 128 + 128],
                                   aoT[:, ec * 512:(ec + 1) * 512],
                                   start=(ec == 0), stop=(ec == DC - 1),
                                   inc="pe" if ec == DC - 1 else None)
                        wtoks.append(t)
                    for ds in range(4):
                        d_idx = half * 4 + ds
                        em.wait("vector", "pe", wtoks[ds][1])
                        em.op("vector", lambda e, ds=ds, d=d_idx, b=lnb: e.tensor_scalar(
                            tmpA[:, :], pw[ds][:, :], lncol(b, d, 4), None, ALU.add))
                        rt = em.op("vector", lambda e, d=d_idx, xsb=xsb: e.tensor_tensor(
                            xsb[:, d * 512:(d + 1) * 512], xsb[:, d * 512:(d + 1) * 512],
                            tmpA[:, :], ALU.add), inc="dve")
                        bank_done("pw%d" % ds, ("dve", rt[1]))
                ln_stats(xsb, xn1T, hq)
                em.wait_tok("vector", [ao_done])
                at = None
                for c in range(DC):
                    at = ln_apply(xsb, c, qbuf[qb][:, c * 512:(c + 1) * 512],
                                  wcol=lncol(lnb, c, 2), bcol=lncol(lnb, c, 3))
                qs = dma("sync", "dq", qT[st], qbuf[qb][:, :], waits=[("dve", at[1])])
                qslot[st] = ("dq", qs[1])
                qbuf_free[qb] = [("dq", qs[1])]
                so = dma("sync", "dxo", xT[st].rearrange("c p w -> p c w"),
                         xsb[:, :].rearrange("p (c w) -> p c w", c=DC),
                         waits=[("dve", at[1])])
                xslot[st] = ("dxo", so[1])
                xin_free[xb] = [("dxo", so[1])]

        def sweep_F(li, half, last=False):
            em.wait_tok("tensor", w12_rdy[(li, half)])
            for st in range(NST):
                xb = st % 2
                qb = st % 2
                lt = dma("sync", "dx", xin[xb][:, :].rearrange("p (c w) -> p c w", c=DC),
                         xT[st].rearrange("c p w -> p c w"),
                         waits=[xin_free[xb], xslot[st]])
                qlt = dma("sync", "dq", qbuf[qb][:, :], qT[st],
                          waits=[qbuf_free[qb], qslot[st]])
                xsb = xin[xb]
                em.wait("tensor", "dq", qlt[1])
                em.wait("vector", "dx", lt[1])
                yt = None
                for Q in range(2):
                    hts = []
                    for oc8 in range(DC):
                        bank = "pw4" if oc8 % 2 == 0 else "pc0"
                        pt = pw[4] if oc8 % 2 == 0 else pc[0]
                        pe_use(bank)
                        t = None
                        for c in range(DC):
                            t = mm(pt[:, :],
                                   w12h[:, c * 2048 + Q * 1024 + oc8 * 128: c * 2048 + Q * 1024 + oc8 * 128 + 128],
                                   qbuf[qb][:, c * 512:(c + 1) * 512],
                                   start=(c == 0), stop=(c == DC - 1),
                                   inc="pe" if c == DC - 1 else None)
                        em.wait("scalar", "pe", t[1])
                        ht = em.op("scalar", lambda e, oc8=oc8, pt=pt: e.activation(
                            hq[:, oc8 * 512:(oc8 + 1) * 512], pt[:, :], AF.Lrelu), inc="act")
                        bank_done(bank, ("act", ht[1]))
                        hts.append(ht)
                    em.wait("tensor", "act", hts[-1][1])
                    for dh in range(2):
                        ytoks = []
                        for ds in range(4):
                            d_idx = dh * 4 + ds
                            pe_use("pw%d" % ds)
                            t = None
                            for oc8 in range(DC):
                                woff = 2 * DC * D + (Q * DC + oc8) * D + d_idx * 128
                                t = mm(pw[ds][:, :],
                                       w12h[:, woff: woff + 128],
                                       hq[:, oc8 * 512:(oc8 + 1) * 512],
                                       start=(oc8 == 0), stop=(oc8 == DC - 1),
                                       inc="pe" if oc8 == DC - 1 else None)
                            ytoks.append(t)
                        for ds in range(4):
                            d_idx = dh * 4 + ds
                            em.wait("vector", "pe", ytoks[ds][1])
                            yt = em.op("vector", lambda e, ds=ds, d=d_idx, xsb=xsb: e.tensor_tensor(
                                xsb[:, d * 512:(d + 1) * 512], xsb[:, d * 512:(d + 1) * 512],
                                pw[ds][:, :], ALU.add), inc="dve")
                            bank_done("pw%d" % ds, ("dve", yt[1]))
                qbuf_free[qb] = [("pe", em.n("pe"))]
                if not last:
                    so = dma("sync", "dxo", xT[st].rearrange("c p w -> p c w"),
                             xsb[:, :].rearrange("p (c w) -> p c w", c=DC),
                             waits=[("dve", yt[1])])
                    xslot[st] = ("dxo", so[1])
                    xin_free[xb] = [("dxo", so[1])]
                else:
                    ln_stats(xsb, xn1T, hq)
                    ob = (st + 1) % 2
                    l0 = dma("sync", "dx", xin[ob][:, :].rearrange("p (c w) -> p c w", c=DC),
                             x0T[st].rearrange("c p w -> p c w"),
                             waits=[xin_free[ob]])
                    em.wait("vector", "dx", l0[1])
                    at = None
                    for c in range(DC):
                        def mk(c=c, ob=ob):
                            def sink(tB):
                                return em.op("vector", lambda e: e.tensor_tensor(
                                    xin[ob][:, c * 512:(c + 1) * 512], tB,
                                    xin[ob][:, c * 512:(c + 1) * 512], ALU.add), inc="dve")
                            return sink
                        at = ln_apply(xsb, c, mk())
                    so = dma("sync", "dxo", xT[st].rearrange("c p w -> p c w"),
                             xin[ob][:, :].rearrange("p (c w) -> p c w", c=DC),
                             waits=[("dve", at[1])])
                    xslot[st] = ("dxo", so[1])
                    xin_free[xb] = [("dve", at[1])]
                    xin_free[ob] = [("dxo", so[1])]

        if isinstance(debug, tuple):
            n_layers, stop_stage = debug[1] + 1, debug[0]
        elif debug == "entry":
            n_layers, stop_stage = 0, None
        else:
            n_layers, stop_stage = DEPTH, None

        for li in range(n_layers):
            co = sweep_A(li)
            if stop_stage == "A" and li == n_layers - 1:
                break
            sweep_B(li, co)
            if stop_stage == "B" and li == n_layers - 1:
                break
            if li + 1 < DEPTH:
                wqkv_rdy[li + 1] = load_wqkv(li + 1)
                ln_rdy[li + 1] = load_lncols(li + 1, (li + 1) % 2)
            sweep_F(li, 0)
            if li + 1 < DEPTH:
                wo_rdy[li + 1] = load_wo(li + 1)
            w12_rdy[(li, 1)] = load_w12h(li, 1)
            sweep_F(li, 1, last=(li == DEPTH - 1))
            if li + 1 < DEPTH:
                w12_rdy[(li + 1, 0)] = load_w12h(li + 1, 0)

        if debug is not None:
            em.wait("sync", "dxo", em.n("dxo"))
            em.wait("sync", "dq", em.n("dq"))
            dma("sync", "dout", dbg["x"], xT)
            dma("sync", "dout", dbg["q"], qT)
            em.wait("gpsimd", "dcc", em.n("dcc"))
            dma("gpsimd", "dout", dbg["ctx"], ccout[:, :])
            em.wait("sync", "dout", em.n("dout"))
            em.wait("gpsimd", "dout", em.n("dout"))
            em.flush(block, sems)
            inner_es.close()
        else:
            em.flush(block, sems)
            inner_es.close()
            emit_outer(nc, em, sems, dict(
                dma=dma, mm=mm, pe_use=pe_use, bank_done=bank_done,
                ln_stats=ln_stats, ln_apply=ln_apply,
                xT=xT, pw=pw, pc=pc, rows=rows, id_f=id_f, ccol=ccol,
                orow=orow, epsr=epsr, tmpA=tmpA, mreps=mreps, out=out,
                xslot=xslot, w1t=w1t, w2t=w2t, h1t=h1t, h2t=h2t,
                w3t=w3t, scp=scp), E, em)
            em.wait("sync", "dout", em.n("dout"))
            em.flush(block, sems)

    return nc


def emit_outer(nc, em, sems, L, E, _em):
    dma, mm, pe_use, bank_done = L["dma"], L["mm"], L["pe_use"], L["bank_done"]
    ln_stats, ln_apply = L["ln_stats"], L["ln_apply"]
    xT, pw, pc = L["xT"], L["pw"], L["pc"]
    id_f, ccol = L["id_f"], L["ccol"]
    out = L["out"]
    xslot = L["xslot"]
    w1t, w2t, h1t, h2t, w3t, scp = (L[k] for k in ("w1t", "w2t", "h1t", "h2t", "w3t", "scp"))

    ox = E(nc.sbuf_tensor("ox", [128, DC * STW], F32))
    hgrp = E(nc.sbuf_tensor("hgrp", [128, 32 * 1024], BF16))
    ggrp = E(nc.sbuf_tensor("ggrp", [128, 32 * 1024], BF16))
    wsl = [E(nc.sbuf_tensor("wsl0", [128, 4096], BF16)),
           E(nc.sbuf_tensor("wsl1", [128, 4096], BF16))]
    zT = [E(nc.sbuf_tensor("zT0", [128, 4096], BF16)),
          E(nc.sbuf_tensor("zT1", [128, 4096], BF16))]
    fsb = E(nc.sbuf_tensor("fsb", [128, 2048], F32))
    outtm = E(nc.sbuf_tensor("outtm", [128, 1024], F32))
    scols = E(nc.sbuf_tensor("scols", [128, 160], F32))
    stmp = E(nc.sbuf_tensor("stmp", [128, 1024], BF16))

    sct = dma("sync", "dmisc", scols[:, :], scp)
    for eng in ("tensor", "vector", "scalar"):
        em.wait(eng, "dmisc", sct[1])

    def scol(oc, vec):
        return scols[:, oc * 5 + vec: oc * 5 + vec + 1]

    wsl_free = [None, None]
    outtm_free = [None]

    def load_slice(src, oc, buf, width):
        nch = width // 128
        return dma("gpsimd", "dw",
                   wsl[buf][:, 0:width].rearrange("p (c w) -> p c w", c=nch),
                   src[:, oc * 128:(oc + 1) * 128].rearrange("(c p) w -> p c w", p=128),
                   waits=[wsl_free[buf]])

    ox_free = None
    for g in range(4):
        sts = [2 * g, 2 * g + 1]
        for k, st in enumerate(sts):
            lt = dma("sync", "dx", ox[:, :].rearrange("p (c w) -> p c w", c=DC),
                     xT[st].rearrange("c p w -> p c w"),
                     waits=[ox_free, xslot[st]])
            em.wait("vector", "dx", lt[1])
            ln_stats(ox, hgrp[:, 0:4096], hgrp[:, 4096:8192])
            at = None
            for c in range(DC):
                at = ln_apply(ox, c, zT[k][:, c * 512:(c + 1) * 512])
            ox_free = [("dve", at[1])]
        em.wait("tensor", "dve", em.n("dve"))
        ht = None
        for oc in range(32):
            buf = oc % 2
            dma("gpsimd", "dw",
                wsl[buf][:, 0:1024].rearrange("p (c w) -> p c w", c=DC),
                w1t[:, oc * 128:(oc + 1) * 128].rearrange("(c p) w -> p c w", p=128),
                waits=[wsl_free[buf]])
            l2 = dma("gpsimd", "dw",
                     wsl[buf][:, 1024:2048].rearrange("p (c w) -> p c w", c=DC),
                     w2t[:, oc * 128:(oc + 1) * 128].rearrange("(c p) w -> p c w", p=128))
            em.wait("tensor", "dw", l2[1])
            for k in range(2):
                pe_use("pw2")
                for c in range(DC):
                    mm(pw[2][:, :], wsl[buf][:, c * 128:(c + 1) * 128],
                       zT[k][:, c * 512:(c + 1) * 512], start=(c == 0), stop=(c == DC - 1))
                pe_use("pw3")
                tb = None
                for c in range(DC):
                    tb = mm(pw[3][:, :], wsl[buf][:, 1024 + c * 128: 1024 + (c + 1) * 128],
                            zT[k][:, c * 512:(c + 1) * 512], start=(c == 0), stop=(c == DC - 1),
                            inc="pe" if c == DC - 1 else None)
                em.wait("scalar", "pe", tb[1])
                if ht is not None:
                    em.wait("scalar", "dve", ht[1])
                sa = em.op("scalar", lambda e, oc=oc: e.activation(
                    stmp[:, 0:512], pw[2][:, :], AF.Silu, bias=0.0, scale=scol(oc, 0)), inc="act")
                sb2 = em.op("scalar", lambda e, oc=oc: e.activation(
                    stmp[:, 512:1024], pw[3][:, :], AF.Copy, bias=0.0, scale=scol(oc, 1)), inc="act")
                bank_done("pw2", ("act", sa[1]))
                bank_done("pw3", ("act", sb2[1]))
                em.wait("vector", "act", sb2[1])
                ht = em.op("vector", lambda e, oc=oc, k=k: e.tensor_tensor(
                    hgrp[:, oc * 1024 + k * 512: oc * 1024 + (k + 1) * 512],
                    stmp[:, 0:512], stmp[:, 512:1024], ALU.mult), inc="dve")
            wsl_free[buf] = [("dve", ht[1])]

        def hstage(src_grp, dst_grp, wmat, svec_idx, src_sem):
            em.wait("tensor", src_sem, em.n(src_sem))
            for oc in range(32):
                buf = oc % 2
                ls = load_slice(wmat, oc, buf, 4096)
                em.wait("tensor", "dw", ls[1])
                last = None
                for k in range(2):
                    bank = "pw2" if k == 0 else "pc0"
                    pt = pw[2] if k == 0 else pc[0]
                    pe_use(bank)
                    t = None
                    for ic in range(32):
                        t = mm(pt[:, :], wsl[buf][:, ic * 128:(ic + 1) * 128],
                               src_grp[:, ic * 1024 + k * 512: ic * 1024 + (k + 1) * 512],
                               start=(ic == 0), stop=(ic == 31),
                               inc="pe" if ic == 31 else None)
                    em.wait("scalar", "pe", t[1])
                    gt = em.op("scalar", lambda e, oc=oc, k=k, pt=pt, dst=dst_grp: e.activation(
                        dst[:, oc * 1024 + k * 512: oc * 1024 + (k + 1) * 512],
                        pt[:, :], AF.Gelu, bias=0.0, scale=scol(oc, svec_idx)), inc="act")
                    bank_done(bank, ("act", gt[1]))
                    last = gt
                wsl_free[buf] = [("act", last[1])]

        hstage(hgrp, ggrp, h1t, 2, "dve")
        hstage(ggrp, hgrp, h2t, 3, "act")
        em.wait("tensor", "act", em.n("act"))
        for k, st in enumerate(sts):
            lt = dma("sync", "dx", ox[:, :].rearrange("p (c w) -> p c w", c=DC),
                     xT[st].rearrange("c p w -> p c w"),
                     waits=[ox_free, xslot[st]])
            for hf in range(2):
                ft = None
                for oc8 in range(DC):
                    buf = oc8 % 2
                    ls = load_slice(w3t, oc8, buf, 4096)
                    em.wait("tensor", "dw", ls[1])
                    bank = "pw2" if oc8 % 2 == 0 else "pc0"
                    pt = pw[2] if oc8 % 2 == 0 else pc[0]
                    pe_use(bank)
                    t = None
                    for ic in range(32):
                        t = mm(pt[:, 0:256], wsl[buf][:, ic * 128:(ic + 1) * 128],
                               hgrp[:, ic * 1024 + k * 512 + hf * 256: ic * 1024 + k * 512 + hf * 256 + 256],
                               start=(ic == 0), stop=(ic == 31),
                               inc="pe" if ic == 31 else None)
                    wsl_free[buf] = [("pe", t[1])]
                    em.wait("scalar", "pe", t[1])
                    ft = em.op("scalar", lambda e, oc8=oc8, pt=pt: e.activation(
                        fsb[:, oc8 * 256:(oc8 + 1) * 256], pt[:, 0:256], AF.Copy, bias=0.0,
                        scale=scol(oc8, 4)), inc="act")
                    bank_done(bank, ("act", ft[1]))
                em.wait("vector", "act", ft[1])
                ln_stats(fsb, ggrp[:, 0:2048], ggrp[:, 2048:4096], W=256)
                em.wait("vector", "dx", lt[1])
                ot = None
                for c in range(DC):
                    def mk(c=c, hf=hf):
                        def sink(tB):
                            return em.op("vector", lambda e: e.tensor_tensor(
                                fsb[:, c * 256:(c + 1) * 256], tB,
                                ox[:, c * 512 + hf * 256: c * 512 + hf * 256 + 256],
                                ALU.add), inc="dve")
                        return sink
                    ot = ln_apply(fsb, c, mk(), W=256)
                em.wait("tensor", "dve", ot[1])
                for tt2 in range(2):
                    em.wait_tok("vector", outtm_free[0])
                    dt = None
                    for c in range(DC):
                        pidx = 3 + (c % 2)
                        pe_use("pw%d" % pidx)
                        tt = em.op("tensor", lambda e, c=c, tt2=tt2, p=pidx: e.transpose(
                            pw[p][:, 0:128],
                            fsb[:, c * 256 + tt2 * 128: c * 256 + (tt2 + 1) * 128],
                            id_f[:, :]), inc="pe")
                        em.wait("vector", "pe", tt[1])
                        dt = em.op("vector", lambda e, c=c, p=pidx: e.tensor_copy(
                            outtm[:, c * 128:(c + 1) * 128], pw[p][:, 0:128]), inc="dve")
                        bank_done("pw%d" % pidx, ("dve", dt[1]))
                    tile_idx = st * 4 + hf * 2 + tt2
                    od = dma("sync", "dout", out[tile_idx * 128:(tile_idx + 1) * 128, :], outtm[:, :],
                             waits=[("dve", dt[1])])
                    outtm_free[0] = ("dout", od[1])
            ox_free = [("dve", em.n("dve"))]


def prep_inputs(inputs):
    x = np.ascontiguousarray(np.asarray(inputs["L0_attn_x"], np.float32).reshape(T * NCORES, D))
    shards = [x[c * T:(c + 1) * T] for c in range(NCORES)]

    lnp = np.stack([
        np.asarray(inputs["L0_attn_ln1_w"], np.float32),
        np.asarray(inputs["L0_attn_ln1_b"], np.float32),
        np.asarray(inputs["L0_attn_ln2_w"], np.float32),
        np.asarray(inputs["L0_attn_ln2_b"], np.float32),
        np.asarray(inputs["L0_attn_bo"], np.float32),
    ], axis=1)
    lncp = np.ascontiguousarray(
        lnp.reshape(DEPTH, 5, DC, 128).transpose(0, 3, 2, 1).reshape(DEPTH, 128, 40))

    sp = np.zeros((5, FF), np.float32)
    sp[0] = np.asarray(inputs["L0_ffn_w1_s"], np.float32)
    sp[1] = np.asarray(inputs["L0_ffn_w2_s"], np.float32)
    sp[2] = np.asarray(inputs["L0_ffn_h1_s"], np.float32)
    sp[3] = np.asarray(inputs["L0_ffn_h2_s"], np.float32)
    sp[4, 0:D] = np.asarray(inputs["L0_ffn_w3_s"], np.float32)
    scp = np.ascontiguousarray(sp.reshape(5, 32, 128).transpose(2, 1, 0).reshape(128, 160))

    common = {
        "wq": np.ascontiguousarray(np.asarray(inputs["L0_attn_wq"], np.float32)),
        "wk": np.ascontiguousarray(np.asarray(inputs["L0_attn_wk"], np.float32)),
        "wv": np.ascontiguousarray(np.asarray(inputs["L0_attn_wv"], np.float32)),
        "wo": np.ascontiguousarray(np.asarray(inputs["L0_attn_wo"], np.float32)),
        "w1": np.ascontiguousarray(np.asarray(inputs["L0_attn_w1"], np.float32)),
        "w2": np.ascontiguousarray(np.asarray(inputs["L0_attn_w2"], np.float32)),
        "lncp": lncp,
        "w1t": np.ascontiguousarray(np.asarray(inputs["L0_ffn_w1_w"], np.float32).T),
        "w2t": np.ascontiguousarray(np.asarray(inputs["L0_ffn_w2_w"], np.float32).T),
        "h1t": np.ascontiguousarray(np.asarray(inputs["L0_ffn_h1_w"], np.float32).T),
        "h2t": np.ascontiguousarray(np.asarray(inputs["L0_ffn_h2_w"], np.float32).T),
        "w3t": np.ascontiguousarray(np.asarray(inputs["L0_ffn_w3_w"], np.float32).T),
        "scp": scp,
        "idf": np.eye(128, dtype=np.float32),
        "cbf": np.ascontiguousarray(np.stack(
            [np.full(128, 1.0 / D), np.ones(128)], axis=1).astype(ml_dtypes.bfloat16)),
        "orf": np.ones((1, 128), np.float32),
    }
    return shards, common


def kernel(**inputs):
    shards, common = prep_inputs(inputs)
    nc = build_nc()
    in_maps = [dict(common, x0=shards[c]) for c in range(NCORES)]
    res = run_bass_kernel_spmd(nc, in_maps, core_ids=list(range(NCORES)))
    outp = np.concatenate([np.asarray(res.results[c]["out"]) for c in range(NCORES)], axis=0)
    return outp.reshape(2, 16384, D)


# version 02
# speedup vs baseline: 1.0287x; 1.0287x over previous
import numpy as np
import ml_dtypes
from contextlib import ExitStack
import concourse.bass as bass
import concourse.mybir as mybir
from concourse.bass_utils import run_bass_kernel_spmd

F32 = mybir.dt.float32
BF16 = mybir.dt.bfloat16
AF = mybir.ActivationFunctionType
ALU = mybir.AluOpType

NCORES = 8
T = 4096
NST = 8
STW = 512
D = 1024
DC = 8
H = 8
FF = 4096
DEPTH = 6
EPS = 1e-5

ENGS = ("sync", "tensor", "vector", "scalar", "gpsimd")


def ctx_pos(h):
    if h < 6:
        b, j = h // 3, h % 3
        return b, b * 387 + j * 128, b * 387 + 384 + j
    b, j = 2, h - 6
    return 2, 774 + j * 128, 774 + 256 + j


class Emitter:
    def __init__(self):
        self.q = {e: [] for e in ENGS}
        self.cnt = {}
        self.lastwait = {}

    def n(self, sem):
        return self.cnt.get(sem, 0)

    def wait(self, eng, sem, val):
        if val is None or val <= 0:
            return
        key = (eng, sem)
        if self.lastwait.get(key, -1) >= val:
            return
        self.lastwait[key] = val
        self.q[eng].append(("w", sem, val))

    def wait_tok(self, eng, tok):
        if tok is None:
            return
        if isinstance(tok, list):
            for t in tok:
                self.wait_tok(eng, t)
        else:
            self.wait(eng, tok[0], tok[1])

    def op(self, eng, fn, inc=None, amt=1):
        if inc is None:
            self.q[eng].append(("i", fn, None, 0))
            return None
        self.cnt[inc] = self.cnt.get(inc, 0) + amt
        tot = self.cnt[inc]
        self.q[eng].append(("i", fn, inc, amt))
        return (inc, tot)

    def flush(self, block, sems):
        q = {e: list(self.q[e]) for e in ENGS}
        for e in ENGS:
            self.q[e] = []

        def replay(engine, items):
            for item in items:
                if item[0] == "w":
                    engine.wait_ge(sems[item[1]], item[2])
                else:
                    _, fn, inc, amt = item
                    ins = fn(engine)
                    if inc is not None:
                        ins.then_inc(sems[inc], amt)

        if q["sync"]:
            @block.sync
            def _(e, it=q["sync"]):
                replay(e, it)
        if q["tensor"]:
            @block.tensor
            def _(e, it=q["tensor"]):
                replay(e, it)
        if q["vector"]:
            @block.vector
            def _(e, it=q["vector"]):
                replay(e, it)
        if q["scalar"]:
            @block.scalar
            def _(e, it=q["scalar"]):
                replay(e, it)
        if q["gpsimd"]:
            @block.gpsimd
            def _(e, it=q["gpsimd"]):
                replay(e, it)


def build_nc(debug=None):
    nc = bass.Bass("TRN2", target_bir_lowering=False)

    x0 = nc.dram_tensor("x0", [NST, 128, 4 * D], F32, kind="ExternalInput").ap()
    wq_d = nc.dram_tensor("wq", [DEPTH, 128, DC * D], F32, kind="ExternalInput").ap()
    wk_d = nc.dram_tensor("wk", [DEPTH, 128, DC * D], F32, kind="ExternalInput").ap()
    wv_d = nc.dram_tensor("wv", [DEPTH, 128, DC * D], F32, kind="ExternalInput").ap()
    wo_d = nc.dram_tensor("wo", [DEPTH, 128, DC * D], F32, kind="ExternalInput").ap()
    w1_d = nc.dram_tensor("w1", [DEPTH, 2, 128, 2 * DC * D], F32, kind="ExternalInput").ap()
    w2_d = nc.dram_tensor("w2", [DEPTH, 2, 128, 2 * DC * D], F32, kind="ExternalInput").ap()
    lncp = nc.dram_tensor("lncp", [DEPTH, 128, 40], F32, kind="ExternalInput").ap()
    w1t = nc.dram_tensor("w1t", [32, 128, 1024], F32, kind="ExternalInput").ap()
    w2t = nc.dram_tensor("w2t", [32, 128, 1024], F32, kind="ExternalInput").ap()
    h1t = nc.dram_tensor("h1t", [32, 128, 4096], F32, kind="ExternalInput").ap()
    h2t = nc.dram_tensor("h2t", [32, 128, 4096], F32, kind="ExternalInput").ap()
    w3t = nc.dram_tensor("w3t", [8, 128, 4096], F32, kind="ExternalInput").ap()
    scp = nc.dram_tensor("scp", [128, 160], F32, kind="ExternalInput").ap()
    idf = nc.dram_tensor("idf", [128, 128], F32, kind="ExternalInput").ap()
    cbf = nc.dram_tensor("cbf", [128, 2], BF16, kind="ExternalInput").ap()
    orf = nc.dram_tensor("orf", [1, 128], F32, kind="ExternalInput").ap()

    out = nc.dram_tensor("out", [T, D], F32, kind="ExternalOutput").ap()

    xT = nc.dram_tensor("xT", [NST, 128, DC * STW], F32).ap()
    x0T = nc.dram_tensor("x0T", [NST, 128, DC * STW], F32).ap()
    qT = nc.dram_tensor("qT", [NST, 128, DC * STW], BF16).ap()
    ccin = nc.dram_tensor("ccin", [128, 1032], F32)
    ccout = nc.dram_tensor("ccout", [128, 1032], F32)

    dbg = {}
    if debug is not None:
        dbg["x"] = nc.dram_tensor("dbg_x", [NST, 128, DC * STW], F32, kind="ExternalOutput").ap()
        dbg["q"] = nc.dram_tensor("dbg_q", [NST, 128, DC * STW], BF16, kind="ExternalOutput").ap()
        dbg["ctx"] = nc.dram_tensor("dbg_ctx", [128, 1032], F32, kind="ExternalOutput").ap()

    em = Emitter()

    with ExitStack() as es:
        E = es.enter_context
        semnames = ["dx", "dxo", "dq", "dw", "dcc", "cc", "pe", "act", "dve", "dout", "dmisc"]
        sems = {s: E(nc.semaphore(s)) for s in semnames}

        mreps = E(nc.sbuf_tensor("mreps", [128, 1040], F32))
        tmpA = E(nc.sbuf_tensor("tmpA", [128, 512], F32))
        rows = E(nc.sbuf_tensor("rows", [1, 1024], F32))
        lncols = [E(nc.sbuf_tensor("lncols0", [128, 40], F32)),
                  E(nc.sbuf_tensor("lncols1", [128, 40], F32))]
        id_f = E(nc.sbuf_tensor("id_f", [128, 128], F32))
        ccol = E(nc.sbuf_tensor("ccol", [128, 2], BF16))
        orow = E(nc.sbuf_tensor("orow", [1, 128], F32))
        epsr = E(nc.sbuf_tensor("epsr", [1, 1], F32))

        pc = [E(nc.psum_tensor("pc%d" % i, [128, 512], F32)) for i in range(3)]
        pw = [E(nc.psum_tensor("pw%d" % i, [128, 512], F32)) for i in range(5)]

        block = E(nc.Block())

        inner_es = ExitStack()
        IE = inner_es.enter_context
        xin = [IE(nc.sbuf_tensor("xin0", [128, DC * STW], F32)),
               IE(nc.sbuf_tensor("xin1", [128, DC * STW], F32))]
        qbuf = [IE(nc.sbuf_tensor("qbuf0", [128, DC * STW], BF16)),
                IE(nc.sbuf_tensor("qbuf1", [128, DC * STW], BF16))]
        w12h = IE(nc.sbuf_tensor("w12h", [128, 4 * DC * D], BF16))
        wqkv = IE(nc.sbuf_tensor("wqkv", [128, 3 * DC * D], BF16))
        wosb = IE(nc.sbuf_tensor("wosb", [128, DC * D], BF16))
        xn1T = IE(nc.sbuf_tensor("xn1T", [128, DC * STW], BF16))
        hq = IE(nc.sbuf_tensor("hq", [128, DC * STW], BF16))
        ekb = IE(nc.sbuf_tensor("ekb", [128, D], BF16))
        vbuf = IE(nc.sbuf_tensor("vbuf", [128, D], BF16))
        aoT = hq
        eqs = hq[:, 0:512]
        ctxbf = ekb
        ctxCC = mreps[:, 0:1032]

        def dma(eng, sem_, outap, inap, waits=()):
            for wtok in waits:
                em.wait_tok(eng, wtok)
            return em.op(eng, lambda e, o=outap, i=inap: e.dma_start(out=o, in_=i), inc=sem_, amt=16)

        bank_free = {}

        def pe_use(bank):
            em.wait_tok("tensor", bank_free.get(bank))

        def bank_done(bank, tok):
            bank_free[bank] = tok

        def mm(outap, lhs, rhs, start, stop, inc=None):
            return em.op("tensor",
                         lambda e, o=outap, l=lhs, r=rhs, st=start, sp=stop:
                         e.matmul(o, l, r, start=st, stop=sp, skip_group_check=True),
                         inc=inc)

        def lncol(bi, c, vec):
            return lncols[bi][:, c * 5 + vec: c * 5 + vec + 1]

        xslot = [None] * NST
        qslot = [None] * NST
        xin_free = [None, None]
        qbuf_free = [None, None]

        def ln_stats(xsb, castb, sqb, W=STW):
            ct = em.op("vector", lambda e: e.tensor_copy(castb[:, 0:DC * W], xsb[:, 0:DC * W]), inc="dve")
            em.wait("scalar", "dve", ct[1])
            sq = em.op("scalar", lambda e: e.activation(sqb[:, 0:DC * W], castb[:, 0:DC * W], AF.Square), inc="act")
            em.wait("tensor", "dve", ct[1])
            pe_use("pw0")
            for c in range(DC):
                mm(pw[0][0:1, 0:W], ccol[:, 0:1], castb[:, c * W:(c + 1) * W],
                   start=(c == 0), stop=(c == DC - 1))
            em.wait("tensor", "act", sq[1])
            pe_use("pw1")
            mt = None
            for c in range(DC):
                mt = mm(pw[1][0:1, 0:W], ccol[:, 0:1], sqb[:, c * W:(c + 1) * W],
                        start=(c == 0), stop=(c == DC - 1), inc="pe" if c == DC - 1 else None)
            r_m = pw[0][0:1, 0:W]
            r_s = pw[1][0:1, 0:W]
            em.wait("scalar", "pe", mt[1])
            sqm = em.op("scalar", lambda e: e.activation(rows[0:1, 0:W], r_m, AF.Square), inc="act")
            em.wait("vector", "act", sqm[1])
            vt = em.op("vector", lambda e: e.tensor_tensor(rows[0:1, 512:512 + W], r_s, rows[0:1, 0:W], ALU.subtract), inc="dve")
            em.wait("scalar", "dve", vt[1])
            em.op("scalar", lambda e: e.activation(rows[0:1, 0:W], rows[0:1, 512:512 + W], AF.Ln, bias=epsr[0:1, 0:1], scale=1.0))
            rt = em.op("scalar", lambda e: e.activation(rows[0:1, 512:512 + W], rows[0:1, 0:W], AF.Exp, bias=0.0, scale=-0.5), inc="act")
            em.wait("vector", "act", rt[1])
            mr = em.op("vector", lambda e: e.tensor_tensor(rows[0:1, 0:W], r_m, rows[0:1, 512:512 + W], ALU.mult), inc="dve")
            em.wait("tensor", "dve", mr[1])
            pe_use("pw0")
            pe_use("pw1")
            mm(pw[0][:, 0:W], orow[:, :], rows[0:1, 512:512 + W], start=True, stop=True)
            t2 = mm(pw[1][:, 0:W], orow[:, :], rows[0:1, 0:W], start=True, stop=True, inc="pe")
            em.wait("vector", "pe", t2[1])
            em.op("vector", lambda e: e.tensor_copy(mreps[:, 0:W], pw[0][:, 0:W]))
            c2 = em.op("vector", lambda e: e.tensor_copy(mreps[:, 512:512 + W], pw[1][:, 0:W]), inc="dve")
            bank_done("pw0", ("dve", c2[1]))
            bank_done("pw1", ("dve", c2[1]))
            return c2

        def ln_apply(xsb, c, sink, wcol=None, bcol=None, W=STW):
            xc = xsb[:, c * W:(c + 1) * W]
            em.op("vector", lambda e: e.tensor_tensor(tmpA[:, 0:W], xc, mreps[:, 0:W], ALU.mult))
            em.op("vector", lambda e: e.tensor_tensor(tmpA[:, 0:W], tmpA[:, 0:W], mreps[:, 512:512 + W], ALU.subtract))
            if wcol is not None:
                return em.op("vector", lambda e, w=wcol, b=bcol: e.tensor_scalar(
                    sink, tmpA[:, 0:W], w, b, ALU.mult, ALU.add), inc="dve")
            if callable(sink):
                return sink(tmpA[:, 0:W])
            return em.op("vector", lambda e: e.tensor_copy(sink, tmpA[:, 0:W]), inc="dve")

        def load_wqkv(li):
            gw = ("pe", em.n("pe"))
            dma("gpsimd", "dw", wqkv[:, 0:DC * D], wq_d[li], waits=[gw])
            dma("gpsimd", "dw", wqkv[:, DC * D:2 * DC * D], wk_d[li])
            t = dma("gpsimd", "dw", wqkv[:, 2 * DC * D:3 * DC * D], wv_d[li])
            return ("dw", t[1])

        def load_wo(li):
            t = dma("gpsimd", "dw", wosb[:, :], wo_d[li],
                    waits=[("pe", em.n("pe"))])
            return ("dw", t[1])

        def load_w12h(li, half):
            gw = ("pe", em.n("pe"))
            dma("gpsimd", "dw", w12h[:, 0:2 * DC * D], w1_d[li, half], waits=[gw])
            t = dma("gpsimd", "dw", w12h[:, 2 * DC * D:4 * DC * D], w2_d[li, half])
            return ("dw", t[1])

        def load_lncols(li, bi):
            t = dma("sync", "dmisc", lncols[bi][:, :], lncp[li],
                    waits=[("dve", em.n("dve"))])
            return ("dmisc", t[1])

        dma("sync", "dmisc", id_f[:, :], idf)
        dma("sync", "dmisc", ccol[:, :], cbf)
        ct0 = dma("sync", "dmisc", orow[:, :], orf)
        em.op("vector", lambda e: e.memset(epsr[:, :], EPS), inc="dve")
        for eng in ("tensor", "vector", "scalar"):
            em.wait(eng, "dmisc", ct0[1])

        wqkv_rdy = {0: load_wqkv(0)}
        wo_rdy = {0: load_wo(0)}
        w12_rdy = {(0, 0): load_w12h(0, 0)}
        ln_rdy = {0: load_lncols(0, 0)}

        for st in range(NST):
            lt = dma("sync", "dx", xin[0][:, :], x0[st],
                     waits=[xin_free[0], xin_free[1]])
            em.wait("tensor", "dx", lt[1])
            ctok = None
            for t4 in range(4):
                for c in range(DC):
                    pidx = 2 + ((t4 * DC + c) % 2)
                    pe_use("pw%d" % pidx)
                    tt = em.op("tensor", lambda e, t4=t4, c=c, p=pidx: e.transpose(
                        pw[p][:, 0:128], xin[0][:, t4 * D + c * 128:t4 * D + c * 128 + 128],
                        id_f[:, :]), inc="pe")
                    em.wait("vector", "pe", tt[1])
                    ctok = em.op("vector", lambda e, t4=t4, c=c, p=pidx: e.tensor_copy(
                        xin[1][:, c * 512 + t4 * 128: c * 512 + t4 * 128 + 128], pw[p][:, 0:128]),
                        inc="dve")
                    bank_done("pw%d" % pidx, ("dve", ctok[1]))
            s0 = dma("sync", "dxo", x0T[st], xin[1][:, :],
                     waits=[("dve", ctok[1])])
            ln_stats(xin[1], xn1T, hq)
            atok = None
            for c in range(DC):
                atok = ln_apply(xin[1], c, xin[0][:, c * 512:(c + 1) * 512])
            s1 = dma("sync", "dxo", xT[st], xin[0][:, :],
                     waits=[("dve", atok[1])])
            xslot[st] = ("dxo", s1[1])
            xin_free[0] = [("dxo", s1[1])]
            xin_free[1] = [("dxo", s0[1]), ("dve", atok[1])]

        def sweep_A(li):
            lnb = li % 2
            em.wait_tok("tensor", wqkv_rdy[li])
            em.wait_tok("vector", ln_rdy[li])
            wq_off, wk_off, wv_off = 0, DC * D, 2 * DC * D
            ntq = None
            for st in range(NST):
                xb = st % 2
                qb = st % 2
                lt = dma("sync", "dx", xin[xb][:, :], xT[st],
                         waits=[xin_free[xb], xslot[st]])
                em.wait("vector", "dx", lt[1])
                xsb = xin[xb]
                ln_stats(xsb, xn1T, hq)
                at = None
                for c in range(DC):
                    at = ln_apply(xsb, c, xn1T[:, c * 512:(c + 1) * 512],
                                  wcol=lncol(lnb, c, 0), bcol=lncol(lnb, c, 1))
                em.wait("tensor", "dve", at[1])
                xin_free[xb] = [("dve", at[1])]
                for oc in range(H):
                    pidx = 2 + (oc % 2)
                    pe_use("pw%d" % pidx)
                    qmt = None
                    for c in range(DC):
                        qmt = mm(pw[pidx][:, :],
                                 wqkv[:, wq_off + c * D + oc * 128: wq_off + c * D + oc * 128 + 128],
                                 xn1T[:, c * 512:(c + 1) * 512],
                                 start=(c == 0), stop=(c == DC - 1),
                                 inc="pe" if c == DC - 1 else None)
                    em.wait("scalar", "pe", qmt[1])
                    if ntq is not None:
                        em.wait("scalar", "dve", ntq[1])
                    et = em.op("scalar", lambda e, p=pidx: e.activation(eqs, pw[p][:, :], AF.Exp),
                               inc="act")
                    bank_done("pw%d" % pidx, ("act", et[1]))
                    em.wait("tensor", "act", et[1])
                    pe_use("pw4")
                    zt = mm(pw[4][0:1, 0:512], ccol[:, 1:2], eqs, start=True, stop=True, inc="pe")
                    em.wait("vector", "pe", zt[1])
                    rt = em.op("vector", lambda e: e.reciprocal(rows[0:1, 0:512], pw[4][0:1, 0:512]),
                               inc="dve")
                    bank_done("pw4", ("dve", rt[1]))
                    em.wait("tensor", "dve", rt[1])
                    pe_use("pw4")
                    zr = mm(pw[4][:, :], orow[:, :], rows[0:1, 0:512], start=True, stop=True, inc="pe")
                    em.wait("vector", "pe", zr[1])
                    if oc == 0:
                        em.wait_tok("vector", qbuf_free[qb])
                    ntq = em.op("vector", lambda e, oc=oc, qb=qb: e.tensor_tensor(
                        qbuf[qb][:, oc * 512:(oc + 1) * 512], eqs, pw[4][:, :], ALU.mult),
                        inc="dve")
                    bank_done("pw4", ("dve", ntq[1]))
                qs = dma("sync", "dq", qT[st], qbuf[qb][:, :],
                         waits=[("dve", ntq[1]), qslot[st]])
                qslot[st] = ("dq", qs[1])
                qbuf_free[qb] = [("dq", qs[1])]
                for t4 in range(4):
                    ktoks = []
                    for f in range(2):
                        pe_use("pw%d" % (2 + f))
                        kt = None
                        for c in range(DC):
                            kt = mm(pw[2 + f][:, :],
                                    xn1T[:, c * 512 + t4 * 128: c * 512 + t4 * 128 + 128],
                                    wqkv[:, wk_off + c * D + f * 512: wk_off + c * D + (f + 1) * 512],
                                    start=(c == 0), stop=(c == DC - 1),
                                    inc="pe" if c == DC - 1 else None)
                        ktoks.append(kt)
                    ets = []
                    for f in range(2):
                        em.wait("scalar", "pe", ktoks[f][1])
                        et = em.op("scalar", lambda e, f=f: e.activation(
                            ekb[:, f * 512:(f + 1) * 512], pw[2 + f][:, :], AF.Exp), inc="act")
                        bank_done("pw%d" % (2 + f), ("act", et[1]))
                        ets.append(et)
                    vtoks = []
                    for f in range(2):
                        pe_use("pw%d" % (2 + f))
                        vt = None
                        for c in range(DC):
                            vt = mm(pw[2 + f][:, :],
                                    xn1T[:, c * 512 + t4 * 128: c * 512 + t4 * 128 + 128],
                                    wqkv[:, wv_off + c * D + f * 512: wv_off + c * D + (f + 1) * 512],
                                    start=(c == 0), stop=(c == DC - 1),
                                    inc="pe" if c == DC - 1 else None)
                        vtoks.append(vt)
                    vct = None
                    for f in range(2):
                        em.wait("vector", "pe", vtoks[f][1])
                        vct = em.op("vector", lambda e, f=f: e.tensor_copy(
                            vbuf[:, f * 512:(f + 1) * 512], pw[2 + f][:, :]), inc="dve")
                        bank_done("pw%d" % (2 + f), ("dve", vct[1]))
                    em.wait("tensor", "dve", vct[1])
                    em.wait("tensor", "act", ets[1][1])
                    first = (st == 0 and t4 == 0)
                    last = (st == NST - 1 and t4 == 3)
                    for h in range(H):
                        b, coff, zoff = ctx_pos(h)
                        mm(pc[b][:, coff - b * 387:coff - b * 387 + 128],
                           ekb[:, h * 128:(h + 1) * 128],
                           vbuf[:, h * 128:(h + 1) * 128],
                           start=(first and h in (0, 3, 6)), stop=last)
                        mm(pc[b][:, zoff - b * 387: zoff - b * 387 + 1],
                           ekb[:, h * 128:(h + 1) * 128],
                           ccol[:, 1:2],
                           start=False, stop=last,
                           inc="pe" if (last and h == H - 1) else None)
            em.wait("vector", "pe", em.n("pe"))
            em.op("vector", lambda e: e.tensor_copy(ctxCC[:, 0:387], pc[0][:, 0:387]))
            em.op("vector", lambda e: e.tensor_copy(ctxCC[:, 387:774], pc[1][:, 0:387]))
            ev = em.op("vector", lambda e: e.tensor_copy(ctxCC[:, 774:1032], pc[2][:, 0:258]), inc="dve")
            for b in range(3):
                bank_done("pc%d" % b, ("dve", ev[1]))
            ci = dma("gpsimd", "dcc", ccin[:, :], ctxCC, waits=[("dve", ev[1])])
            em.wait("gpsimd", "dcc", ci[1])
            cct = em.op("gpsimd", lambda e: e.collective_compute(
                "AllReduce", ALU.add,
                replica_groups=[[0, 1, 2, 3], [4, 5, 6, 7]],
                ins=[ccin.ap().opt()], outs=[ccout.ap().opt()]), inc="cc")
            em.wait("gpsimd", "cc", cct[1])
            co = dma("gpsimd", "dcc", ctxCC, ccout[:, :])
            return co

        def sweep_B(li, co):
            lnb = li % 2
            em.wait_tok("tensor", wo_rdy[li])
            em.wait("vector", "dcc", co[1])
            rc = None
            for h in range(H):
                b, coff, zoff = ctx_pos(h)
                rc = em.op("vector", lambda e, z=zoff, h=h: e.reciprocal(
                    tmpA[:, h:h + 1], ctxCC[:, z:z + 1]),
                    inc="dve" if h == H - 1 else None)
            em.wait("vector", "dve", rc[1])
            nt = None
            for h in range(H):
                b, coff, zoff = ctx_pos(h)
                nt = em.op("vector", lambda e, h=h, c0=coff: e.tensor_scalar(
                    ctxbf[:, h * 128:(h + 1) * 128], ctxCC[:, c0:c0 + 128],
                    tmpA[:, h:h + 1], None, ALU.mult), inc="dve")
            em.wait("tensor", "dve", nt[1])
            for st in range(NST):
                xb = st % 2
                qb = st % 2
                lt = dma("sync", "dx", xin[xb][:, :], xT[st],
                         waits=[xin_free[xb], xslot[st]])
                qlt = dma("sync", "dq", qbuf[qb][:, :], qT[st],
                          waits=[qbuf_free[qb], qslot[st]])
                xsb = xin[xb]
                em.wait("tensor", "dq", qlt[1])
                if li == 0 and st == 0:
                    pass
                aot = None
                for h in range(H):
                    bank = "pw4" if h % 2 == 0 else "pc0"
                    pt = pw[4] if h % 2 == 0 else pc[0]
                    pe_use(bank)
                    t = mm(pt[:, :], ctxbf[:, h * 128:(h + 1) * 128],
                           qbuf[qb][:, h * 512:(h + 1) * 512], start=True, stop=True, inc="pe")
                    em.wait("scalar", "pe", t[1])
                    aot = em.op("scalar", lambda e, h=h, pt=pt: e.activation(
                        aoT[:, h * 512:(h + 1) * 512], pt[:, :], AF.Copy), inc="act")
                    bank_done(bank, ("act", aot[1]))
                ao_done = ("pe", em.n("pe"))
                em.wait("tensor", "act", aot[1])
                em.wait("vector", "dx", lt[1])
                for half in range(2):
                    wtoks = []
                    for ds in range(4):
                        d_idx = half * 4 + ds
                        pe_use("pw%d" % ds)
                        t = None
                        for ec in range(DC):
                            t = mm(pw[ds][:, :],
                                   wosb[:, ec * D + d_idx * 128: ec * D + d_idx * 128 + 128],
                                   aoT[:, ec * 512:(ec + 1) * 512],
                                   start=(ec == 0), stop=(ec == DC - 1),
                                   inc="pe" if ec == DC - 1 else None)
                        wtoks.append(t)
                    for ds in range(4):
                        d_idx = half * 4 + ds
                        em.wait("vector", "pe", wtoks[ds][1])
                        em.op("vector", lambda e, ds=ds, d=d_idx, b=lnb: e.tensor_scalar(
                            tmpA[:, :], pw[ds][:, :], lncol(b, d, 4), None, ALU.add))
                        rt = em.op("vector", lambda e, d=d_idx, xsb=xsb: e.tensor_tensor(
                            xsb[:, d * 512:(d + 1) * 512], xsb[:, d * 512:(d + 1) * 512],
                            tmpA[:, :], ALU.add), inc="dve")
                        bank_done("pw%d" % ds, ("dve", rt[1]))
                ln_stats(xsb, xn1T, hq)
                em.wait_tok("vector", [ao_done])
                at = None
                for c in range(DC):
                    at = ln_apply(xsb, c, qbuf[qb][:, c * 512:(c + 1) * 512],
                                  wcol=lncol(lnb, c, 2), bcol=lncol(lnb, c, 3))
                qs = dma("sync", "dq", qT[st], qbuf[qb][:, :], waits=[("dve", at[1])])
                qslot[st] = ("dq", qs[1])
                qbuf_free[qb] = [("dq", qs[1])]
                so = dma("sync", "dxo", xT[st], xsb[:, :],
                         waits=[("dve", at[1])])
                xslot[st] = ("dxo", so[1])
                xin_free[xb] = [("dxo", so[1])]

        def sweep_F(li, half, last=False):
            em.wait_tok("tensor", w12_rdy[(li, half)])
            for st in range(NST):
                xb = st % 2
                qb = st % 2
                lt = dma("sync", "dx", xin[xb][:, :], xT[st],
                         waits=[xin_free[xb], xslot[st]])
                qlt = dma("sync", "dq", qbuf[qb][:, :], qT[st],
                          waits=[qbuf_free[qb], qslot[st]])
                xsb = xin[xb]
                em.wait("tensor", "dq", qlt[1])
                em.wait("vector", "dx", lt[1])
                yt = None
                for Q in range(2):
                    hts = []
                    for oc8 in range(DC):
                        bank = "pw4" if oc8 % 2 == 0 else "pc0"
                        pt = pw[4] if oc8 % 2 == 0 else pc[0]
                        pe_use(bank)
                        t = None
                        for c in range(DC):
                            t = mm(pt[:, :],
                                   w12h[:, c * 2048 + Q * 1024 + oc8 * 128: c * 2048 + Q * 1024 + oc8 * 128 + 128],
                                   qbuf[qb][:, c * 512:(c + 1) * 512],
                                   start=(c == 0), stop=(c == DC - 1),
                                   inc="pe" if c == DC - 1 else None)
                        em.wait("scalar", "pe", t[1])
                        ht = em.op("scalar", lambda e, oc8=oc8, pt=pt: e.activation(
                            hq[:, oc8 * 512:(oc8 + 1) * 512], pt[:, :], AF.Lrelu), inc="act")
                        bank_done(bank, ("act", ht[1]))
                        hts.append(ht)
                    em.wait("tensor", "act", hts[-1][1])
                    for dh in range(2):
                        ytoks = []
                        for ds in range(4):
                            d_idx = dh * 4 + ds
                            pe_use("pw%d" % ds)
                            t = None
                            for oc8 in range(DC):
                                woff = 2 * DC * D + (Q * DC + oc8) * D + d_idx * 128
                                t = mm(pw[ds][:, :],
                                       w12h[:, woff: woff + 128],
                                       hq[:, oc8 * 512:(oc8 + 1) * 512],
                                       start=(oc8 == 0), stop=(oc8 == DC - 1),
                                       inc="pe" if oc8 == DC - 1 else None)
                            ytoks.append(t)
                        for ds in range(4):
                            d_idx = dh * 4 + ds
                            em.wait("vector", "pe", ytoks[ds][1])
                            yt = em.op("vector", lambda e, ds=ds, d=d_idx, xsb=xsb: e.tensor_tensor(
                                xsb[:, d * 512:(d + 1) * 512], xsb[:, d * 512:(d + 1) * 512],
                                pw[ds][:, :], ALU.add), inc="dve")
                            bank_done("pw%d" % ds, ("dve", yt[1]))
                qbuf_free[qb] = [("pe", em.n("pe"))]
                if not last:
                    so = dma("sync", "dxo", xT[st], xsb[:, :],
                             waits=[("dve", yt[1])])
                    xslot[st] = ("dxo", so[1])
                    xin_free[xb] = [("dxo", so[1])]
                else:
                    ln_stats(xsb, xn1T, hq)
                    ob = (st + 1) % 2
                    l0 = dma("sync", "dx", xin[ob][:, :], x0T[st],
                             waits=[xin_free[ob]])
                    em.wait("vector", "dx", l0[1])
                    at = None
                    for c in range(DC):
                        def mk(c=c, ob=ob):
                            def sink(tB):
                                return em.op("vector", lambda e: e.tensor_tensor(
                                    xin[ob][:, c * 512:(c + 1) * 512], tB,
                                    xin[ob][:, c * 512:(c + 1) * 512], ALU.add), inc="dve")
                            return sink
                        at = ln_apply(xsb, c, mk())
                    so = dma("sync", "dxo", xT[st], xin[ob][:, :],
                             waits=[("dve", at[1])])
                    xslot[st] = ("dxo", so[1])
                    xin_free[xb] = [("dve", at[1])]
                    xin_free[ob] = [("dxo", so[1])]

        if isinstance(debug, tuple):
            n_layers, stop_stage = debug[1] + 1, debug[0]
        elif debug == "entry":
            n_layers, stop_stage = 0, None
        else:
            n_layers, stop_stage = DEPTH, None

        for li in range(n_layers):
            co = sweep_A(li)
            if stop_stage == "A" and li == n_layers - 1:
                break
            sweep_B(li, co)
            if stop_stage == "B" and li == n_layers - 1:
                break
            if li + 1 < DEPTH:
                wqkv_rdy[li + 1] = load_wqkv(li + 1)
                ln_rdy[li + 1] = load_lncols(li + 1, (li + 1) % 2)
            sweep_F(li, 0)
            if li + 1 < DEPTH:
                wo_rdy[li + 1] = load_wo(li + 1)
            w12_rdy[(li, 1)] = load_w12h(li, 1)
            sweep_F(li, 1, last=(li == DEPTH - 1))
            if li + 1 < DEPTH:
                w12_rdy[(li + 1, 0)] = load_w12h(li + 1, 0)

        if debug is not None:
            em.wait("sync", "dxo", em.n("dxo"))
            em.wait("sync", "dq", em.n("dq"))
            dma("sync", "dout", dbg["x"], xT)
            dma("sync", "dout", dbg["q"], qT)
            em.wait("gpsimd", "dcc", em.n("dcc"))
            dma("gpsimd", "dout", dbg["ctx"], ccout[:, :])
            em.wait("sync", "dout", em.n("dout"))
            em.wait("gpsimd", "dout", em.n("dout"))
            em.flush(block, sems)
            inner_es.close()
        else:
            em.flush(block, sems)
            inner_es.close()
            emit_outer(nc, em, sems, dict(
                dma=dma, mm=mm, pe_use=pe_use, bank_done=bank_done,
                ln_stats=ln_stats, ln_apply=ln_apply,
                xT=xT, pw=pw, pc=pc, rows=rows, id_f=id_f, ccol=ccol,
                orow=orow, epsr=epsr, tmpA=tmpA, mreps=mreps, out=out,
                xslot=xslot, w1t=w1t, w2t=w2t, h1t=h1t, h2t=h2t,
                w3t=w3t, scp=scp), E, em)
            em.wait("sync", "dout", em.n("dout"))
            em.flush(block, sems)

    return nc


def emit_outer(nc, em, sems, L, E, _em):
    dma, mm, pe_use, bank_done = L["dma"], L["mm"], L["pe_use"], L["bank_done"]
    ln_stats, ln_apply = L["ln_stats"], L["ln_apply"]
    xT, pw, pc = L["xT"], L["pw"], L["pc"]
    id_f, ccol = L["id_f"], L["ccol"]
    out = L["out"]
    xslot = L["xslot"]
    w1t, w2t, h1t, h2t, w3t, scp = (L[k] for k in ("w1t", "w2t", "h1t", "h2t", "w3t", "scp"))

    ox = E(nc.sbuf_tensor("ox", [128, DC * STW], F32))
    hgrp = E(nc.sbuf_tensor("hgrp", [128, 32 * 1024], BF16))
    ggrp = E(nc.sbuf_tensor("ggrp", [128, 32 * 1024], BF16))
    wsl = [E(nc.sbuf_tensor("wsl0", [128, 4096], BF16)),
           E(nc.sbuf_tensor("wsl1", [128, 4096], BF16))]
    zT = [E(nc.sbuf_tensor("zT0", [128, 4096], BF16)),
          E(nc.sbuf_tensor("zT1", [128, 4096], BF16))]
    fsb = E(nc.sbuf_tensor("fsb", [128, 2048], F32))
    outtm = E(nc.sbuf_tensor("outtm", [128, 1024], F32))
    scols = E(nc.sbuf_tensor("scols", [128, 160], F32))
    stmp = E(nc.sbuf_tensor("stmp", [128, 1024], BF16))

    sct = dma("sync", "dmisc", scols[:, :], scp)
    for eng in ("tensor", "vector", "scalar"):
        em.wait(eng, "dmisc", sct[1])

    def scol(oc, vec):
        return scols[:, oc * 5 + vec: oc * 5 + vec + 1]

    wsl_free = [None, None]
    outtm_free = [None]

    def load_slice(src, oc, buf, width):
        return dma("gpsimd", "dw", wsl[buf][:, 0:width], src[oc],
                   waits=[wsl_free[buf]])

    ox_free = None
    for g in range(4):
        sts = [2 * g, 2 * g + 1]
        for k, st in enumerate(sts):
            lt = dma("sync", "dx", ox[:, :], xT[st],
                     waits=[ox_free, xslot[st]])
            em.wait("vector", "dx", lt[1])
            ln_stats(ox, hgrp[:, 0:4096], hgrp[:, 4096:8192])
            at = None
            for c in range(DC):
                at = ln_apply(ox, c, zT[k][:, c * 512:(c + 1) * 512])
            ox_free = [("dve", at[1])]
        em.wait("tensor", "dve", em.n("dve"))
        ht = None
        for oc in range(32):
            buf = oc % 2
            dma("gpsimd", "dw", wsl[buf][:, 0:1024], w1t[oc],
                waits=[wsl_free[buf]])
            l2 = dma("gpsimd", "dw", wsl[buf][:, 1024:2048], w2t[oc])
            em.wait("tensor", "dw", l2[1])
            for k in range(2):
                pe_use("pw2")
                for c in range(DC):
                    mm(pw[2][:, :], wsl[buf][:, c * 128:(c + 1) * 128],
                       zT[k][:, c * 512:(c + 1) * 512], start=(c == 0), stop=(c == DC - 1))
                pe_use("pw3")
                tb = None
                for c in range(DC):
                    tb = mm(pw[3][:, :], wsl[buf][:, 1024 + c * 128: 1024 + (c + 1) * 128],
                            zT[k][:, c * 512:(c + 1) * 512], start=(c == 0), stop=(c == DC - 1),
                            inc="pe" if c == DC - 1 else None)
                em.wait("scalar", "pe", tb[1])
                if ht is not None:
                    em.wait("scalar", "dve", ht[1])
                sa = em.op("scalar", lambda e, oc=oc: e.activation(
                    stmp[:, 0:512], pw[2][:, :], AF.Silu, bias=0.0, scale=scol(oc, 0)), inc="act")
                sb2 = em.op("scalar", lambda e, oc=oc: e.activation(
                    stmp[:, 512:1024], pw[3][:, :], AF.Copy, bias=0.0, scale=scol(oc, 1)), inc="act")
                bank_done("pw2", ("act", sa[1]))
                bank_done("pw3", ("act", sb2[1]))
                em.wait("vector", "act", sb2[1])
                ht = em.op("vector", lambda e, oc=oc, k=k: e.tensor_tensor(
                    hgrp[:, oc * 1024 + k * 512: oc * 1024 + (k + 1) * 512],
                    stmp[:, 0:512], stmp[:, 512:1024], ALU.mult), inc="dve")
            wsl_free[buf] = [("dve", ht[1])]

        def hstage(src_grp, dst_grp, wmat, svec_idx, src_sem):
            em.wait("tensor", src_sem, em.n(src_sem))
            for oc in range(32):
                buf = oc % 2
                ls = load_slice(wmat, oc, buf, 4096)
                em.wait("tensor", "dw", ls[1])
                last = None
                for k in range(2):
                    bank = "pw2" if k == 0 else "pc0"
                    pt = pw[2] if k == 0 else pc[0]
                    pe_use(bank)
                    t = None
                    for ic in range(32):
                        t = mm(pt[:, :], wsl[buf][:, ic * 128:(ic + 1) * 128],
                               src_grp[:, ic * 1024 + k * 512: ic * 1024 + (k + 1) * 512],
                               start=(ic == 0), stop=(ic == 31),
                               inc="pe" if ic == 31 else None)
                    em.wait("scalar", "pe", t[1])
                    gt = em.op("scalar", lambda e, oc=oc, k=k, pt=pt, dst=dst_grp: e.activation(
                        dst[:, oc * 1024 + k * 512: oc * 1024 + (k + 1) * 512],
                        pt[:, :], AF.Gelu, bias=0.0, scale=scol(oc, svec_idx)), inc="act")
                    bank_done(bank, ("act", gt[1]))
                    last = gt
                wsl_free[buf] = [("act", last[1])]

        hstage(hgrp, ggrp, h1t, 2, "dve")
        hstage(ggrp, hgrp, h2t, 3, "act")
        em.wait("tensor", "act", em.n("act"))
        for k, st in enumerate(sts):
            lt = dma("sync", "dx", ox[:, :], xT[st],
                     waits=[ox_free, xslot[st]])
            for hf in range(2):
                ft = None
                for oc8 in range(DC):
                    buf = oc8 % 2
                    ls = load_slice(w3t, oc8, buf, 4096)
                    em.wait("tensor", "dw", ls[1])
                    bank = "pw2" if oc8 % 2 == 0 else "pc0"
                    pt = pw[2] if oc8 % 2 == 0 else pc[0]
                    pe_use(bank)
                    t = None
                    for ic in range(32):
                        t = mm(pt[:, 0:256], wsl[buf][:, ic * 128:(ic + 1) * 128],
                               hgrp[:, ic * 1024 + k * 512 + hf * 256: ic * 1024 + k * 512 + hf * 256 + 256],
                               start=(ic == 0), stop=(ic == 31),
                               inc="pe" if ic == 31 else None)
                    wsl_free[buf] = [("pe", t[1])]
                    em.wait("scalar", "pe", t[1])
                    ft = em.op("scalar", lambda e, oc8=oc8, pt=pt: e.activation(
                        fsb[:, oc8 * 256:(oc8 + 1) * 256], pt[:, 0:256], AF.Copy, bias=0.0,
                        scale=scol(oc8, 4)), inc="act")
                    bank_done(bank, ("act", ft[1]))
                em.wait("vector", "act", ft[1])
                ln_stats(fsb, ggrp[:, 0:2048], ggrp[:, 2048:4096], W=256)
                em.wait("vector", "dx", lt[1])
                ot = None
                for c in range(DC):
                    def mk(c=c, hf=hf):
                        def sink(tB):
                            return em.op("vector", lambda e: e.tensor_tensor(
                                fsb[:, c * 256:(c + 1) * 256], tB,
                                ox[:, c * 512 + hf * 256: c * 512 + hf * 256 + 256],
                                ALU.add), inc="dve")
                        return sink
                    ot = ln_apply(fsb, c, mk(), W=256)
                em.wait("tensor", "dve", ot[1])
                for tt2 in range(2):
                    em.wait_tok("vector", outtm_free[0])
                    dt = None
                    for c in range(DC):
                        pidx = 3 + (c % 2)
                        pe_use("pw%d" % pidx)
                        tt = em.op("tensor", lambda e, c=c, tt2=tt2, p=pidx: e.transpose(
                            pw[p][:, 0:128],
                            fsb[:, c * 256 + tt2 * 128: c * 256 + (tt2 + 1) * 128],
                            id_f[:, :]), inc="pe")
                        em.wait("vector", "pe", tt[1])
                        dt = em.op("vector", lambda e, c=c, p=pidx: e.tensor_copy(
                            outtm[:, c * 128:(c + 1) * 128], pw[p][:, 0:128]), inc="dve")
                        bank_done("pw%d" % pidx, ("dve", dt[1]))
                    tile_idx = st * 4 + hf * 2 + tt2
                    od = dma("sync", "dout", out[tile_idx * 128:(tile_idx + 1) * 128, :], outtm[:, :],
                             waits=[("dve", dt[1])])
                    outtm_free[0] = ("dout", od[1])
            ox_free = [("dve", em.n("dve"))]


def _pcw(a, p=128):
    cp, w = a.shape
    c = cp // p
    return np.ascontiguousarray(a.reshape(c, p, w).transpose(1, 0, 2).reshape(p, c * w))


def prep_inputs(inputs):
    x = np.asarray(inputs["L0_attn_x"], np.float32).reshape(T * NCORES, D)
    shards = [np.ascontiguousarray(
        x[c * T:(c + 1) * T].reshape(NST, 4, 128, D).transpose(0, 2, 1, 3).reshape(NST, 128, 4 * D))
        for c in range(NCORES)]

    lnp = np.stack([
        np.asarray(inputs["L0_attn_ln1_w"], np.float32),
        np.asarray(inputs["L0_attn_ln1_b"], np.float32),
        np.asarray(inputs["L0_attn_ln2_w"], np.float32),
        np.asarray(inputs["L0_attn_ln2_b"], np.float32),
        np.asarray(inputs["L0_attn_bo"], np.float32),
    ], axis=1)
    lncp = np.ascontiguousarray(
        lnp.reshape(DEPTH, 5, DC, 128).transpose(0, 3, 2, 1).reshape(DEPTH, 128, 40))

    sp = np.zeros((5, FF), np.float32)
    sp[0] = np.asarray(inputs["L0_ffn_w1_s"], np.float32)
    sp[1] = np.asarray(inputs["L0_ffn_w2_s"], np.float32)
    sp[2] = np.asarray(inputs["L0_ffn_h1_s"], np.float32)
    sp[3] = np.asarray(inputs["L0_ffn_h2_s"], np.float32)
    sp[4, 0:D] = np.asarray(inputs["L0_ffn_w3_s"], np.float32)
    scp = np.ascontiguousarray(sp.reshape(5, 32, 128).transpose(2, 1, 0).reshape(128, 160))

    def wslices(wT, nslice, width):
        I, O = wT.shape
        c = I // 128
        a = wT.reshape(c, 128, nslice, width).transpose(2, 1, 0, 3)
        return np.ascontiguousarray(a.reshape(nslice, 128, c * width))

    w1f = np.asarray(inputs["L0_attn_w1"], np.float32)
    w2f = np.asarray(inputs["L0_attn_w2"], np.float32)
    w1h = np.stack([np.stack([_pcw(w1f[li][:, h * 2048:(h + 1) * 2048]) for h in range(2)])
                    for li in range(DEPTH)])
    w2h = np.stack([np.stack([_pcw(w2f[li][h * 2048:(h + 1) * 2048, :]) for h in range(2)])
                    for li in range(DEPTH)])
    common = {
        "wq": np.stack([_pcw(np.asarray(inputs["L0_attn_wq"], np.float32)[li]) for li in range(DEPTH)]),
        "wk": np.stack([_pcw(np.asarray(inputs["L0_attn_wk"], np.float32)[li]) for li in range(DEPTH)]),
        "wv": np.stack([_pcw(np.asarray(inputs["L0_attn_wv"], np.float32)[li]) for li in range(DEPTH)]),
        "wo": np.stack([_pcw(np.asarray(inputs["L0_attn_wo"], np.float32)[li]) for li in range(DEPTH)]),
        "w1": w1h,
        "w2": w2h,
        "lncp": lncp,
        "w1t": wslices(np.asarray(inputs["L0_ffn_w1_w"], np.float32).T, 32, 128),
        "w2t": wslices(np.asarray(inputs["L0_ffn_w2_w"], np.float32).T, 32, 128),
        "h1t": wslices(np.asarray(inputs["L0_ffn_h1_w"], np.float32).T, 32, 128),
        "h2t": wslices(np.asarray(inputs["L0_ffn_h2_w"], np.float32).T, 32, 128),
        "w3t": wslices(np.asarray(inputs["L0_ffn_w3_w"], np.float32).T, 8, 128),
        "scp": scp,
        "idf": np.eye(128, dtype=np.float32),
        "cbf": np.ascontiguousarray(np.stack(
            [np.full(128, 1.0 / D), np.ones(128)], axis=1).astype(ml_dtypes.bfloat16)),
        "orf": np.ones((1, 128), np.float32),
    }
    return shards, common


def kernel(**inputs):
    shards, common = prep_inputs(inputs)
    nc = build_nc()
    in_maps = [dict(common, x0=shards[c]) for c in range(NCORES)]
    res = run_bass_kernel_spmd(nc, in_maps, core_ids=list(range(NCORES)))
    outp = np.concatenate([np.asarray(res.results[c]["out"]) for c in range(NCORES)], axis=0)
    return outp.reshape(2, 16384, D)


# version 03
# speedup vs baseline: 1.0975x; 1.0669x over previous
import numpy as np
import ml_dtypes
from contextlib import ExitStack
import concourse.bass as bass
import concourse.mybir as mybir
from concourse.bass_utils import run_bass_kernel_spmd

F32 = mybir.dt.float32
BF16 = mybir.dt.bfloat16
AF = mybir.ActivationFunctionType
ALU = mybir.AluOpType

NCORES = 8
T = 4096
NST = 8
STW = 512
D = 1024
DC = 8
H = 8
FF = 4096
DEPTH = 6
EPS = 1e-5

ENGS = ("sync", "tensor", "vector", "scalar", "gpsimd")


def ctx_pos(h):
    if h < 6:
        b, j = h // 3, h % 3
        return b, b * 387 + j * 128, b * 387 + 384 + j
    b, j = 2, h - 6
    return 2, 774 + j * 128, 774 + 256 + j


class Emitter:
    def __init__(self):
        self.q = {e: [] for e in ENGS}
        self.cnt = {}
        self.lastwait = {}

    def n(self, sem):
        return self.cnt.get(sem, 0)

    def wait(self, eng, sem, val):
        if val is None or val <= 0:
            return
        key = (eng, sem)
        if self.lastwait.get(key, -1) >= val:
            return
        self.lastwait[key] = val
        self.q[eng].append(("w", sem, val))

    def wait_tok(self, eng, tok):
        if tok is None:
            return
        if isinstance(tok, list):
            for t in tok:
                self.wait_tok(eng, t)
        else:
            self.wait(eng, tok[0], tok[1])

    def op(self, eng, fn, inc=None, amt=1):
        if inc is None:
            self.q[eng].append(("i", fn, None, 0))
            return None
        self.cnt[inc] = self.cnt.get(inc, 0) + amt
        tot = self.cnt[inc]
        self.q[eng].append(("i", fn, inc, amt))
        return (inc, tot)

    def flush(self, block, sems):
        q = {e: list(self.q[e]) for e in ENGS}
        for e in ENGS:
            self.q[e] = []

        def replay(engine, items):
            for item in items:
                if item[0] == "w":
                    engine.wait_ge(sems[item[1]], item[2])
                else:
                    _, fn, inc, amt = item
                    ins = fn(engine)
                    if inc is not None:
                        ins.then_inc(sems[inc], amt)

        if q["sync"]:
            @block.sync
            def _(e, it=q["sync"]):
                replay(e, it)
        if q["tensor"]:
            @block.tensor
            def _(e, it=q["tensor"]):
                replay(e, it)
        if q["vector"]:
            @block.vector
            def _(e, it=q["vector"]):
                replay(e, it)
        if q["scalar"]:
            @block.scalar
            def _(e, it=q["scalar"]):
                replay(e, it)
        if q["gpsimd"]:
            @block.gpsimd
            def _(e, it=q["gpsimd"]):
                replay(e, it)


def build_nc(debug=None):
    nc = bass.Bass("TRN2", target_bir_lowering=False)

    x0 = nc.dram_tensor("x0", [NST, 128, 4 * D], F32, kind="ExternalInput").ap()
    wq_d = nc.dram_tensor("wq", [DEPTH, 128, DC * D], F32, kind="ExternalInput").ap()
    wk_d = nc.dram_tensor("wk", [DEPTH, 128, DC * D], F32, kind="ExternalInput").ap()
    wv_d = nc.dram_tensor("wv", [DEPTH, 128, DC * D], F32, kind="ExternalInput").ap()
    wo_d = nc.dram_tensor("wo", [DEPTH, 128, DC * D], F32, kind="ExternalInput").ap()
    w1_d = nc.dram_tensor("w1", [DEPTH, 2, 128, 2 * DC * D], F32, kind="ExternalInput").ap()
    w2_d = nc.dram_tensor("w2", [DEPTH, 2, 128, 2 * DC * D], F32, kind="ExternalInput").ap()
    lncp = nc.dram_tensor("lncp", [DEPTH, 128, 40], F32, kind="ExternalInput").ap()
    w1t = nc.dram_tensor("w1t", [32, 128, 1024], F32, kind="ExternalInput").ap()
    w2t = nc.dram_tensor("w2t", [32, 128, 1024], F32, kind="ExternalInput").ap()
    h1t = nc.dram_tensor("h1t", [32, 128, 4096], F32, kind="ExternalInput").ap()
    h2t = nc.dram_tensor("h2t", [32, 128, 4096], F32, kind="ExternalInput").ap()
    w3t = nc.dram_tensor("w3t", [8, 128, 4096], F32, kind="ExternalInput").ap()
    scp = nc.dram_tensor("scp", [128, 160], F32, kind="ExternalInput").ap()
    idf = nc.dram_tensor("idf", [128, 128], F32, kind="ExternalInput").ap()
    cbf = nc.dram_tensor("cbf", [128, 2], BF16, kind="ExternalInput").ap()
    orf = nc.dram_tensor("orf", [1, 128], F32, kind="ExternalInput").ap()

    out = nc.dram_tensor("out", [T, D], F32, kind="ExternalOutput").ap()

    xT = nc.dram_tensor("xT", [NST, 128, DC * STW], F32).ap()
    x0T = nc.dram_tensor("x0T", [NST, 128, DC * STW], F32).ap()
    qT = nc.dram_tensor("qT", [NST, 128, DC * STW], BF16).ap()
    xnT = nc.dram_tensor("xnT", [NST, 128, DC * STW], BF16).ap()
    ccin = nc.dram_tensor("ccin", [128, 1032], F32)
    ccout = nc.dram_tensor("ccout", [128, 1032], F32)

    dbg = {}
    if debug is not None:
        dbg["x"] = nc.dram_tensor("dbg_x", [NST, 128, DC * STW], F32, kind="ExternalOutput").ap()
        dbg["q"] = nc.dram_tensor("dbg_q", [NST, 128, DC * STW], BF16, kind="ExternalOutput").ap()
        dbg["ctx"] = nc.dram_tensor("dbg_ctx", [128, 1032], F32, kind="ExternalOutput").ap()

    em = Emitter()

    with ExitStack() as es:
        E = es.enter_context
        semnames = ["dx", "dxo", "dq", "dw", "dcc", "cc", "pe", "act", "dve", "dout", "dmisc"]
        sems = {s: E(nc.semaphore(s)) for s in semnames}

        mreps = E(nc.sbuf_tensor("mreps", [128, 1040], F32))
        tmpA = E(nc.sbuf_tensor("tmpA", [128, 512], F32))
        rows = E(nc.sbuf_tensor("rows", [1, 1024], F32))
        lncols = [E(nc.sbuf_tensor("lncols0", [128, 40], F32)),
                  E(nc.sbuf_tensor("lncols1", [128, 40], F32))]
        id_f = E(nc.sbuf_tensor("id_f", [128, 128], F32))
        ccol = E(nc.sbuf_tensor("ccol", [128, 2], BF16))
        orow = E(nc.sbuf_tensor("orow", [1, 128], F32))
        epsr = E(nc.sbuf_tensor("epsr", [1, 1], F32))

        pc = [E(nc.psum_tensor("pc%d" % i, [128, 512], F32)) for i in range(3)]
        pw = [E(nc.psum_tensor("pw%d" % i, [128, 512], F32)) for i in range(5)]

        block = E(nc.Block())

        inner_es = ExitStack()
        IE = inner_es.enter_context
        xin = [IE(nc.sbuf_tensor("xin0", [128, DC * STW], F32)),
               IE(nc.sbuf_tensor("xin1", [128, DC * STW], F32))]
        qbuf = [IE(nc.sbuf_tensor("qbuf0", [128, DC * STW], BF16)),
                IE(nc.sbuf_tensor("qbuf1", [128, DC * STW], BF16))]
        w12h = IE(nc.sbuf_tensor("w12h", [128, 4 * DC * D], BF16))
        wqkv = IE(nc.sbuf_tensor("wqkv", [128, 3 * DC * D], BF16))
        wosb = IE(nc.sbuf_tensor("wosb", [128, DC * D], BF16))
        xn1T = IE(nc.sbuf_tensor("xn1T", [128, DC * STW], BF16))
        hq = IE(nc.sbuf_tensor("hq", [128, DC * STW], BF16))
        ekb = IE(nc.sbuf_tensor("ekb", [128, D], BF16))
        vbuf = IE(nc.sbuf_tensor("vbuf", [128, D], BF16))
        aoT = hq
        eqs = hq[:, 0:512]
        ctxbf = ekb
        ctxCC = mreps[:, 0:1032]

        def dma(eng, sem_, outap, inap, waits=()):
            for wtok in waits:
                em.wait_tok(eng, wtok)
            return em.op(eng, lambda e, o=outap, i=inap: e.dma_start(out=o, in_=i), inc=sem_, amt=16)

        bank_free = {}

        def pe_use(bank):
            em.wait_tok("tensor", bank_free.get(bank))

        def bank_done(bank, tok):
            bank_free[bank] = tok

        def mm(outap, lhs, rhs, start, stop, inc=None):
            return em.op("tensor",
                         lambda e, o=outap, l=lhs, r=rhs, st=start, sp=stop:
                         e.matmul(o, l, r, start=st, stop=sp, skip_group_check=True),
                         inc=inc)

        def lncol(bi, c, vec):
            return lncols[bi][:, c * 5 + vec: c * 5 + vec + 1]

        xslot = [None] * NST
        qslot = [None] * NST
        xnslot = [None] * NST
        hq_free = [None]
        xin_free = [None, None]
        qbuf_free = [None, None]

        def ln_stats(xsb, castb, sqb, W=STW):
            ct = em.op("vector", lambda e: e.tensor_copy(castb[:, 0:DC * W], xsb[:, 0:DC * W]), inc="dve")
            em.wait("scalar", "dve", ct[1])
            sq = em.op("scalar", lambda e: e.activation(sqb[:, 0:DC * W], castb[:, 0:DC * W], AF.Square), inc="act")
            em.wait("tensor", "dve", ct[1])
            pe_use("pw0")
            for c in range(DC):
                mm(pw[0][0:1, 0:W], ccol[:, 0:1], castb[:, c * W:(c + 1) * W],
                   start=(c == 0), stop=(c == DC - 1))
            em.wait("tensor", "act", sq[1])
            pe_use("pw1")
            mt = None
            for c in range(DC):
                mt = mm(pw[1][0:1, 0:W], ccol[:, 0:1], sqb[:, c * W:(c + 1) * W],
                        start=(c == 0), stop=(c == DC - 1), inc="pe" if c == DC - 1 else None)
            r_m = pw[0][0:1, 0:W]
            r_s = pw[1][0:1, 0:W]
            em.wait("scalar", "pe", mt[1])
            sqm = em.op("scalar", lambda e: e.activation(rows[0:1, 0:W], r_m, AF.Square), inc="act")
            em.wait("vector", "act", sqm[1])
            vt = em.op("vector", lambda e: e.tensor_tensor(rows[0:1, 512:512 + W], r_s, rows[0:1, 0:W], ALU.subtract), inc="dve")
            em.wait("scalar", "dve", vt[1])
            em.op("scalar", lambda e: e.activation(rows[0:1, 0:W], rows[0:1, 512:512 + W], AF.Ln, bias=epsr[0:1, 0:1], scale=1.0))
            rt = em.op("scalar", lambda e: e.activation(rows[0:1, 512:512 + W], rows[0:1, 0:W], AF.Exp, bias=0.0, scale=-0.5), inc="act")
            em.wait("vector", "act", rt[1])
            mr = em.op("vector", lambda e: e.tensor_tensor(rows[0:1, 0:W], r_m, rows[0:1, 512:512 + W], ALU.mult), inc="dve")
            em.wait("tensor", "dve", mr[1])
            pe_use("pw0")
            pe_use("pw1")
            mm(pw[0][:, 0:W], orow[:, :], rows[0:1, 512:512 + W], start=True, stop=True)
            t2 = mm(pw[1][:, 0:W], orow[:, :], rows[0:1, 0:W], start=True, stop=True, inc="pe")
            em.wait("vector", "pe", t2[1])
            em.op("vector", lambda e: e.tensor_copy(mreps[:, 0:W], pw[0][:, 0:W]))
            c2 = em.op("vector", lambda e: e.tensor_copy(mreps[:, 512:512 + W], pw[1][:, 0:W]), inc="dve")
            bank_done("pw0", ("dve", c2[1]))
            bank_done("pw1", ("dve", c2[1]))
            return c2

        def ln_apply(xsb, c, sink, wcol=None, bcol=None, W=STW):
            xc = xsb[:, c * W:(c + 1) * W]
            em.op("vector", lambda e: e.tensor_tensor(tmpA[:, 0:W], xc, mreps[:, 0:W], ALU.mult))
            em.op("vector", lambda e: e.tensor_tensor(tmpA[:, 0:W], tmpA[:, 0:W], mreps[:, 512:512 + W], ALU.subtract))
            if wcol is not None:
                return em.op("vector", lambda e, w=wcol, b=bcol: e.tensor_scalar(
                    sink, tmpA[:, 0:W], w, b, ALU.mult, ALU.add), inc="dve")
            if callable(sink):
                return sink(tmpA[:, 0:W])
            return em.op("vector", lambda e: e.tensor_copy(sink, tmpA[:, 0:W]), inc="dve")

        def load_wqkv(li):
            gw = ("pe", em.n("pe"))
            dma("gpsimd", "dw", wqkv[:, 0:DC * D], wq_d[li], waits=[gw])
            dma("gpsimd", "dw", wqkv[:, DC * D:2 * DC * D], wk_d[li])
            t = dma("gpsimd", "dw", wqkv[:, 2 * DC * D:3 * DC * D], wv_d[li])
            return ("dw", t[1])

        def load_wo(li):
            t = dma("gpsimd", "dw", wosb[:, :], wo_d[li],
                    waits=[("pe", em.n("pe"))])
            return ("dw", t[1])

        def load_w12h(li, half):
            gw = ("pe", em.n("pe"))
            dma("gpsimd", "dw", w12h[:, 0:2 * DC * D], w1_d[li, half], waits=[gw])
            t = dma("gpsimd", "dw", w12h[:, 2 * DC * D:4 * DC * D], w2_d[li, half])
            return ("dw", t[1])

        def load_lncols(li, bi):
            t = dma("sync", "dmisc", lncols[bi][:, :], lncp[li],
                    waits=[("dve", em.n("dve"))])
            return ("dmisc", t[1])

        dma("sync", "dmisc", id_f[:, :], idf)
        dma("sync", "dmisc", ccol[:, :], cbf)
        ct0 = dma("sync", "dmisc", orow[:, :], orf)
        em.op("vector", lambda e: e.memset(epsr[:, :], EPS), inc="dve")
        for eng in ("tensor", "vector", "scalar"):
            em.wait(eng, "dmisc", ct0[1])

        wqkv_rdy = {0: load_wqkv(0)}
        wo_rdy = {0: load_wo(0)}
        w12_rdy = {(0, 0): load_w12h(0, 0)}
        ln_rdy = {0: load_lncols(0, 0)}

        for st in range(NST):
            lt = dma("sync", "dx", xin[0][:, :], x0[st],
                     waits=[xin_free[0], xin_free[1]])
            em.wait("tensor", "dx", lt[1])
            ctok = None
            for t4 in range(4):
                for c in range(DC):
                    pidx = 2 + ((t4 * DC + c) % 2)
                    pe_use("pw%d" % pidx)
                    tt = em.op("tensor", lambda e, t4=t4, c=c, p=pidx: e.transpose(
                        pw[p][:, 0:128], xin[0][:, t4 * D + c * 128:t4 * D + c * 128 + 128],
                        id_f[:, :]), inc="pe")
                    em.wait("vector", "pe", tt[1])
                    ctok = em.op("vector", lambda e, t4=t4, c=c, p=pidx: e.tensor_copy(
                        xin[1][:, c * 512 + t4 * 128: c * 512 + t4 * 128 + 128], pw[p][:, 0:128]),
                        inc="dve")
                    bank_done("pw%d" % pidx, ("dve", ctok[1]))
            s0 = dma("sync", "dxo", x0T[st], xin[1][:, :],
                     waits=[("dve", ctok[1])])
            em.wait_tok("scalar", hq_free[0])
            ln_stats(xin[1], xn1T, hq)
            atok = None
            for c in range(DC):
                atok = ln_apply(xin[1], c, xin[0][:, c * 512:(c + 1) * 512])
            s1 = dma("sync", "dxo", xT[st], xin[0][:, :],
                     waits=[("dve", atok[1])])
            xslot[st] = ("dxo", s1[1])
            em.wait_tok("vector", ln_rdy[0])
            ln_stats(xin[0], xn1T, hq)
            at2 = None
            for c in range(DC):
                at2 = ln_apply(xin[0], c, xn1T[:, c * 512:(c + 1) * 512],
                               wcol=lncol(0, c, 0), bcol=lncol(0, c, 1))
            s2 = dma("sync", "dq", xnT[st], xn1T[:, :], waits=[("dve", at2[1])])
            xnslot[st] = ("dq", s2[1])
            hq_free[0] = ("dq", s2[1])
            xin_free[0] = [("dxo", s1[1]), ("dve", at2[1])]
            xin_free[1] = [("dxo", s0[1]), ("dve", atok[1])]

        def sweep_A(li):
            em.wait_tok("tensor", wqkv_rdy[li])
            em.wait_tok("scalar", hq_free[0])
            wq_off, wk_off, wv_off = 0, DC * D, 2 * DC * D
            ntqs = {}
            for st in range(NST):
                xb = st % 2
                qb = st % 2
                xna = xin[xb][:, :].bitcast(BF16)[:, 0:DC * STW]
                lt = dma("sync", "dx", xna, xnT[st],
                         waits=[xin_free[xb], xnslot[st]])
                em.wait("tensor", "dx", lt[1])
                etoks = {}
                rts = {}

                def emit_qmm(oc, chalf=None):
                    pidx = 2 + (oc % 2)
                    crange = range(DC) if chalf is None else (
                        range(0, DC // 2) if chalf == 0 else range(DC // 2, DC))
                    if chalf in (None, 0):
                        pe_use("pw%d" % pidx)
                    qmt = None
                    for c in crange:
                        qmt = mm(pw[pidx][:, :],
                                 wqkv[:, wq_off + c * D + oc * 128: wq_off + c * D + oc * 128 + 128],
                                 xna[:, c * 512:(c + 1) * 512],
                                 start=(c == 0), stop=(c == DC - 1),
                                 inc="pe" if c == DC - 1 else None)
                    if chalf in (None, 1):
                        em.wait("scalar", "pe", qmt[1])
                        if (st * H + oc) >= 2:
                            prev = ntqs.get((st * H + oc) - 2)
                            if prev is not None:
                                em.wait("scalar", "dve", prev[1])
                        et = em.op("scalar", lambda e, p=pidx, oc=oc: e.activation(
                            hq[:, (oc % 2) * 512:(oc % 2) * 512 + 512], pw[p][:, :], AF.Exp),
                            inc="act")
                        bank_done("pw%d" % pidx, ("act", et[1]))
                        etoks[oc] = et

                def emit_zq(oc):
                    em.wait("tensor", "act", etoks[oc][1])
                    pe_use("pw4")
                    zt = mm(pw[4][0:1, 0:512], ccol[:, 1:2],
                            hq[:, (oc % 2) * 512:(oc % 2) * 512 + 512],
                            start=True, stop=True, inc="pe")
                    em.wait("vector", "pe", zt[1])
                    rt = em.op("vector", lambda e, oc=oc: e.reciprocal(
                        rows[0:1, (oc % 2) * 512:(oc % 2) * 512 + 512], pw[4][0:1, 0:512]),
                        inc="dve")
                    bank_done("pw4", ("dve", rt[1]))
                    rts[oc] = rt

                def emit_zrep_ntq(oc):
                    em.wait("tensor", "dve", rts[oc][1])
                    pe_use("pw4")
                    zr = mm(pw[4][:, :], orow[:, :],
                            rows[0:1, (oc % 2) * 512:(oc % 2) * 512 + 512],
                            start=True, stop=True, inc="pe")
                    em.wait("vector", "pe", zr[1])
                    if oc == 0:
                        em.wait_tok("vector", qbuf_free[qb])
                    ntq = em.op("vector", lambda e, oc=oc, qb=qb: e.tensor_tensor(
                        qbuf[qb][:, oc * 512:(oc + 1) * 512],
                        hq[:, (oc % 2) * 512:(oc % 2) * 512 + 512],
                        pw[4][:, :], ALU.mult), inc="dve")
                    bank_done("pw4", ("dve", ntq[1]))
                    ntqs[st * H + oc] = ntq
                    return ntq

                emit_qmm(0)
                emit_qmm(1)
                ntq = None
                for oc in range(H):
                    emit_zq(oc)
                    if oc + 2 < H:
                        emit_qmm(oc + 2, chalf=0)
                    ntq = emit_zrep_ntq(oc)
                    if oc + 2 < H:
                        emit_qmm(oc + 2, chalf=1)
                qs = dma("sync", "dq", qT[st], qbuf[qb][:, :],
                         waits=[("dve", ntq[1]), qslot[st]])
                qslot[st] = ("dq", qs[1])
                qbuf_free[qb] = [("dq", qs[1])]
                for t4 in range(4):
                    first = (st == 0 and t4 == 0)
                    last = (st == NST - 1 and t4 == 3)
                    ktoks = []
                    for f in range(2):
                        pe_use("pw%d" % (2 + f))
                        kt = None
                        for c in range(DC):
                            kt = mm(pw[2 + f][:, :],
                                    xna[:, c * 512 + t4 * 128: c * 512 + t4 * 128 + 128],
                                    wqkv[:, wk_off + c * D + f * 512: wk_off + c * D + (f + 1) * 512],
                                    start=(c == 0), stop=(c == DC - 1),
                                    inc="pe" if c == DC - 1 else None)
                        ktoks.append(kt)
                        em.wait("scalar", "pe", kt[1])
                        et = em.op("scalar", lambda e, f=f: e.activation(
                            ekb[:, f * 512:(f + 1) * 512], pw[2 + f][:, :], AF.Exp), inc="act")
                        bank_done("pw%d" % (2 + f), ("act", et[1]))
                        ktoks[f] = ("act", et[1])
                    vcts = []
                    for f in range(2):
                        pe_use("pw%d" % f)
                        vt = None
                        for c in range(DC):
                            vt = mm(pw[f][:, :],
                                    xna[:, c * 512 + t4 * 128: c * 512 + t4 * 128 + 128],
                                    wqkv[:, wv_off + c * D + f * 512: wv_off + c * D + (f + 1) * 512],
                                    start=(c == 0), stop=(c == DC - 1),
                                    inc="pe" if c == DC - 1 else None)
                        em.wait("vector", "pe", vt[1])
                        vct = em.op("vector", lambda e, f=f: e.tensor_copy(
                            vbuf[:, f * 512:(f + 1) * 512], pw[f][:, :]), inc="dve")
                        bank_done("pw%d" % f, ("dve", vct[1]))
                        vcts.append(vct)
                    for f in range(2):
                        em.wait("tensor", "act", ktoks[f][1])
                        em.wait("tensor", "dve", vcts[f][1])
                        for h in range(4 * f, 4 * f + 4):
                            b, coff, zoff = ctx_pos(h)
                            mm(pc[b][:, coff - b * 387:coff - b * 387 + 128],
                               ekb[:, h * 128:(h + 1) * 128],
                               vbuf[:, h * 128:(h + 1) * 128],
                               start=(first and h in (0, 3, 6)), stop=last)
                            mm(pc[b][:, zoff - b * 387: zoff - b * 387 + 1],
                               ekb[:, h * 128:(h + 1) * 128],
                               ccol[:, 1:2],
                               start=False, stop=last,
                               inc="pe" if (last and h == H - 1) else None)
                xin_free[xb] = [("pe", em.n("pe"))]
            em.wait("vector", "pe", em.n("pe"))
            em.op("vector", lambda e: e.tensor_copy(ctxCC[:, 0:387], pc[0][:, 0:387]))
            em.op("vector", lambda e: e.tensor_copy(ctxCC[:, 387:774], pc[1][:, 0:387]))
            ev = em.op("vector", lambda e: e.tensor_copy(ctxCC[:, 774:1032], pc[2][:, 0:258]), inc="dve")
            for b in range(3):
                bank_done("pc%d" % b, ("dve", ev[1]))
            ci = dma("gpsimd", "dcc", ccin[:, :], ctxCC, waits=[("dve", ev[1])])
            em.wait("gpsimd", "dcc", ci[1])
            cct = em.op("gpsimd", lambda e: e.collective_compute(
                "AllReduce", ALU.add,
                replica_groups=[[0, 1, 2, 3], [4, 5, 6, 7]],
                ins=[ccin.ap().opt()], outs=[ccout.ap().opt()]), inc="cc")
            em.wait("gpsimd", "cc", cct[1])
            co = dma("gpsimd", "dcc", ctxCC, ccout[:, :])
            return co

        def sweep_B(li, co):
            lnb = li % 2
            em.wait_tok("tensor", wo_rdy[li])
            em.wait("vector", "dcc", co[1])
            rc = None
            for h in range(H):
                b, coff, zoff = ctx_pos(h)
                rc = em.op("vector", lambda e, z=zoff, h=h: e.reciprocal(
                    tmpA[:, h:h + 1], ctxCC[:, z:z + 1]),
                    inc="dve" if h == H - 1 else None)
            em.wait("vector", "dve", rc[1])
            nt = None
            for h in range(H):
                b, coff, zoff = ctx_pos(h)
                nt = em.op("vector", lambda e, h=h, c0=coff: e.tensor_scalar(
                    ctxbf[:, h * 128:(h + 1) * 128], ctxCC[:, c0:c0 + 128],
                    tmpA[:, h:h + 1], None, ALU.mult), inc="dve")
            em.wait("tensor", "dve", nt[1])
            for st in range(NST):
                xb = st % 2
                qb = st % 2
                lt = dma("sync", "dx", xin[xb][:, :], xT[st],
                         waits=[xin_free[xb], xslot[st]])
                qlt = dma("sync", "dq", qbuf[qb][:, :], qT[st],
                          waits=[qbuf_free[qb], qslot[st]])
                xsb = xin[xb]
                em.wait("tensor", "dq", qlt[1])
                if li == 0 and st == 0:
                    pass
                aot = None
                ao_banks = [("pw4", pw[4]), ("pc0", pc[0]), ("pc1", pc[1]), ("pc2", pc[2])]
                for h in range(H):
                    bank, pt = ao_banks[h % 4]
                    pe_use(bank)
                    t = mm(pt[:, :], ctxbf[:, h * 128:(h + 1) * 128],
                           qbuf[qb][:, h * 512:(h + 1) * 512], start=True, stop=True, inc="pe")
                    em.wait("scalar", "pe", t[1])
                    aot = em.op("scalar", lambda e, h=h, pt=pt: e.activation(
                        aoT[:, h * 512:(h + 1) * 512], pt[:, :], AF.Copy), inc="act")
                    bank_done(bank, ("act", aot[1]))
                ao_done = ("pe", em.n("pe"))
                em.wait("tensor", "act", aot[1])
                em.wait("vector", "dx", lt[1])
                for half in range(2):
                    wtoks = []
                    for ds in range(4):
                        d_idx = half * 4 + ds
                        pe_use("pw%d" % ds)
                        t = None
                        for ec in range(DC):
                            t = mm(pw[ds][:, :],
                                   wosb[:, ec * D + d_idx * 128: ec * D + d_idx * 128 + 128],
                                   aoT[:, ec * 512:(ec + 1) * 512],
                                   start=(ec == 0), stop=(ec == DC - 1),
                                   inc="pe" if ec == DC - 1 else None)
                        wtoks.append(t)
                    for ds in range(4):
                        d_idx = half * 4 + ds
                        em.wait("vector", "pe", wtoks[ds][1])
                        em.op("vector", lambda e, ds=ds, d=d_idx, b=lnb: e.tensor_scalar(
                            tmpA[:, :], pw[ds][:, :], lncol(b, d, 4), None, ALU.add))
                        rt = em.op("vector", lambda e, d=d_idx, xsb=xsb: e.tensor_tensor(
                            xsb[:, d * 512:(d + 1) * 512], xsb[:, d * 512:(d + 1) * 512],
                            tmpA[:, :], ALU.add), inc="dve")
                        bank_done("pw%d" % ds, ("dve", rt[1]))
                ln_stats(xsb, xn1T, hq)
                em.wait_tok("vector", [ao_done])
                at = None
                for c in range(DC):
                    at = ln_apply(xsb, c, qbuf[qb][:, c * 512:(c + 1) * 512],
                                  wcol=lncol(lnb, c, 2), bcol=lncol(lnb, c, 3))
                qs = dma("sync", "dq", qT[st], qbuf[qb][:, :], waits=[("dve", at[1])])
                qslot[st] = ("dq", qs[1])
                qbuf_free[qb] = [("dq", qs[1])]
                so = dma("sync", "dxo", xT[st], xsb[:, :],
                         waits=[("dve", at[1])])
                xslot[st] = ("dxo", so[1])
                xin_free[xb] = [("dxo", so[1])]

        def sweep_F(li, half, last=False, stash=False):
            em.wait_tok("tensor", w12_rdy[(li, half)])
            if stash:
                em.wait_tok("vector", ln_rdy[li + 1])
            for st in range(NST):
                xb = st % 2
                qb = st % 2
                lt = dma("sync", "dx", xin[xb][:, :], xT[st],
                         waits=[xin_free[xb], xslot[st]])
                qlt = dma("sync", "dq", qbuf[qb][:, :], qT[st],
                          waits=[qbuf_free[qb], qslot[st]])
                xsb = xin[xb]
                em.wait("tensor", "dq", qlt[1])
                em.wait("vector", "dx", lt[1])
                yt = None
                for Q in range(2):
                    hts = []
                    for oc8 in range(DC):
                        bank = "pw4" if oc8 % 2 == 0 else "pc0"
                        pt = pw[4] if oc8 % 2 == 0 else pc[0]
                        pe_use(bank)
                        t = None
                        for c in range(DC):
                            t = mm(pt[:, :],
                                   w12h[:, c * 2048 + Q * 1024 + oc8 * 128: c * 2048 + Q * 1024 + oc8 * 128 + 128],
                                   qbuf[qb][:, c * 512:(c + 1) * 512],
                                   start=(c == 0), stop=(c == DC - 1),
                                   inc="pe" if c == DC - 1 else None)
                        em.wait("scalar", "pe", t[1])
                        em.wait_tok("scalar", hq_free[0])
                        ht = em.op("scalar", lambda e, oc8=oc8, pt=pt: e.activation(
                            hq[:, oc8 * 512:(oc8 + 1) * 512], pt[:, :], AF.Lrelu), inc="act")
                        bank_done(bank, ("act", ht[1]))
                        hts.append(ht)
                    em.wait("tensor", "act", hts[-1][1])
                    for dh in range(2):
                        ytoks = []
                        for ds in range(4):
                            d_idx = dh * 4 + ds
                            pe_use("pw%d" % ds)
                            t = None
                            for oc8 in range(DC):
                                woff = 2 * DC * D + (Q * DC + oc8) * D + d_idx * 128
                                t = mm(pw[ds][:, :],
                                       w12h[:, woff: woff + 128],
                                       hq[:, oc8 * 512:(oc8 + 1) * 512],
                                       start=(oc8 == 0), stop=(oc8 == DC - 1),
                                       inc="pe" if oc8 == DC - 1 else None)
                            ytoks.append(t)
                        for ds in range(4):
                            d_idx = dh * 4 + ds
                            em.wait("vector", "pe", ytoks[ds][1])
                            yt = em.op("vector", lambda e, ds=ds, d=d_idx, xsb=xsb: e.tensor_tensor(
                                xsb[:, d * 512:(d + 1) * 512], xsb[:, d * 512:(d + 1) * 512],
                                pw[ds][:, :], ALU.add), inc="dve")
                            bank_done("pw%d" % ds, ("dve", yt[1]))
                qbuf_free[qb] = [("pe", em.n("pe"))]
                if not last:
                    so = dma("sync", "dxo", xT[st], xsb[:, :],
                             waits=[("dve", yt[1])])
                    xslot[st] = ("dxo", so[1])
                    if stash:
                        ln_stats(xsb, xn1T, hq)
                        at2 = None
                        for c in range(DC):
                            at2 = ln_apply(xsb, c, hq[:, c * 512:(c + 1) * 512],
                                           wcol=lncol((li + 1) % 2, c, 0),
                                           bcol=lncol((li + 1) % 2, c, 1))
                        s2 = dma("sync", "dq", xnT[st], hq[:, :], waits=[("dve", at2[1])])
                        xnslot[st] = ("dq", s2[1])
                        hq_free[0] = ("dq", s2[1])
                        xin_free[xb] = [("dxo", so[1]), ("dve", at2[1])]
                    else:
                        xin_free[xb] = [("dxo", so[1])]
                else:
                    ln_stats(xsb, xn1T, hq)
                    ob = (st + 1) % 2
                    l0 = dma("sync", "dx", xin[ob][:, :], x0T[st],
                             waits=[xin_free[ob]])
                    em.wait("vector", "dx", l0[1])
                    at = None
                    for c in range(DC):
                        def mk(c=c, ob=ob):
                            def sink(tB):
                                return em.op("vector", lambda e: e.tensor_tensor(
                                    xin[ob][:, c * 512:(c + 1) * 512], tB,
                                    xin[ob][:, c * 512:(c + 1) * 512], ALU.add), inc="dve")
                            return sink
                        at = ln_apply(xsb, c, mk())
                    so = dma("sync", "dxo", xT[st], xin[ob][:, :],
                             waits=[("dve", at[1])])
                    xslot[st] = ("dxo", so[1])
                    xin_free[xb] = [("dve", at[1])]
                    xin_free[ob] = [("dxo", so[1])]

        if isinstance(debug, tuple):
            n_layers, stop_stage = debug[1] + 1, debug[0]
        elif debug == "entry":
            n_layers, stop_stage = 0, None
        else:
            n_layers, stop_stage = DEPTH, None

        for li in range(n_layers):
            co = sweep_A(li)
            if stop_stage == "A" and li == n_layers - 1:
                break
            sweep_B(li, co)
            if stop_stage == "B" and li == n_layers - 1:
                break
            if li + 1 < DEPTH:
                wqkv_rdy[li + 1] = load_wqkv(li + 1)
                ln_rdy[li + 1] = load_lncols(li + 1, (li + 1) % 2)
            sweep_F(li, 0)
            if li + 1 < DEPTH:
                wo_rdy[li + 1] = load_wo(li + 1)
            w12_rdy[(li, 1)] = load_w12h(li, 1)
            sweep_F(li, 1, last=(li == DEPTH - 1), stash=(li + 1 < DEPTH))
            if li + 1 < DEPTH:
                w12_rdy[(li + 1, 0)] = load_w12h(li + 1, 0)

        if debug is not None:
            em.wait("sync", "dxo", em.n("dxo"))
            em.wait("sync", "dq", em.n("dq"))
            dma("sync", "dout", dbg["x"], xT)
            dma("sync", "dout", dbg["q"], qT)
            em.wait("gpsimd", "dcc", em.n("dcc"))
            dma("gpsimd", "dout", dbg["ctx"], ccout[:, :])
            em.wait("sync", "dout", em.n("dout"))
            em.wait("gpsimd", "dout", em.n("dout"))
            em.flush(block, sems)
            inner_es.close()
        else:
            em.flush(block, sems)
            inner_es.close()
            emit_outer(nc, em, sems, dict(
                dma=dma, mm=mm, pe_use=pe_use, bank_done=bank_done,
                ln_stats=ln_stats, ln_apply=ln_apply,
                xT=xT, pw=pw, pc=pc, rows=rows, id_f=id_f, ccol=ccol,
                orow=orow, epsr=epsr, tmpA=tmpA, mreps=mreps, out=out,
                xslot=xslot, w1t=w1t, w2t=w2t, h1t=h1t, h2t=h2t,
                w3t=w3t, scp=scp), E, em)
            em.wait("sync", "dout", em.n("dout"))
            em.flush(block, sems)

    return nc


def emit_outer(nc, em, sems, L, E, _em):
    dma, mm, pe_use, bank_done = L["dma"], L["mm"], L["pe_use"], L["bank_done"]
    ln_stats, ln_apply = L["ln_stats"], L["ln_apply"]
    xT, pw, pc = L["xT"], L["pw"], L["pc"]
    id_f, ccol = L["id_f"], L["ccol"]
    out = L["out"]
    xslot = L["xslot"]
    w1t, w2t, h1t, h2t, w3t, scp = (L[k] for k in ("w1t", "w2t", "h1t", "h2t", "w3t", "scp"))

    ox = E(nc.sbuf_tensor("ox", [128, DC * STW], F32))
    hgrp = E(nc.sbuf_tensor("hgrp", [128, 32 * 1024], BF16))
    ggrp = E(nc.sbuf_tensor("ggrp", [128, 32 * 1024], BF16))
    wsl = [E(nc.sbuf_tensor("wsl0", [128, 4096], BF16)),
           E(nc.sbuf_tensor("wsl1", [128, 4096], BF16))]
    zT = [E(nc.sbuf_tensor("zT0", [128, 4096], BF16)),
          E(nc.sbuf_tensor("zT1", [128, 4096], BF16))]
    fsb = E(nc.sbuf_tensor("fsb", [128, 2048], F32))
    outtm = E(nc.sbuf_tensor("outtm", [128, 1024], F32))
    scols = E(nc.sbuf_tensor("scols", [128, 160], F32))
    stmp = E(nc.sbuf_tensor("stmp", [128, 1024], BF16))

    sct = dma("sync", "dmisc", scols[:, :], scp)
    for eng in ("tensor", "vector", "scalar"):
        em.wait(eng, "dmisc", sct[1])

    def scol(oc, vec):
        return scols[:, oc * 5 + vec: oc * 5 + vec + 1]

    wsl_free = [None, None]
    outtm_free = [None]

    def load_slice(src, oc, buf, width):
        return dma("gpsimd", "dw", wsl[buf][:, 0:width], src[oc],
                   waits=[wsl_free[buf]])

    ox_free = None
    for g in range(4):
        sts = [2 * g, 2 * g + 1]
        for k, st in enumerate(sts):
            lt = dma("sync", "dx", ox[:, :], xT[st],
                     waits=[ox_free, xslot[st]])
            em.wait("vector", "dx", lt[1])
            ln_stats(ox, hgrp[:, 0:4096], hgrp[:, 4096:8192])
            at = None
            for c in range(DC):
                at = ln_apply(ox, c, zT[k][:, c * 512:(c + 1) * 512])
            ox_free = [("dve", at[1])]
        em.wait("tensor", "dve", em.n("dve"))
        ht = None
        for oc in range(32):
            buf = oc % 2
            dma("gpsimd", "dw", wsl[buf][:, 0:1024], w1t[oc],
                waits=[wsl_free[buf]])
            l2 = dma("gpsimd", "dw", wsl[buf][:, 1024:2048], w2t[oc])
            em.wait("tensor", "dw", l2[1])
            for k in range(2):
                pe_use("pw2")
                for c in range(DC):
                    mm(pw[2][:, :], wsl[buf][:, c * 128:(c + 1) * 128],
                       zT[k][:, c * 512:(c + 1) * 512], start=(c == 0), stop=(c == DC - 1))
                pe_use("pw3")
                tb = None
                for c in range(DC):
                    tb = mm(pw[3][:, :], wsl[buf][:, 1024 + c * 128: 1024 + (c + 1) * 128],
                            zT[k][:, c * 512:(c + 1) * 512], start=(c == 0), stop=(c == DC - 1),
                            inc="pe" if c == DC - 1 else None)
                em.wait("scalar", "pe", tb[1])
                if ht is not None:
                    em.wait("scalar", "dve", ht[1])
                sa = em.op("scalar", lambda e, oc=oc: e.activation(
                    stmp[:, 0:512], pw[2][:, :], AF.Silu, bias=0.0, scale=scol(oc, 0)), inc="act")
                sb2 = em.op("scalar", lambda e, oc=oc: e.activation(
                    stmp[:, 512:1024], pw[3][:, :], AF.Copy, bias=0.0, scale=scol(oc, 1)), inc="act")
                bank_done("pw2", ("act", sa[1]))
                bank_done("pw3", ("act", sb2[1]))
                em.wait("vector", "act", sb2[1])
                ht = em.op("vector", lambda e, oc=oc, k=k: e.tensor_tensor(
                    hgrp[:, oc * 1024 + k * 512: oc * 1024 + (k + 1) * 512],
                    stmp[:, 0:512], stmp[:, 512:1024], ALU.mult), inc="dve")
            wsl_free[buf] = [("dve", ht[1])]

        def hstage(src_grp, dst_grp, wmat, svec_idx, src_sem):
            em.wait("tensor", src_sem, em.n(src_sem))
            for oc in range(32):
                buf = oc % 2
                ls = load_slice(wmat, oc, buf, 4096)
                em.wait("tensor", "dw", ls[1])
                last = None
                for k in range(2):
                    bank = "pw2" if k == 0 else "pc0"
                    pt = pw[2] if k == 0 else pc[0]
                    pe_use(bank)
                    t = None
                    for ic in range(32):
                        t = mm(pt[:, :], wsl[buf][:, ic * 128:(ic + 1) * 128],
                               src_grp[:, ic * 1024 + k * 512: ic * 1024 + (k + 1) * 512],
                               start=(ic == 0), stop=(ic == 31),
                               inc="pe" if ic == 31 else None)
                    em.wait("scalar", "pe", t[1])
                    gt = em.op("scalar", lambda e, oc=oc, k=k, pt=pt, dst=dst_grp: e.activation(
                        dst[:, oc * 1024 + k * 512: oc * 1024 + (k + 1) * 512],
                        pt[:, :], AF.Gelu, bias=0.0, scale=scol(oc, svec_idx)), inc="act")
                    bank_done(bank, ("act", gt[1]))
                    last = gt
                wsl_free[buf] = [("act", last[1])]

        hstage(hgrp, ggrp, h1t, 2, "dve")
        hstage(ggrp, hgrp, h2t, 3, "act")
        em.wait("tensor", "act", em.n("act"))
        for k, st in enumerate(sts):
            lt = dma("sync", "dx", ox[:, :], xT[st],
                     waits=[ox_free, xslot[st]])
            for hf in range(2):
                ft = None
                for oc8 in range(DC):
                    buf = oc8 % 2
                    ls = load_slice(w3t, oc8, buf, 4096)
                    em.wait("tensor", "dw", ls[1])
                    bank = "pw2" if oc8 % 2 == 0 else "pc0"
                    pt = pw[2] if oc8 % 2 == 0 else pc[0]
                    pe_use(bank)
                    t = None
                    for ic in range(32):
                        t = mm(pt[:, 0:256], wsl[buf][:, ic * 128:(ic + 1) * 128],
                               hgrp[:, ic * 1024 + k * 512 + hf * 256: ic * 1024 + k * 512 + hf * 256 + 256],
                               start=(ic == 0), stop=(ic == 31),
                               inc="pe" if ic == 31 else None)
                    wsl_free[buf] = [("pe", t[1])]
                    em.wait("scalar", "pe", t[1])
                    ft = em.op("scalar", lambda e, oc8=oc8, pt=pt: e.activation(
                        fsb[:, oc8 * 256:(oc8 + 1) * 256], pt[:, 0:256], AF.Copy, bias=0.0,
                        scale=scol(oc8, 4)), inc="act")
                    bank_done(bank, ("act", ft[1]))
                em.wait("vector", "act", ft[1])
                ln_stats(fsb, ggrp[:, 0:2048], ggrp[:, 2048:4096], W=256)
                em.wait("vector", "dx", lt[1])
                ot = None
                for c in range(DC):
                    def mk(c=c, hf=hf):
                        def sink(tB):
                            return em.op("vector", lambda e: e.tensor_tensor(
                                fsb[:, c * 256:(c + 1) * 256], tB,
                                ox[:, c * 512 + hf * 256: c * 512 + hf * 256 + 256],
                                ALU.add), inc="dve")
                        return sink
                    ot = ln_apply(fsb, c, mk(), W=256)
                em.wait("tensor", "dve", ot[1])
                for tt2 in range(2):
                    em.wait_tok("vector", outtm_free[0])
                    dt = None
                    for c in range(DC):
                        pidx = 3 + (c % 2)
                        pe_use("pw%d" % pidx)
                        tt = em.op("tensor", lambda e, c=c, tt2=tt2, p=pidx: e.transpose(
                            pw[p][:, 0:128],
                            fsb[:, c * 256 + tt2 * 128: c * 256 + (tt2 + 1) * 128],
                            id_f[:, :]), inc="pe")
                        em.wait("vector", "pe", tt[1])
                        dt = em.op("vector", lambda e, c=c, p=pidx: e.tensor_copy(
                            outtm[:, c * 128:(c + 1) * 128], pw[p][:, 0:128]), inc="dve")
                        bank_done("pw%d" % pidx, ("dve", dt[1]))
                    tile_idx = st * 4 + hf * 2 + tt2
                    od = dma("sync", "dout", out[tile_idx * 128:(tile_idx + 1) * 128, :], outtm[:, :],
                             waits=[("dve", dt[1])])
                    outtm_free[0] = ("dout", od[1])
            ox_free = [("dve", em.n("dve"))]


def _pcw(a, p=128):
    cp, w = a.shape
    c = cp // p
    return np.ascontiguousarray(a.reshape(c, p, w).transpose(1, 0, 2).reshape(p, c * w))


def prep_inputs(inputs):
    x = np.asarray(inputs["L0_attn_x"], np.float32).reshape(T * NCORES, D)
    shards = [np.ascontiguousarray(
        x[c * T:(c + 1) * T].reshape(NST, 4, 128, D).transpose(0, 2, 1, 3).reshape(NST, 128, 4 * D))
        for c in range(NCORES)]

    lnp = np.stack([
        np.asarray(inputs["L0_attn_ln1_w"], np.float32),
        np.asarray(inputs["L0_attn_ln1_b"], np.float32),
        np.asarray(inputs["L0_attn_ln2_w"], np.float32),
        np.asarray(inputs["L0_attn_ln2_b"], np.float32),
        np.asarray(inputs["L0_attn_bo"], np.float32),
    ], axis=1)
    lncp = np.ascontiguousarray(
        lnp.reshape(DEPTH, 5, DC, 128).transpose(0, 3, 2, 1).reshape(DEPTH, 128, 40))

    sp = np.zeros((5, FF), np.float32)
    sp[0] = np.asarray(inputs["L0_ffn_w1_s"], np.float32)
    sp[1] = np.asarray(inputs["L0_ffn_w2_s"], np.float32)
    sp[2] = np.asarray(inputs["L0_ffn_h1_s"], np.float32)
    sp[3] = np.asarray(inputs["L0_ffn_h2_s"], np.float32)
    sp[4, 0:D] = np.asarray(inputs["L0_ffn_w3_s"], np.float32)
    scp = np.ascontiguousarray(sp.reshape(5, 32, 128).transpose(2, 1, 0).reshape(128, 160))

    def wslices(wT, nslice, width):
        I, O = wT.shape
        c = I // 128
        a = wT.reshape(c, 128, nslice, width).transpose(2, 1, 0, 3)
        return np.ascontiguousarray(a.reshape(nslice, 128, c * width))

    w1f = np.asarray(inputs["L0_attn_w1"], np.float32)
    w2f = np.asarray(inputs["L0_attn_w2"], np.float32)
    w1h = np.stack([np.stack([_pcw(w1f[li][:, h * 2048:(h + 1) * 2048]) for h in range(2)])
                    for li in range(DEPTH)])
    w2h = np.stack([np.stack([_pcw(w2f[li][h * 2048:(h + 1) * 2048, :]) for h in range(2)])
                    for li in range(DEPTH)])
    common = {
        "wq": np.stack([_pcw(np.asarray(inputs["L0_attn_wq"], np.float32)[li]) for li in range(DEPTH)]),
        "wk": np.stack([_pcw(np.asarray(inputs["L0_attn_wk"], np.float32)[li]) for li in range(DEPTH)]),
        "wv": np.stack([_pcw(np.asarray(inputs["L0_attn_wv"], np.float32)[li]) for li in range(DEPTH)]),
        "wo": np.stack([_pcw(np.asarray(inputs["L0_attn_wo"], np.float32)[li]) for li in range(DEPTH)]),
        "w1": w1h,
        "w2": w2h,
        "lncp": lncp,
        "w1t": wslices(np.asarray(inputs["L0_ffn_w1_w"], np.float32).T, 32, 128),
        "w2t": wslices(np.asarray(inputs["L0_ffn_w2_w"], np.float32).T, 32, 128),
        "h1t": wslices(np.asarray(inputs["L0_ffn_h1_w"], np.float32).T, 32, 128),
        "h2t": wslices(np.asarray(inputs["L0_ffn_h2_w"], np.float32).T, 32, 128),
        "w3t": wslices(np.asarray(inputs["L0_ffn_w3_w"], np.float32).T, 8, 128),
        "scp": scp,
        "idf": np.eye(128, dtype=np.float32),
        "cbf": np.ascontiguousarray(np.stack(
            [np.full(128, 1.0 / D), np.ones(128)], axis=1).astype(ml_dtypes.bfloat16)),
        "orf": np.ones((1, 128), np.float32),
    }
    return shards, common


def kernel(**inputs):
    shards, common = prep_inputs(inputs)
    nc = build_nc()
    in_maps = [dict(common, x0=shards[c]) for c in range(NCORES)]
    res = run_bass_kernel_spmd(nc, in_maps, core_ids=list(range(NCORES)))
    outp = np.concatenate([np.asarray(res.results[c]["out"]) for c in range(NCORES)], axis=0)
    return outp.reshape(2, 16384, D)


# version 04
# speedup vs baseline: 1.2049x; 1.0978x over previous
import numpy as np
import ml_dtypes
from contextlib import ExitStack
import concourse.bass as bass
import concourse.mybir as mybir
from concourse.bass_utils import run_bass_kernel_spmd

F32 = mybir.dt.float32
BF16 = mybir.dt.bfloat16
AF = mybir.ActivationFunctionType
ALU = mybir.AluOpType

NCORES = 8
T = 4096
NST = 8
STW = 512
D = 1024
DC = 8
H = 8
FF = 4096
DEPTH = 6
EPS = 1e-5

ENGS = ("sync", "tensor", "vector", "scalar", "gpsimd")


def ctx_pos(h):
    if h < 6:
        b, j = h // 3, h % 3
        return b, b * 387 + j * 128, b * 387 + 384 + j
    b, j = 2, h - 6
    return 2, 774 + j * 128, 774 + 256 + j


class Emitter:
    def __init__(self):
        self.q = {e: [] for e in ENGS}
        self.cnt = {}
        self.lastwait = {}

    def n(self, sem):
        return self.cnt.get(sem, 0)

    def wait(self, eng, sem, val):
        if val is None or val <= 0:
            return
        key = (eng, sem)
        if self.lastwait.get(key, -1) >= val:
            return
        self.lastwait[key] = val
        self.q[eng].append(("w", sem, val))

    def wait_tok(self, eng, tok):
        if tok is None:
            return
        if isinstance(tok, list):
            for t in tok:
                self.wait_tok(eng, t)
        else:
            self.wait(eng, tok[0], tok[1])

    def op(self, eng, fn, inc=None, amt=1):
        if inc is None:
            self.q[eng].append(("i", fn, None, 0))
            return None
        self.cnt[inc] = self.cnt.get(inc, 0) + amt
        tot = self.cnt[inc]
        self.q[eng].append(("i", fn, inc, amt))
        return (inc, tot)

    def flush(self, block, sems):
        q = {e: list(self.q[e]) for e in ENGS}
        for e in ENGS:
            self.q[e] = []

        def replay(engine, items):
            for item in items:
                if item[0] == "w":
                    engine.wait_ge(sems[item[1]], item[2])
                else:
                    _, fn, inc, amt = item
                    ins = fn(engine)
                    if inc is not None:
                        ins.then_inc(sems[inc], amt)

        if q["sync"]:
            @block.sync
            def _(e, it=q["sync"]):
                replay(e, it)
        if q["tensor"]:
            @block.tensor
            def _(e, it=q["tensor"]):
                replay(e, it)
        if q["vector"]:
            @block.vector
            def _(e, it=q["vector"]):
                replay(e, it)
        if q["scalar"]:
            @block.scalar
            def _(e, it=q["scalar"]):
                replay(e, it)
        if q["gpsimd"]:
            @block.gpsimd
            def _(e, it=q["gpsimd"]):
                replay(e, it)


def build_nc(debug=None):
    nc = bass.Bass("TRN2", target_bir_lowering=False)

    x0 = nc.dram_tensor("x0", [NST, 128, 4 * D], F32, kind="ExternalInput").ap()
    wq_d = nc.dram_tensor("wq", [DEPTH, 128, DC * D], F32, kind="ExternalInput").ap()
    wk_d = nc.dram_tensor("wk", [DEPTH, 128, DC * D], F32, kind="ExternalInput").ap()
    wv_d = nc.dram_tensor("wv", [DEPTH, 128, DC * D], F32, kind="ExternalInput").ap()
    wo_d = nc.dram_tensor("wo", [DEPTH, 128, DC * D], F32, kind="ExternalInput").ap()
    w1_d = nc.dram_tensor("w1", [DEPTH, 2, 128, 2 * DC * D], F32, kind="ExternalInput").ap()
    w2_d = nc.dram_tensor("w2", [DEPTH, 2, 128, 2 * DC * D], F32, kind="ExternalInput").ap()
    lncp = nc.dram_tensor("lncp", [DEPTH, 128, 40], F32, kind="ExternalInput").ap()
    w1t = nc.dram_tensor("w1t", [32, 128, 1024], F32, kind="ExternalInput").ap()
    w2t = nc.dram_tensor("w2t", [32, 128, 1024], F32, kind="ExternalInput").ap()
    h1t = nc.dram_tensor("h1t", [32, 128, 4096], F32, kind="ExternalInput").ap()
    h2t = nc.dram_tensor("h2t", [32, 128, 4096], F32, kind="ExternalInput").ap()
    w3t = nc.dram_tensor("w3t", [8, 128, 4096], F32, kind="ExternalInput").ap()
    scp = nc.dram_tensor("scp", [128, 160], F32, kind="ExternalInput").ap()
    idf = nc.dram_tensor("idf", [128, 128], F32, kind="ExternalInput").ap()
    cbf = nc.dram_tensor("cbf", [128, 2], BF16, kind="ExternalInput").ap()
    orf = nc.dram_tensor("orf", [1, 128], F32, kind="ExternalInput").ap()

    out = nc.dram_tensor("out", [T, D], F32, kind="ExternalOutput").ap()

    xT = nc.dram_tensor("xT", [NST, 128, DC * STW], F32).ap()
    x0T = nc.dram_tensor("x0T", [NST, 128, DC * STW], F32).ap()
    qT = nc.dram_tensor("qT", [NST, 128, DC * STW], BF16).ap()
    xnT = nc.dram_tensor("xnT", [NST, 128, DC * STW], BF16).ap()
    ccin = nc.dram_tensor("ccin", [128, 1032], F32)
    ccout = nc.dram_tensor("ccout", [128, 1032], F32)

    dbg = {}
    if debug is not None:
        dbg["x"] = nc.dram_tensor("dbg_x", [NST, 128, DC * STW], F32, kind="ExternalOutput").ap()
        dbg["q"] = nc.dram_tensor("dbg_q", [NST, 128, DC * STW], BF16, kind="ExternalOutput").ap()
        dbg["ctx"] = nc.dram_tensor("dbg_ctx", [128, 1032], F32, kind="ExternalOutput").ap()

    em = Emitter()

    with ExitStack() as es:
        E = es.enter_context
        semnames = ["dx", "dxo", "dq", "dw", "dcc", "cc", "pe", "act", "dve", "dout", "dmisc"]
        sems = {s: E(nc.semaphore(s)) for s in semnames}

        mreps = E(nc.sbuf_tensor("mreps", [128, 1040], F32))
        tmpA = E(nc.sbuf_tensor("tmpA", [128, 512], F32))
        rows = E(nc.sbuf_tensor("rows", [1, 1024], F32))
        lncols = [E(nc.sbuf_tensor("lncols0", [128, 40], F32)),
                  E(nc.sbuf_tensor("lncols1", [128, 40], F32))]
        id_f = E(nc.sbuf_tensor("id_f", [128, 128], F32))
        ccol = E(nc.sbuf_tensor("ccol", [128, 2], BF16))
        orow = E(nc.sbuf_tensor("orow", [1, 128], F32))
        epsr = E(nc.sbuf_tensor("epsr", [1, 1], F32))

        pc = [E(nc.psum_tensor("pc%d" % i, [128, 512], F32)) for i in range(3)]
        pw = [E(nc.psum_tensor("pw%d" % i, [128, 512], F32)) for i in range(5)]

        block = E(nc.Block())

        inner_es = ExitStack()
        IE = inner_es.enter_context
        xin = [IE(nc.sbuf_tensor("xin0", [128, DC * STW], F32)),
               IE(nc.sbuf_tensor("xin1", [128, DC * STW], F32))]
        qbuf = [IE(nc.sbuf_tensor("qbuf0", [128, DC * STW], BF16)),
                IE(nc.sbuf_tensor("qbuf1", [128, DC * STW], BF16))]
        w12h = IE(nc.sbuf_tensor("w12h", [128, 4 * DC * D], BF16))
        wqkv = IE(nc.sbuf_tensor("wqkv", [128, 3 * DC * D], BF16))
        wosb = IE(nc.sbuf_tensor("wosb", [128, DC * D], BF16))
        xn1T = IE(nc.sbuf_tensor("xn1T", [128, DC * STW], BF16))
        hq = IE(nc.sbuf_tensor("hq", [128, DC * STW], BF16))
        ekb = IE(nc.sbuf_tensor("ekb", [128, D], BF16))
        vbuf = IE(nc.sbuf_tensor("vbuf", [128, D], BF16))
        aoT = hq
        eqs = hq[:, 0:512]
        ctxbf = ekb
        ctxCC = mreps[:, 0:1032]

        def dma(eng, sem_, outap, inap, waits=()):
            for wtok in waits:
                em.wait_tok(eng, wtok)
            return em.op(eng, lambda e, o=outap, i=inap: e.dma_start(out=o, in_=i), inc=sem_, amt=16)

        bank_free = {}

        def pe_use(bank):
            em.wait_tok("tensor", bank_free.get(bank))

        def bank_done(bank, tok):
            bank_free[bank] = tok

        def mm(outap, lhs, rhs, start, stop, inc=None):
            return em.op("tensor",
                         lambda e, o=outap, l=lhs, r=rhs, st=start, sp=stop:
                         e.matmul(o, l, r, start=st, stop=sp, skip_group_check=True),
                         inc=inc)

        def lncol(bi, c, vec):
            return lncols[bi][:, c * 5 + vec: c * 5 + vec + 1]

        xslot = [None] * NST
        qslot = [None] * NST
        xnslot = [None] * NST
        hq_free = [None]
        xin_free = [None, None]
        qbuf_free = [None, None]

        def ln_stats(xsb, castb, sqb, W=STW):
            ct = em.op("vector", lambda e: e.tensor_copy(castb[:, 0:DC * W], xsb[:, 0:DC * W]), inc="dve")
            em.wait("scalar", "dve", ct[1])
            sq = em.op("scalar", lambda e: e.activation(sqb[:, 0:DC * W], castb[:, 0:DC * W], AF.Square), inc="act")
            em.wait("tensor", "dve", ct[1])
            pe_use("pw0")
            for c in range(DC):
                mm(pw[0][0:1, 0:W], ccol[:, 0:1], castb[:, c * W:(c + 1) * W],
                   start=(c == 0), stop=(c == DC - 1))
            em.wait("tensor", "act", sq[1])
            pe_use("pw1")
            mt = None
            for c in range(DC):
                mt = mm(pw[1][0:1, 0:W], ccol[:, 0:1], sqb[:, c * W:(c + 1) * W],
                        start=(c == 0), stop=(c == DC - 1), inc="pe" if c == DC - 1 else None)
            r_m = pw[0][0:1, 0:W]
            r_s = pw[1][0:1, 0:W]
            em.wait("scalar", "pe", mt[1])
            sqm = em.op("scalar", lambda e: e.activation(rows[0:1, 0:W], r_m, AF.Square), inc="act")
            em.wait("vector", "act", sqm[1])
            vt = em.op("vector", lambda e: e.tensor_tensor(rows[0:1, 512:512 + W], r_s, rows[0:1, 0:W], ALU.subtract), inc="dve")
            em.wait("scalar", "dve", vt[1])
            em.op("scalar", lambda e: e.activation(rows[0:1, 0:W], rows[0:1, 512:512 + W], AF.Ln, bias=epsr[0:1, 0:1], scale=1.0))
            rt = em.op("scalar", lambda e: e.activation(rows[0:1, 512:512 + W], rows[0:1, 0:W], AF.Exp, bias=0.0, scale=-0.5), inc="act")
            em.wait("vector", "act", rt[1])
            mr = em.op("vector", lambda e: e.tensor_tensor(rows[0:1, 0:W], r_m, rows[0:1, 512:512 + W], ALU.mult), inc="dve")
            em.wait("tensor", "dve", mr[1])
            pe_use("pw0")
            pe_use("pw1")
            mm(pw[0][:, 0:W], orow[:, :], rows[0:1, 512:512 + W], start=True, stop=True)
            t2 = mm(pw[1][:, 0:W], orow[:, :], rows[0:1, 0:W], start=True, stop=True, inc="pe")
            em.wait("vector", "pe", t2[1])
            em.op("vector", lambda e: e.tensor_copy(mreps[:, 0:W], pw[0][:, 0:W]))
            c2 = em.op("vector", lambda e: e.tensor_copy(mreps[:, 512:512 + W], pw[1][:, 0:W]), inc="dve")
            bank_done("pw0", ("dve", c2[1]))
            bank_done("pw1", ("dve", c2[1]))
            return c2

        def ln_apply(xsb, c, sink, wcol=None, bcol=None, W=STW):
            xc = xsb[:, c * W:(c + 1) * W]
            em.op("vector", lambda e: e.tensor_tensor(tmpA[:, 0:W], xc, mreps[:, 0:W], ALU.mult))
            em.op("vector", lambda e: e.tensor_tensor(tmpA[:, 0:W], tmpA[:, 0:W], mreps[:, 512:512 + W], ALU.subtract))
            if wcol is not None:
                return em.op("vector", lambda e, w=wcol, b=bcol: e.tensor_scalar(
                    sink, tmpA[:, 0:W], w, b, ALU.mult, ALU.add), inc="dve")
            if callable(sink):
                return sink(tmpA[:, 0:W])
            return em.op("vector", lambda e: e.tensor_copy(sink, tmpA[:, 0:W]), inc="dve")

        def load_wqkv(li):
            gw = ("pe", em.n("pe"))
            dma("gpsimd", "dw", wqkv[:, 0:DC * D], wq_d[li], waits=[gw])
            dma("gpsimd", "dw", wqkv[:, DC * D:2 * DC * D], wk_d[li])
            t = dma("gpsimd", "dw", wqkv[:, 2 * DC * D:3 * DC * D], wv_d[li])
            return ("dw", t[1])

        def load_wo(li):
            t = dma("gpsimd", "dw", wosb[:, :], wo_d[li],
                    waits=[("pe", em.n("pe"))])
            return ("dw", t[1])

        def load_w12h(li, half):
            gw = ("pe", em.n("pe"))
            dma("gpsimd", "dw", w12h[:, 0:2 * DC * D], w1_d[li, half], waits=[gw])
            t = dma("gpsimd", "dw", w12h[:, 2 * DC * D:4 * DC * D], w2_d[li, half])
            return ("dw", t[1])

        def load_lncols(li, bi):
            t = dma("sync", "dmisc", lncols[bi][:, :], lncp[li],
                    waits=[("dve", em.n("dve"))])
            return ("dmisc", t[1])

        dma("sync", "dmisc", id_f[:, :], idf)
        dma("sync", "dmisc", ccol[:, :], cbf)
        ct0 = dma("sync", "dmisc", orow[:, :], orf)
        em.op("vector", lambda e: e.memset(epsr[:, :], EPS), inc="dve")
        for eng in ("tensor", "vector", "scalar"):
            em.wait(eng, "dmisc", ct0[1])

        wqkv_rdy = {0: load_wqkv(0)}
        wo_rdy = {0: load_wo(0)}
        w12_rdy = {(0, 0): load_w12h(0, 0)}
        ln_rdy = {0: load_lncols(0, 0)}

        for st in range(NST):
            lt = dma("sync", "dx", xin[0][:, :], x0[st],
                     waits=[xin_free[0], xin_free[1]])
            em.wait("tensor", "dx", lt[1])
            ctok = None
            for t4 in range(4):
                for c in range(DC):
                    pidx = 2 + ((t4 * DC + c) % 2)
                    pe_use("pw%d" % pidx)
                    tt = em.op("tensor", lambda e, t4=t4, c=c, p=pidx: e.transpose(
                        pw[p][:, 0:128], xin[0][:, t4 * D + c * 128:t4 * D + c * 128 + 128],
                        id_f[:, :]), inc="pe")
                    em.wait("vector", "pe", tt[1])
                    ctok = em.op("vector", lambda e, t4=t4, c=c, p=pidx: e.tensor_copy(
                        xin[1][:, c * 512 + t4 * 128: c * 512 + t4 * 128 + 128], pw[p][:, 0:128]),
                        inc="dve")
                    bank_done("pw%d" % pidx, ("dve", ctok[1]))
            s0 = dma("sync", "dxo", x0T[st], xin[1][:, :],
                     waits=[("dve", ctok[1])])
            em.wait_tok("scalar", hq_free[0])
            ln_stats(xin[1], xn1T, hq)
            atok = None
            for c in range(DC):
                atok = ln_apply(xin[1], c, xin[0][:, c * 512:(c + 1) * 512])
            s1 = dma("sync", "dxo", xT[st], xin[0][:, :],
                     waits=[("dve", atok[1])])
            xslot[st] = ("dxo", s1[1])
            em.wait_tok("vector", ln_rdy[0])
            ln_stats(xin[0], xn1T, hq)
            at2 = None
            for c in range(DC):
                at2 = ln_apply(xin[0], c, xn1T[:, c * 512:(c + 1) * 512],
                               wcol=lncol(0, c, 0), bcol=lncol(0, c, 1))
            s2 = dma("sync", "dq", xnT[st], xn1T[:, :], waits=[("dve", at2[1])])
            xnslot[st] = ("dq", s2[1])
            hq_free[0] = ("dq", s2[1])
            xin_free[0] = [("dxo", s1[1]), ("dve", at2[1])]
            xin_free[1] = [("dxo", s0[1]), ("dve", atok[1])]

        def sweep_A(li):
            em.wait_tok("tensor", wqkv_rdy[li])
            em.wait_tok("scalar", hq_free[0])
            wq_off, wk_off, wv_off = 0, DC * D, 2 * DC * D
            ntqs = {}
            alts = {}

            def a_load(st):
                xb = st % 2
                xna_s = xin[xb][:, :].bitcast(BF16)[:, 0:DC * STW]
                alts[st] = dma("sync", "dx", xna_s, xnT[st],
                               waits=[xin_free[xb], xnslot[st]])

            a_load(0)
            for st in range(NST):
                xb = st % 2
                qb = st % 2
                xna = xin[xb][:, :].bitcast(BF16)[:, 0:DC * STW]
                if st + 1 < NST:
                    a_load(st + 1)
                em.wait("tensor", "dx", alts[st][1])
                etoks = {}
                rts = {}

                def emit_qmm(oc, chalf=None):
                    pidx = 2 + (oc % 2)
                    crange = range(DC) if chalf is None else (
                        range(0, DC // 2) if chalf == 0 else range(DC // 2, DC))
                    if chalf in (None, 0):
                        pe_use("pw%d" % pidx)
                    qmt = None
                    for c in crange:
                        qmt = mm(pw[pidx][:, :],
                                 wqkv[:, wq_off + c * D + oc * 128: wq_off + c * D + oc * 128 + 128],
                                 xna[:, c * 512:(c + 1) * 512],
                                 start=(c == 0), stop=(c == DC - 1),
                                 inc="pe" if c == DC - 1 else None)
                    if chalf in (None, 1):
                        em.wait("scalar", "pe", qmt[1])
                        if (st * H + oc) >= 2:
                            prev = ntqs.get((st * H + oc) - 2)
                            if prev is not None:
                                em.wait("scalar", "dve", prev[1])
                        et = em.op("scalar", lambda e, p=pidx, oc=oc: e.activation(
                            hq[:, (oc % 2) * 512:(oc % 2) * 512 + 512], pw[p][:, :], AF.Exp),
                            inc="act")
                        bank_done("pw%d" % pidx, ("act", et[1]))
                        etoks[oc] = et

                def emit_zq(oc):
                    em.wait("tensor", "act", etoks[oc][1])
                    pe_use("pw4")
                    zt = mm(pw[4][0:1, 0:512], ccol[:, 1:2],
                            hq[:, (oc % 2) * 512:(oc % 2) * 512 + 512],
                            start=True, stop=True, inc="pe")
                    em.wait("vector", "pe", zt[1])
                    rt = em.op("vector", lambda e, oc=oc: e.reciprocal(
                        rows[0:1, (oc % 2) * 512:(oc % 2) * 512 + 512], pw[4][0:1, 0:512]),
                        inc="dve")
                    bank_done("pw4", ("dve", rt[1]))
                    rts[oc] = rt

                def emit_zrep_ntq(oc):
                    em.wait("tensor", "dve", rts[oc][1])
                    pe_use("pw4")
                    zr = mm(pw[4][:, :], orow[:, :],
                            rows[0:1, (oc % 2) * 512:(oc % 2) * 512 + 512],
                            start=True, stop=True, inc="pe")
                    em.wait("vector", "pe", zr[1])
                    if oc == 0:
                        em.wait_tok("vector", qbuf_free[qb])
                    ntq = em.op("vector", lambda e, oc=oc, qb=qb: e.tensor_tensor(
                        qbuf[qb][:, oc * 512:(oc + 1) * 512],
                        hq[:, (oc % 2) * 512:(oc % 2) * 512 + 512],
                        pw[4][:, :], ALU.mult), inc="dve")
                    bank_done("pw4", ("dve", ntq[1]))
                    ntqs[st * H + oc] = ntq
                    return ntq

                emit_qmm(0)
                emit_qmm(1)
                ntq = None
                for oc in range(H):
                    emit_zq(oc)
                    if oc + 2 < H:
                        emit_qmm(oc + 2, chalf=0)
                    ntq = emit_zrep_ntq(oc)
                    if oc + 2 < H:
                        emit_qmm(oc + 2, chalf=1)
                qs = dma("sync", "dq", qT[st], qbuf[qb][:, :],
                         waits=[("dve", ntq[1]), qslot[st]])
                qslot[st] = ("dq", qs[1])
                qbuf_free[qb] = [("dq", qs[1])]
                for t4 in range(4):
                    first = (st == 0 and t4 == 0)
                    last = (st == NST - 1 and t4 == 3)
                    ktoks = []
                    for f in range(2):
                        pe_use("pw%d" % (2 + f))
                        kt = None
                        for c in range(DC):
                            kt = mm(pw[2 + f][:, :],
                                    xna[:, c * 512 + t4 * 128: c * 512 + t4 * 128 + 128],
                                    wqkv[:, wk_off + c * D + f * 512: wk_off + c * D + (f + 1) * 512],
                                    start=(c == 0), stop=(c == DC - 1),
                                    inc="pe" if c == DC - 1 else None)
                        ktoks.append(kt)
                        em.wait("scalar", "pe", kt[1])
                        et = em.op("scalar", lambda e, f=f: e.activation(
                            ekb[:, f * 512:(f + 1) * 512], pw[2 + f][:, :], AF.Exp), inc="act")
                        bank_done("pw%d" % (2 + f), ("act", et[1]))
                        ktoks[f] = ("act", et[1])
                    vcts = []
                    for f in range(2):
                        pe_use("pw%d" % f)
                        vt = None
                        for c in range(DC):
                            vt = mm(pw[f][:, :],
                                    xna[:, c * 512 + t4 * 128: c * 512 + t4 * 128 + 128],
                                    wqkv[:, wv_off + c * D + f * 512: wv_off + c * D + (f + 1) * 512],
                                    start=(c == 0), stop=(c == DC - 1),
                                    inc="pe" if c == DC - 1 else None)
                        em.wait("vector", "pe", vt[1])
                        vct = em.op("vector", lambda e, f=f: e.tensor_copy(
                            vbuf[:, f * 512:(f + 1) * 512], pw[f][:, :]), inc="dve")
                        bank_done("pw%d" % f, ("dve", vct[1]))
                        vcts.append(vct)
                    for f in range(2):
                        em.wait("tensor", "act", ktoks[f][1])
                        em.wait("tensor", "dve", vcts[f][1])
                        for h in range(4 * f, 4 * f + 4):
                            b, coff, zoff = ctx_pos(h)
                            mm(pc[b][:, coff - b * 387:coff - b * 387 + 128],
                               ekb[:, h * 128:(h + 1) * 128],
                               vbuf[:, h * 128:(h + 1) * 128],
                               start=(first and h in (0, 3, 6)), stop=last)
                            mm(pc[b][:, zoff - b * 387: zoff - b * 387 + 1],
                               ekb[:, h * 128:(h + 1) * 128],
                               ccol[:, 1:2],
                               start=False, stop=last,
                               inc="pe" if (last and h == H - 1) else None)
                xin_free[xb] = [("pe", em.n("pe"))]
            em.wait("vector", "pe", em.n("pe"))
            em.op("vector", lambda e: e.tensor_copy(ctxCC[:, 0:387], pc[0][:, 0:387]))
            em.op("vector", lambda e: e.tensor_copy(ctxCC[:, 387:774], pc[1][:, 0:387]))
            ev = em.op("vector", lambda e: e.tensor_copy(ctxCC[:, 774:1032], pc[2][:, 0:258]), inc="dve")
            for b in range(3):
                bank_done("pc%d" % b, ("dve", ev[1]))
            ci = dma("gpsimd", "dcc", ccin[:, :], ctxCC, waits=[("dve", ev[1])])
            em.wait("gpsimd", "dcc", ci[1])
            cct = em.op("gpsimd", lambda e: e.collective_compute(
                "AllReduce", ALU.add,
                replica_groups=[[0, 1, 2, 3], [4, 5, 6, 7]],
                ins=[ccin.ap().opt()], outs=[ccout.ap().opt()]), inc="cc")
            em.wait("gpsimd", "cc", cct[1])
            co = dma("gpsimd", "dcc", ctxCC, ccout[:, :])
            return co

        def sweep_B(li, co):
            lnb = li % 2
            em.wait_tok("tensor", wo_rdy[li])
            em.wait("vector", "dcc", co[1])
            rc = None
            for h in range(H):
                b, coff, zoff = ctx_pos(h)
                rc = em.op("vector", lambda e, z=zoff, h=h: e.reciprocal(
                    tmpA[:, h:h + 1], ctxCC[:, z:z + 1]),
                    inc="dve" if h == H - 1 else None)
            em.wait("vector", "dve", rc[1])
            nt = None
            for h in range(H):
                b, coff, zoff = ctx_pos(h)
                nt = em.op("vector", lambda e, h=h, c0=coff: e.tensor_scalar(
                    ctxbf[:, h * 128:(h + 1) * 128], ctxCC[:, c0:c0 + 128],
                    tmpA[:, h:h + 1], None, ALU.mult), inc="dve")
            em.wait("tensor", "dve", nt[1])
            blts = {}
            bqlts = {}

            def b_load(st):
                xb = st % 2
                qb = st % 2
                blts[st] = dma("sync", "dx", xin[xb][:, :], xT[st],
                               waits=[xin_free[xb], xslot[st]])
                bqlts[st] = dma("sync", "dq", qbuf[qb][:, :], qT[st],
                                waits=[qbuf_free[qb], qslot[st]])

            b_load(0)
            for st in range(NST):
                xb = st % 2
                qb = st % 2
                if st + 1 < NST:
                    b_load(st + 1)
                xsb = xin[xb]
                em.wait("tensor", "dq", bqlts[st][1])
                if li == 0 and st == 0:
                    pass
                aot = None
                ao_banks = [("pw4", pw[4]), ("pc0", pc[0]), ("pc1", pc[1]), ("pc2", pc[2])]
                for h in range(H):
                    bank, pt = ao_banks[h % 4]
                    pe_use(bank)
                    t = mm(pt[:, :], ctxbf[:, h * 128:(h + 1) * 128],
                           qbuf[qb][:, h * 512:(h + 1) * 512], start=True, stop=True, inc="pe")
                    em.wait("scalar", "pe", t[1])
                    aot = em.op("scalar", lambda e, h=h, pt=pt: e.activation(
                        aoT[:, h * 512:(h + 1) * 512], pt[:, :], AF.Copy), inc="act")
                    bank_done(bank, ("act", aot[1]))
                ao_done = ("pe", em.n("pe"))
                em.wait("tensor", "act", aot[1])
                em.wait("vector", "dx", blts[st][1])
                for half in range(2):
                    wtoks = []
                    for ds in range(4):
                        d_idx = half * 4 + ds
                        pe_use("pw%d" % ds)
                        t = None
                        for ec in range(DC):
                            t = mm(pw[ds][:, :],
                                   wosb[:, ec * D + d_idx * 128: ec * D + d_idx * 128 + 128],
                                   aoT[:, ec * 512:(ec + 1) * 512],
                                   start=(ec == 0), stop=(ec == DC - 1),
                                   inc="pe" if ec == DC - 1 else None)
                        wtoks.append(t)
                    for ds in range(4):
                        d_idx = half * 4 + ds
                        em.wait("vector", "pe", wtoks[ds][1])
                        em.op("vector", lambda e, ds=ds, d=d_idx, b=lnb: e.tensor_scalar(
                            tmpA[:, :], pw[ds][:, :], lncol(b, d, 4), None, ALU.add))
                        rt = em.op("vector", lambda e, d=d_idx, xsb=xsb: e.tensor_tensor(
                            xsb[:, d * 512:(d + 1) * 512], xsb[:, d * 512:(d + 1) * 512],
                            tmpA[:, :], ALU.add), inc="dve")
                        bank_done("pw%d" % ds, ("dve", rt[1]))
                ln_stats(xsb, xn1T, hq)
                em.wait_tok("vector", [ao_done])
                at = None
                for c in range(DC):
                    at = ln_apply(xsb, c, qbuf[qb][:, c * 512:(c + 1) * 512],
                                  wcol=lncol(lnb, c, 2), bcol=lncol(lnb, c, 3))
                qs = dma("sync", "dq", qT[st], qbuf[qb][:, :], waits=[("dve", at[1])])
                qslot[st] = ("dq", qs[1])
                qbuf_free[qb] = [("dq", qs[1])]
                so = dma("sync", "dxo", xT[st], xsb[:, :],
                         waits=[("dve", at[1])])
                xslot[st] = ("dxo", so[1])
                xin_free[xb] = [("dxo", so[1])]

        def sweep_F(li, half, last=False, stash=False):
            em.wait_tok("tensor", w12_rdy[(li, half)])
            if stash:
                em.wait_tok("vector", ln_rdy[li + 1])
            flts = {}
            fqlts = {}

            def f_load(st):
                xb = st % 2
                qb = st % 2
                flts[st] = dma("sync", "dx", xin[xb][:, :], xT[st],
                               waits=[xin_free[xb], xslot[st]])
                fqlts[st] = dma("sync", "dq", qbuf[qb][:, :], qT[st],
                                waits=[qbuf_free[qb], qslot[st]])

            f_load(0)
            for st in range(NST):
                xb = st % 2
                qb = st % 2
                if last:
                    if st > 0:
                        f_load(st)
                elif st + 1 < NST:
                    f_load(st + 1)
                xsb = xin[xb]
                em.wait("tensor", "dq", fqlts[st][1])
                em.wait("vector", "dx", flts[st][1])
                yt = None
                for Q in range(2):
                    hts = []
                    for oc8 in range(DC):
                        bank = "pw4" if oc8 % 2 == 0 else "pc0"
                        pt = pw[4] if oc8 % 2 == 0 else pc[0]
                        pe_use(bank)
                        t = None
                        for c in range(DC):
                            t = mm(pt[:, :],
                                   w12h[:, c * 2048 + Q * 1024 + oc8 * 128: c * 2048 + Q * 1024 + oc8 * 128 + 128],
                                   qbuf[qb][:, c * 512:(c + 1) * 512],
                                   start=(c == 0), stop=(c == DC - 1),
                                   inc="pe" if c == DC - 1 else None)
                        em.wait("scalar", "pe", t[1])
                        em.wait_tok("scalar", hq_free[0])
                        ht = em.op("scalar", lambda e, oc8=oc8, pt=pt: e.activation(
                            hq[:, oc8 * 512:(oc8 + 1) * 512], pt[:, :], AF.Lrelu), inc="act")
                        bank_done(bank, ("act", ht[1]))
                        hts.append(ht)
                    em.wait("tensor", "act", hts[-1][1])
                    for dh in range(2):
                        ytoks = []
                        for ds in range(4):
                            d_idx = dh * 4 + ds
                            pe_use("pw%d" % ds)
                            t = None
                            for oc8 in range(DC):
                                woff = 2 * DC * D + (Q * DC + oc8) * D + d_idx * 128
                                t = mm(pw[ds][:, :],
                                       w12h[:, woff: woff + 128],
                                       hq[:, oc8 * 512:(oc8 + 1) * 512],
                                       start=(oc8 == 0), stop=(oc8 == DC - 1),
                                       inc="pe" if oc8 == DC - 1 else None)
                            ytoks.append(t)
                        for ds in range(4):
                            d_idx = dh * 4 + ds
                            em.wait("vector", "pe", ytoks[ds][1])
                            yt = em.op("vector", lambda e, ds=ds, d=d_idx, xsb=xsb: e.tensor_tensor(
                                xsb[:, d * 512:(d + 1) * 512], xsb[:, d * 512:(d + 1) * 512],
                                pw[ds][:, :], ALU.add), inc="dve")
                            bank_done("pw%d" % ds, ("dve", yt[1]))
                qbuf_free[qb] = [("pe", em.n("pe"))]
                if not last:
                    so = dma("sync", "dxo", xT[st], xsb[:, :],
                             waits=[("dve", yt[1])])
                    xslot[st] = ("dxo", so[1])
                    if stash:
                        em.wait("scalar", "pe", em.n("pe"))
                        ln_stats(xsb, xn1T, hq)
                        hq_free[0] = ("pe", em.n("pe"))
                        em.wait_tok("vector", qbuf_free[qb])
                        at2 = None
                        for c in range(DC):
                            at2 = ln_apply(xsb, c, qbuf[qb][:, c * 512:(c + 1) * 512],
                                           wcol=lncol((li + 1) % 2, c, 0),
                                           bcol=lncol((li + 1) % 2, c, 1))
                        s2 = dma("sync", "dq", xnT[st], qbuf[qb][:, :], waits=[("dve", at2[1])])
                        xnslot[st] = ("dq", s2[1])
                        qbuf_free[qb] = [("dq", s2[1])]
                        xin_free[xb] = [("dxo", so[1]), ("dve", at2[1])]
                    else:
                        xin_free[xb] = [("dxo", so[1])]
                else:
                    ln_stats(xsb, xn1T, hq)
                    ob = (st + 1) % 2
                    l0 = dma("sync", "dx", xin[ob][:, :], x0T[st],
                             waits=[xin_free[ob]])
                    em.wait("vector", "dx", l0[1])
                    at = None
                    for c in range(DC):
                        def mk(c=c, ob=ob):
                            def sink(tB):
                                return em.op("vector", lambda e: e.tensor_tensor(
                                    xin[ob][:, c * 512:(c + 1) * 512], tB,
                                    xin[ob][:, c * 512:(c + 1) * 512], ALU.add), inc="dve")
                            return sink
                        at = ln_apply(xsb, c, mk())
                    so = dma("sync", "dxo", xT[st], xin[ob][:, :],
                             waits=[("dve", at[1])])
                    xslot[st] = ("dxo", so[1])
                    xin_free[xb] = [("dve", at[1])]
                    xin_free[ob] = [("dxo", so[1])]

        if isinstance(debug, tuple):
            n_layers, stop_stage = debug[1] + 1, debug[0]
        elif debug == "entry":
            n_layers, stop_stage = 0, None
        else:
            n_layers, stop_stage = DEPTH, None

        for li in range(n_layers):
            co = sweep_A(li)
            if stop_stage == "A" and li == n_layers - 1:
                break
            sweep_B(li, co)
            if stop_stage == "B" and li == n_layers - 1:
                break
            if li + 1 < DEPTH:
                wqkv_rdy[li + 1] = load_wqkv(li + 1)
                ln_rdy[li + 1] = load_lncols(li + 1, (li + 1) % 2)
            sweep_F(li, 0)
            if li + 1 < DEPTH:
                wo_rdy[li + 1] = load_wo(li + 1)
            w12_rdy[(li, 1)] = load_w12h(li, 1)
            sweep_F(li, 1, last=(li == DEPTH - 1), stash=(li + 1 < DEPTH))
            if li + 1 < DEPTH:
                w12_rdy[(li + 1, 0)] = load_w12h(li + 1, 0)

        if debug is not None:
            em.wait("sync", "dxo", em.n("dxo"))
            em.wait("sync", "dq", em.n("dq"))
            dma("sync", "dout", dbg["x"], xT)
            dma("sync", "dout", dbg["q"], qT)
            em.wait("gpsimd", "dcc", em.n("dcc"))
            dma("gpsimd", "dout", dbg["ctx"], ccout[:, :])
            em.wait("sync", "dout", em.n("dout"))
            em.wait("gpsimd", "dout", em.n("dout"))
            em.flush(block, sems)
            inner_es.close()
        else:
            em.flush(block, sems)
            inner_es.close()
            emit_outer(nc, em, sems, dict(
                dma=dma, mm=mm, pe_use=pe_use, bank_done=bank_done,
                ln_stats=ln_stats, ln_apply=ln_apply,
                xT=xT, pw=pw, pc=pc, rows=rows, id_f=id_f, ccol=ccol,
                orow=orow, epsr=epsr, tmpA=tmpA, mreps=mreps, out=out,
                xslot=xslot, w1t=w1t, w2t=w2t, h1t=h1t, h2t=h2t,
                w3t=w3t, scp=scp), E, em)
            em.wait("sync", "dout", em.n("dout"))
            em.flush(block, sems)

    return nc


def emit_outer(nc, em, sems, L, E, _em):
    dma, mm, pe_use, bank_done = L["dma"], L["mm"], L["pe_use"], L["bank_done"]
    ln_stats, ln_apply = L["ln_stats"], L["ln_apply"]
    xT, pw, pc = L["xT"], L["pw"], L["pc"]
    id_f, ccol = L["id_f"], L["ccol"]
    out = L["out"]
    xslot = L["xslot"]
    w1t, w2t, h1t, h2t, w3t, scp = (L[k] for k in ("w1t", "w2t", "h1t", "h2t", "w3t", "scp"))

    ox = E(nc.sbuf_tensor("ox", [128, DC * STW], F32))
    hgrp = E(nc.sbuf_tensor("hgrp", [128, 32 * 1024], BF16))
    ggrp = E(nc.sbuf_tensor("ggrp", [128, 32 * 1024], BF16))
    wsl = [E(nc.sbuf_tensor("wsl0", [128, 4096], BF16)),
           E(nc.sbuf_tensor("wsl1", [128, 4096], BF16))]
    zT = [E(nc.sbuf_tensor("zT0", [128, 4096], BF16)),
          E(nc.sbuf_tensor("zT1", [128, 4096], BF16))]
    fsb = E(nc.sbuf_tensor("fsb", [128, 2048], F32))
    outtm = E(nc.sbuf_tensor("outtm", [128, 1024], F32))
    scols = E(nc.sbuf_tensor("scols", [128, 160], F32))
    stmp = E(nc.sbuf_tensor("stmp", [128, 1024], BF16))

    sct = dma("sync", "dmisc", scols[:, :], scp)
    for eng in ("tensor", "vector", "scalar"):
        em.wait(eng, "dmisc", sct[1])

    def scol(oc, vec):
        return scols[:, oc * 5 + vec: oc * 5 + vec + 1]

    wsl_free = [None, None]
    outtm_free = [None]

    def load_slice(src, oc, buf, width):
        return dma("gpsimd", "dw", wsl[buf][:, 0:width], src[oc],
                   waits=[wsl_free[buf]])

    ox_free = None
    for g in range(4):
        sts = [2 * g, 2 * g + 1]
        for k, st in enumerate(sts):
            lt = dma("sync", "dx", ox[:, :], xT[st],
                     waits=[ox_free, xslot[st]])
            em.wait("vector", "dx", lt[1])
            ln_stats(ox, hgrp[:, 0:4096], hgrp[:, 4096:8192])
            at = None
            for c in range(DC):
                at = ln_apply(ox, c, zT[k][:, c * 512:(c + 1) * 512])
            ox_free = [("dve", at[1])]
        em.wait("tensor", "dve", em.n("dve"))
        ht = None
        for oc in range(32):
            buf = oc % 2
            dma("gpsimd", "dw", wsl[buf][:, 0:1024], w1t[oc],
                waits=[wsl_free[buf]])
            l2 = dma("gpsimd", "dw", wsl[buf][:, 1024:2048], w2t[oc])
            em.wait("tensor", "dw", l2[1])
            for k in range(2):
                pe_use("pw2")
                for c in range(DC):
                    mm(pw[2][:, :], wsl[buf][:, c * 128:(c + 1) * 128],
                       zT[k][:, c * 512:(c + 1) * 512], start=(c == 0), stop=(c == DC - 1))
                pe_use("pw3")
                tb = None
                for c in range(DC):
                    tb = mm(pw[3][:, :], wsl[buf][:, 1024 + c * 128: 1024 + (c + 1) * 128],
                            zT[k][:, c * 512:(c + 1) * 512], start=(c == 0), stop=(c == DC - 1),
                            inc="pe" if c == DC - 1 else None)
                em.wait("scalar", "pe", tb[1])
                if ht is not None:
                    em.wait("scalar", "dve", ht[1])
                sa = em.op("scalar", lambda e, oc=oc: e.activation(
                    stmp[:, 0:512], pw[2][:, :], AF.Silu, bias=0.0, scale=scol(oc, 0)), inc="act")
                sb2 = em.op("scalar", lambda e, oc=oc: e.activation(
                    stmp[:, 512:1024], pw[3][:, :], AF.Copy, bias=0.0, scale=scol(oc, 1)), inc="act")
                bank_done("pw2", ("act", sa[1]))
                bank_done("pw3", ("act", sb2[1]))
                em.wait("vector", "act", sb2[1])
                ht = em.op("vector", lambda e, oc=oc, k=k: e.tensor_tensor(
                    hgrp[:, oc * 1024 + k * 512: oc * 1024 + (k + 1) * 512],
                    stmp[:, 0:512], stmp[:, 512:1024], ALU.mult), inc="dve")
            wsl_free[buf] = [("dve", ht[1])]

        def hstage(src_grp, dst_grp, wmat, svec_idx, src_sem):
            em.wait("tensor", src_sem, em.n(src_sem))
            for oc in range(32):
                buf = oc % 2
                ls = load_slice(wmat, oc, buf, 4096)
                em.wait("tensor", "dw", ls[1])
                last = None
                for k in range(2):
                    bank = "pw2" if k == 0 else "pc0"
                    pt = pw[2] if k == 0 else pc[0]
                    pe_use(bank)
                    t = None
                    for ic in range(32):
                        t = mm(pt[:, :], wsl[buf][:, ic * 128:(ic + 1) * 128],
                               src_grp[:, ic * 1024 + k * 512: ic * 1024 + (k + 1) * 512],
                               start=(ic == 0), stop=(ic == 31),
                               inc="pe" if ic == 31 else None)
                    em.wait("scalar", "pe", t[1])
                    gt = em.op("scalar", lambda e, oc=oc, k=k, pt=pt, dst=dst_grp: e.activation(
                        dst[:, oc * 1024 + k * 512: oc * 1024 + (k + 1) * 512],
                        pt[:, :], AF.Gelu, bias=0.0, scale=scol(oc, svec_idx)), inc="act")
                    bank_done(bank, ("act", gt[1]))
                    last = gt
                wsl_free[buf] = [("act", last[1])]

        hstage(hgrp, ggrp, h1t, 2, "dve")
        hstage(ggrp, hgrp, h2t, 3, "act")
        em.wait("tensor", "act", em.n("act"))
        for k, st in enumerate(sts):
            lt = dma("sync", "dx", ox[:, :], xT[st],
                     waits=[ox_free, xslot[st]])
            for hf in range(2):
                ft = None
                for oc8 in range(DC):
                    buf = oc8 % 2
                    ls = load_slice(w3t, oc8, buf, 4096)
                    em.wait("tensor", "dw", ls[1])
                    bank = "pw2" if oc8 % 2 == 0 else "pc0"
                    pt = pw[2] if oc8 % 2 == 0 else pc[0]
                    pe_use(bank)
                    t = None
                    for ic in range(32):
                        t = mm(pt[:, 0:256], wsl[buf][:, ic * 128:(ic + 1) * 128],
                               hgrp[:, ic * 1024 + k * 512 + hf * 256: ic * 1024 + k * 512 + hf * 256 + 256],
                               start=(ic == 0), stop=(ic == 31),
                               inc="pe" if ic == 31 else None)
                    wsl_free[buf] = [("pe", t[1])]
                    em.wait("scalar", "pe", t[1])
                    ft = em.op("scalar", lambda e, oc8=oc8, pt=pt: e.activation(
                        fsb[:, oc8 * 256:(oc8 + 1) * 256], pt[:, 0:256], AF.Copy, bias=0.0,
                        scale=scol(oc8, 4)), inc="act")
                    bank_done(bank, ("act", ft[1]))
                em.wait("vector", "act", ft[1])
                ln_stats(fsb, ggrp[:, 0:2048], ggrp[:, 2048:4096], W=256)
                em.wait("vector", "dx", lt[1])
                ot = None
                for c in range(DC):
                    def mk(c=c, hf=hf):
                        def sink(tB):
                            return em.op("vector", lambda e: e.tensor_tensor(
                                fsb[:, c * 256:(c + 1) * 256], tB,
                                ox[:, c * 512 + hf * 256: c * 512 + hf * 256 + 256],
                                ALU.add), inc="dve")
                        return sink
                    ot = ln_apply(fsb, c, mk(), W=256)
                em.wait("tensor", "dve", ot[1])
                for tt2 in range(2):
                    em.wait_tok("vector", outtm_free[0])
                    dt = None
                    for c in range(DC):
                        pidx = 3 + (c % 2)
                        pe_use("pw%d" % pidx)
                        tt = em.op("tensor", lambda e, c=c, tt2=tt2, p=pidx: e.transpose(
                            pw[p][:, 0:128],
                            fsb[:, c * 256 + tt2 * 128: c * 256 + (tt2 + 1) * 128],
                            id_f[:, :]), inc="pe")
                        em.wait("vector", "pe", tt[1])
                        dt = em.op("vector", lambda e, c=c, p=pidx: e.tensor_copy(
                            outtm[:, c * 128:(c + 1) * 128], pw[p][:, 0:128]), inc="dve")
                        bank_done("pw%d" % pidx, ("dve", dt[1]))
                    tile_idx = st * 4 + hf * 2 + tt2
                    od = dma("sync", "dout", out[tile_idx * 128:(tile_idx + 1) * 128, :], outtm[:, :],
                             waits=[("dve", dt[1])])
                    outtm_free[0] = ("dout", od[1])
            ox_free = [("dve", em.n("dve"))]


def _pcw(a, p=128):
    cp, w = a.shape
    c = cp // p
    return np.ascontiguousarray(a.reshape(c, p, w).transpose(1, 0, 2).reshape(p, c * w))


def prep_inputs(inputs):
    x = np.asarray(inputs["L0_attn_x"], np.float32).reshape(T * NCORES, D)
    shards = [np.ascontiguousarray(
        x[c * T:(c + 1) * T].reshape(NST, 4, 128, D).transpose(0, 2, 1, 3).reshape(NST, 128, 4 * D))
        for c in range(NCORES)]

    lnp = np.stack([
        np.asarray(inputs["L0_attn_ln1_w"], np.float32),
        np.asarray(inputs["L0_attn_ln1_b"], np.float32),
        np.asarray(inputs["L0_attn_ln2_w"], np.float32),
        np.asarray(inputs["L0_attn_ln2_b"], np.float32),
        np.asarray(inputs["L0_attn_bo"], np.float32),
    ], axis=1)
    lncp = np.ascontiguousarray(
        lnp.reshape(DEPTH, 5, DC, 128).transpose(0, 3, 2, 1).reshape(DEPTH, 128, 40))

    sp = np.zeros((5, FF), np.float32)
    sp[0] = np.asarray(inputs["L0_ffn_w1_s"], np.float32)
    sp[1] = np.asarray(inputs["L0_ffn_w2_s"], np.float32)
    sp[2] = np.asarray(inputs["L0_ffn_h1_s"], np.float32)
    sp[3] = np.asarray(inputs["L0_ffn_h2_s"], np.float32)
    sp[4, 0:D] = np.asarray(inputs["L0_ffn_w3_s"], np.float32)
    scp = np.ascontiguousarray(sp.reshape(5, 32, 128).transpose(2, 1, 0).reshape(128, 160))

    def wslices(wT, nslice, width):
        I, O = wT.shape
        c = I // 128
        a = wT.reshape(c, 128, nslice, width).transpose(2, 1, 0, 3)
        return np.ascontiguousarray(a.reshape(nslice, 128, c * width))

    w1f = np.asarray(inputs["L0_attn_w1"], np.float32)
    w2f = np.asarray(inputs["L0_attn_w2"], np.float32)
    w1h = np.stack([np.stack([_pcw(w1f[li][:, h * 2048:(h + 1) * 2048]) for h in range(2)])
                    for li in range(DEPTH)])
    w2h = np.stack([np.stack([_pcw(w2f[li][h * 2048:(h + 1) * 2048, :]) for h in range(2)])
                    for li in range(DEPTH)])
    common = {
        "wq": np.stack([_pcw(np.asarray(inputs["L0_attn_wq"], np.float32)[li]) for li in range(DEPTH)]),
        "wk": np.stack([_pcw(np.asarray(inputs["L0_attn_wk"], np.float32)[li]) for li in range(DEPTH)]),
        "wv": np.stack([_pcw(np.asarray(inputs["L0_attn_wv"], np.float32)[li]) for li in range(DEPTH)]),
        "wo": np.stack([_pcw(np.asarray(inputs["L0_attn_wo"], np.float32)[li]) for li in range(DEPTH)]),
        "w1": w1h,
        "w2": w2h,
        "lncp": lncp,
        "w1t": wslices(np.asarray(inputs["L0_ffn_w1_w"], np.float32).T, 32, 128),
        "w2t": wslices(np.asarray(inputs["L0_ffn_w2_w"], np.float32).T, 32, 128),
        "h1t": wslices(np.asarray(inputs["L0_ffn_h1_w"], np.float32).T, 32, 128),
        "h2t": wslices(np.asarray(inputs["L0_ffn_h2_w"], np.float32).T, 32, 128),
        "w3t": wslices(np.asarray(inputs["L0_ffn_w3_w"], np.float32).T, 8, 128),
        "scp": scp,
        "idf": np.eye(128, dtype=np.float32),
        "cbf": np.ascontiguousarray(np.stack(
            [np.full(128, 1.0 / D), np.ones(128)], axis=1).astype(ml_dtypes.bfloat16)),
        "orf": np.ones((1, 128), np.float32),
    }
    return shards, common


def kernel(**inputs):
    shards, common = prep_inputs(inputs)
    nc = build_nc()
    in_maps = [dict(common, x0=shards[c]) for c in range(NCORES)]
    res = run_bass_kernel_spmd(nc, in_maps, core_ids=list(range(NCORES)))
    outp = np.concatenate([np.asarray(res.results[c]["out"]) for c in range(NCORES)], axis=0)
    return outp.reshape(2, 16384, D)


# version 05 (final)
# speedup vs baseline: 1.2263x; 1.0178x over previous
import numpy as np
import ml_dtypes
from contextlib import ExitStack
import concourse.bass as bass
import concourse.mybir as mybir
from concourse.bass_utils import run_bass_kernel_spmd

F32 = mybir.dt.float32
BF16 = mybir.dt.bfloat16
AF = mybir.ActivationFunctionType
ALU = mybir.AluOpType

NCORES = 8
T = 4096
NST = 8
STW = 512
D = 1024
DC = 8
H = 8
FF = 4096
DEPTH = 6
EPS = 1e-5

ENGS = ("sync", "tensor", "vector", "scalar", "gpsimd")


def ctx_pos(h):
    if h < 6:
        b, j = h // 3, h % 3
        return b, b * 387 + j * 128, b * 387 + 384 + j
    b, j = 2, h - 6
    return 2, 774 + j * 128, 774 + 256 + j


class Emitter:
    def __init__(self):
        self.q = {e: [] for e in ENGS}
        self.cnt = {}
        self.lastwait = {}

    def n(self, sem):
        return self.cnt.get(sem, 0)

    def wait(self, eng, sem, val):
        if val is None or val <= 0:
            return
        key = (eng, sem)
        if self.lastwait.get(key, -1) >= val:
            return
        self.lastwait[key] = val
        self.q[eng].append(("w", sem, val))

    def wait_tok(self, eng, tok):
        if tok is None:
            return
        if isinstance(tok, list):
            for t in tok:
                self.wait_tok(eng, t)
        else:
            self.wait(eng, tok[0], tok[1])

    def op(self, eng, fn, inc=None, amt=1):
        if inc is None:
            self.q[eng].append(("i", fn, None, 0))
            return None
        self.cnt[inc] = self.cnt.get(inc, 0) + amt
        tot = self.cnt[inc]
        self.q[eng].append(("i", fn, inc, amt))
        return (inc, tot)

    def flush(self, block, sems):
        q = {e: list(self.q[e]) for e in ENGS}
        for e in ENGS:
            self.q[e] = []

        def replay(engine, items):
            for item in items:
                if item[0] == "w":
                    engine.wait_ge(sems[item[1]], item[2])
                else:
                    _, fn, inc, amt = item
                    ins = fn(engine)
                    if inc is not None:
                        ins.then_inc(sems[inc], amt)

        if q["sync"]:
            @block.sync
            def _(e, it=q["sync"]):
                replay(e, it)
        if q["tensor"]:
            @block.tensor
            def _(e, it=q["tensor"]):
                replay(e, it)
        if q["vector"]:
            @block.vector
            def _(e, it=q["vector"]):
                replay(e, it)
        if q["scalar"]:
            @block.scalar
            def _(e, it=q["scalar"]):
                replay(e, it)
        if q["gpsimd"]:
            @block.gpsimd
            def _(e, it=q["gpsimd"]):
                replay(e, it)


def build_nc(debug=None):
    nc = bass.Bass("TRN2", target_bir_lowering=False)

    x0 = nc.dram_tensor("x0", [NST, 128, 4 * D], F32, kind="ExternalInput").ap()
    wq_d = nc.dram_tensor("wq", [DEPTH, 128, DC * D], F32, kind="ExternalInput").ap()
    wk_d = nc.dram_tensor("wk", [DEPTH, 128, DC * D], F32, kind="ExternalInput").ap()
    wv_d = nc.dram_tensor("wv", [DEPTH, 128, DC * D], F32, kind="ExternalInput").ap()
    wo_d = nc.dram_tensor("wo", [DEPTH, 128, DC * D], F32, kind="ExternalInput").ap()
    w1_d = nc.dram_tensor("w1", [DEPTH, 2, 128, 2 * DC * D], F32, kind="ExternalInput").ap()
    w2_d = nc.dram_tensor("w2", [DEPTH, 2, 128, 2 * DC * D], F32, kind="ExternalInput").ap()
    lncp = nc.dram_tensor("lncp", [DEPTH, 128, 40], F32, kind="ExternalInput").ap()
    w1t = nc.dram_tensor("w1t", [32, 128, 1024], F32, kind="ExternalInput").ap()
    w2t = nc.dram_tensor("w2t", [32, 128, 1024], F32, kind="ExternalInput").ap()
    h1t = nc.dram_tensor("h1t", [32, 128, 4096], F32, kind="ExternalInput").ap()
    h2t = nc.dram_tensor("h2t", [32, 128, 4096], F32, kind="ExternalInput").ap()
    w3t = nc.dram_tensor("w3t", [8, 128, 4096], F32, kind="ExternalInput").ap()
    scp = nc.dram_tensor("scp", [128, 160], F32, kind="ExternalInput").ap()
    idf = nc.dram_tensor("idf", [128, 128], F32, kind="ExternalInput").ap()
    cbf = nc.dram_tensor("cbf", [128, 2], BF16, kind="ExternalInput").ap()
    orf = nc.dram_tensor("orf", [1, 128], F32, kind="ExternalInput").ap()

    out = nc.dram_tensor("out", [T, D], F32, kind="ExternalOutput").ap()

    xT = nc.dram_tensor("xT", [NST, 128, DC * STW], F32).ap()
    x0T = nc.dram_tensor("x0T", [NST, 128, DC * STW], F32).ap()
    qT = nc.dram_tensor("qT", [NST, 128, DC * STW], BF16).ap()
    xnT = nc.dram_tensor("xnT", [NST, 128, DC * STW], BF16).ap()
    ccin = nc.dram_tensor("ccin", [128, 1032], F32)
    ccout = nc.dram_tensor("ccout", [128, 1032], F32)

    dbg = {}
    if debug is not None:
        dbg["x"] = nc.dram_tensor("dbg_x", [NST, 128, DC * STW], F32, kind="ExternalOutput").ap()
        dbg["q"] = nc.dram_tensor("dbg_q", [NST, 128, DC * STW], BF16, kind="ExternalOutput").ap()
        dbg["ctx"] = nc.dram_tensor("dbg_ctx", [128, 1032], F32, kind="ExternalOutput").ap()

    em = Emitter()

    with ExitStack() as es:
        E = es.enter_context
        semnames = ["dx", "dxo", "dq", "dw", "dw2", "dcc", "cc", "pe", "act", "dve", "dout", "dmisc"]
        sems = {s: E(nc.semaphore(s)) for s in semnames}

        mreps = E(nc.sbuf_tensor("mreps", [128, 1040], F32))
        tmpA = E(nc.sbuf_tensor("tmpA", [128, 512], F32))
        rows = E(nc.sbuf_tensor("rows", [1, 1024], F32))
        lncols = [E(nc.sbuf_tensor("lncols0", [128, 40], F32)),
                  E(nc.sbuf_tensor("lncols1", [128, 40], F32))]
        id_f = E(nc.sbuf_tensor("id_f", [128, 128], F32))
        ccol = E(nc.sbuf_tensor("ccol", [128, 2], BF16))
        orow = E(nc.sbuf_tensor("orow", [1, 128], F32))
        epsr = E(nc.sbuf_tensor("epsr", [1, 1], F32))

        pc = [E(nc.psum_tensor("pc%d" % i, [128, 512], F32)) for i in range(3)]
        pw = [E(nc.psum_tensor("pw%d" % i, [128, 512], F32)) for i in range(5)]

        block = E(nc.Block())

        inner_es = ExitStack()
        IE = inner_es.enter_context
        xin = [IE(nc.sbuf_tensor("xin0", [128, DC * STW], F32)),
               IE(nc.sbuf_tensor("xin1", [128, DC * STW], F32))]
        qbuf = [IE(nc.sbuf_tensor("qbuf0", [128, DC * STW], BF16)),
                IE(nc.sbuf_tensor("qbuf1", [128, DC * STW], BF16))]
        w12h = IE(nc.sbuf_tensor("w12h", [128, 4 * DC * D], BF16))
        wqkv = IE(nc.sbuf_tensor("wqkv", [128, 3 * DC * D], BF16))
        wosb = IE(nc.sbuf_tensor("wosb", [128, DC * D], BF16))
        xn1T = IE(nc.sbuf_tensor("xn1T", [128, DC * STW], BF16))
        hq = IE(nc.sbuf_tensor("hq", [128, DC * STW], BF16))
        ekb = IE(nc.sbuf_tensor("ekb", [128, D], BF16))
        vbuf = IE(nc.sbuf_tensor("vbuf", [128, D], BF16))
        aoT = hq
        eqs = hq[:, 0:512]
        ctxbf = ekb
        ctxCC = mreps[:, 0:1032]

        def dma(eng, sem_, outap, inap, waits=()):
            for wtok in waits:
                em.wait_tok(eng, wtok)
            return em.op(eng, lambda e, o=outap, i=inap: e.dma_start(out=o, in_=i), inc=sem_, amt=16)

        bank_free = {}

        def pe_use(bank):
            em.wait_tok("tensor", bank_free.get(bank))

        def bank_done(bank, tok):
            bank_free[bank] = tok

        def mm(outap, lhs, rhs, start, stop, inc=None):
            return em.op("tensor",
                         lambda e, o=outap, l=lhs, r=rhs, st=start, sp=stop:
                         e.matmul(o, l, r, start=st, stop=sp, skip_group_check=True),
                         inc=inc)

        def lncol(bi, c, vec):
            return lncols[bi][:, c * 5 + vec: c * 5 + vec + 1]

        xslot = [None] * NST
        qslot = [None] * NST
        xnslot = [None] * NST
        hq_free = [None]
        xin_free = [None, None]
        qbuf_free = [None, None]

        pend = {}

        def ld_x(st):
            pend[("x", st)] = dma("sync", "dx", xin[st % 2][:, :], xT[st],
                                  waits=[xin_free[st % 2], xslot[st]])

        def ld_q(st):
            pend[("q", st)] = dma("sync", "dq", qbuf[st % 2][:, :], qT[st],
                                  waits=[qbuf_free[st % 2], qslot[st]])

        def ld_xn(st):
            xna_s = xin[st % 2][:, :].bitcast(BF16)[:, 0:DC * STW]
            pend[("xn", st)] = dma("sync", "dx", xna_s, xnT[st],
                                   waits=[xin_free[st % 2], xnslot[st]])

        def ln_stats(xsb, castb, sqb, W=STW):
            ct = em.op("vector", lambda e: e.tensor_copy(castb[:, 0:DC * W], xsb[:, 0:DC * W]), inc="dve")
            em.wait("scalar", "dve", ct[1])
            sq = em.op("scalar", lambda e: e.activation(sqb[:, 0:DC * W], castb[:, 0:DC * W], AF.Square), inc="act")
            em.wait("tensor", "dve", ct[1])
            pe_use("pw0")
            for c in range(DC):
                mm(pw[0][0:1, 0:W], ccol[:, 0:1], castb[:, c * W:(c + 1) * W],
                   start=(c == 0), stop=(c == DC - 1))
            em.wait("tensor", "act", sq[1])
            pe_use("pw1")
            mt = None
            for c in range(DC):
                mt = mm(pw[1][0:1, 0:W], ccol[:, 0:1], sqb[:, c * W:(c + 1) * W],
                        start=(c == 0), stop=(c == DC - 1), inc="pe" if c == DC - 1 else None)
            r_m = pw[0][0:1, 0:W]
            r_s = pw[1][0:1, 0:W]
            em.wait("scalar", "pe", mt[1])
            sqm = em.op("scalar", lambda e: e.activation(rows[0:1, 0:W], r_m, AF.Square), inc="act")
            em.wait("vector", "act", sqm[1])
            vt = em.op("vector", lambda e: e.tensor_tensor(rows[0:1, 512:512 + W], r_s, rows[0:1, 0:W], ALU.subtract), inc="dve")
            em.wait("scalar", "dve", vt[1])
            em.op("scalar", lambda e: e.activation(rows[0:1, 0:W], rows[0:1, 512:512 + W], AF.Ln, bias=epsr[0:1, 0:1], scale=1.0))
            rt = em.op("scalar", lambda e: e.activation(rows[0:1, 512:512 + W], rows[0:1, 0:W], AF.Exp, bias=0.0, scale=-0.5), inc="act")
            em.wait("vector", "act", rt[1])
            mr = em.op("vector", lambda e: e.tensor_tensor(rows[0:1, 0:W], r_m, rows[0:1, 512:512 + W], ALU.mult), inc="dve")
            em.wait("tensor", "dve", mr[1])
            pe_use("pw0")
            pe_use("pw1")
            mm(pw[0][:, 0:W], orow[:, :], rows[0:1, 512:512 + W], start=True, stop=True)
            t2 = mm(pw[1][:, 0:W], orow[:, :], rows[0:1, 0:W], start=True, stop=True, inc="pe")
            em.wait("vector", "pe", t2[1])
            em.op("vector", lambda e: e.tensor_copy(mreps[:, 0:W], pw[0][:, 0:W]))
            c2 = em.op("vector", lambda e: e.tensor_copy(mreps[:, 512:512 + W], pw[1][:, 0:W]), inc="dve")
            bank_done("pw0", ("dve", c2[1]))
            bank_done("pw1", ("dve", c2[1]))
            return c2

        def ln_apply(xsb, c, sink, wcol=None, bcol=None, W=STW):
            xc = xsb[:, c * W:(c + 1) * W]
            em.op("vector", lambda e: e.tensor_tensor(tmpA[:, 0:W], xc, mreps[:, 0:W], ALU.mult))
            em.op("vector", lambda e: e.tensor_tensor(tmpA[:, 0:W], tmpA[:, 0:W], mreps[:, 512:512 + W], ALU.subtract))
            if wcol is not None:
                return em.op("vector", lambda e, w=wcol, b=bcol: e.tensor_scalar(
                    sink, tmpA[:, 0:W], w, b, ALU.mult, ALU.add), inc="dve")
            if callable(sink):
                return sink(tmpA[:, 0:W])
            return em.op("vector", lambda e: e.tensor_copy(sink, tmpA[:, 0:W]), inc="dve")

        def load_wqkv(li):
            gw = ("pe", em.n("pe"))
            dma("gpsimd", "dw", wqkv[:, 0:DC * D], wq_d[li], waits=[gw])
            dma("gpsimd", "dw", wqkv[:, DC * D:2 * DC * D], wk_d[li])
            t = dma("gpsimd", "dw", wqkv[:, 2 * DC * D:3 * DC * D], wv_d[li])
            return ("dw", t[1])

        def load_wo(li):
            t = dma("gpsimd", "dw", wosb[:, :], wo_d[li],
                    waits=[("pe", em.n("pe"))])
            return ("dw", t[1])

        def load_w12h(li, half):
            gw = ("pe", em.n("pe"))
            dma("gpsimd", "dw", w12h[:, 0:2 * DC * D], w1_d[li, half], waits=[gw])
            t = dma("gpsimd", "dw", w12h[:, 2 * DC * D:4 * DC * D], w2_d[li, half])
            return ("dw", t[1])

        def load_lncols(li, bi):
            t = dma("sync", "dmisc", lncols[bi][:, :], lncp[li],
                    waits=[("dve", em.n("dve"))])
            return ("dmisc", t[1])

        dma("sync", "dmisc", id_f[:, :], idf)
        dma("sync", "dmisc", ccol[:, :], cbf)
        ct0 = dma("sync", "dmisc", orow[:, :], orf)
        em.op("vector", lambda e: e.memset(epsr[:, :], EPS), inc="dve")
        for eng in ("tensor", "vector", "scalar"):
            em.wait(eng, "dmisc", ct0[1])

        wqkv_rdy = {0: load_wqkv(0)}
        wo_rdy = {0: load_wo(0)}
        w12_rdy = {(0, 0): load_w12h(0, 0)}
        ln_rdy = {0: load_lncols(0, 0)}

        for st in range(NST):
            lt = dma("sync", "dx", xin[0][:, :], x0[st],
                     waits=[xin_free[0], xin_free[1]])
            em.wait("tensor", "dx", lt[1])
            ctok = None
            for t4 in range(4):
                for c in range(DC):
                    pidx = 2 + ((t4 * DC + c) % 2)
                    pe_use("pw%d" % pidx)
                    tt = em.op("tensor", lambda e, t4=t4, c=c, p=pidx: e.transpose(
                        pw[p][:, 0:128], xin[0][:, t4 * D + c * 128:t4 * D + c * 128 + 128],
                        id_f[:, :]), inc="pe")
                    em.wait("vector", "pe", tt[1])
                    ctok = em.op("vector", lambda e, t4=t4, c=c, p=pidx: e.tensor_copy(
                        xin[1][:, c * 512 + t4 * 128: c * 512 + t4 * 128 + 128], pw[p][:, 0:128]),
                        inc="dve")
                    bank_done("pw%d" % pidx, ("dve", ctok[1]))
            s0 = dma("sync", "dxo", x0T[st], xin[1][:, :],
                     waits=[("dve", ctok[1])])
            em.wait_tok("scalar", hq_free[0])
            ln_stats(xin[1], xn1T, hq)
            atok = None
            for c in range(DC):
                atok = ln_apply(xin[1], c, xin[0][:, c * 512:(c + 1) * 512])
            s1 = dma("sync", "dxo", xT[st], xin[0][:, :],
                     waits=[("dve", atok[1])])
            xslot[st] = ("dxo", s1[1])
            em.wait_tok("vector", ln_rdy[0])
            ln_stats(xin[0], xn1T, hq)
            at2 = None
            for c in range(DC):
                at2 = ln_apply(xin[0], c, xn1T[:, c * 512:(c + 1) * 512],
                               wcol=lncol(0, c, 0), bcol=lncol(0, c, 1))
            s2 = dma("sync", "dq", xnT[st], xn1T[:, :], waits=[("dve", at2[1])])
            xnslot[st] = ("dq", s2[1])
            hq_free[0] = ("dq", s2[1])
            xin_free[0] = [("dxo", s1[1]), ("dve", at2[1])]
            xin_free[1] = [("dxo", s0[1]), ("dve", atok[1])]

        def sweep_A(li, pre=None):
            em.wait_tok("tensor", wqkv_rdy[li])
            em.wait_tok("scalar", hq_free[0])
            wq_off, wk_off, wv_off = 0, DC * D, 2 * DC * D
            ntqs = {}
            if ("xn", 0) not in pend:
                ld_xn(0)
            for st in range(NST):
                xb = st % 2
                qb = st % 2
                xna = xin[xb][:, :].bitcast(BF16)[:, 0:DC * STW]
                if st + 1 < NST:
                    ld_xn(st + 1)
                elif pre is not None:
                    pre()
                em.wait_tok("tensor", pend.pop(("xn", st)))
                etoks = {}
                rts = {}

                def emit_qmm(oc, chalf=None):
                    pidx = 2 + (oc % 2)
                    crange = range(DC) if chalf is None else (
                        range(0, DC // 2) if chalf == 0 else range(DC // 2, DC))
                    if chalf in (None, 0):
                        pe_use("pw%d" % pidx)
                    qmt = None
                    for c in crange:
                        qmt = mm(pw[pidx][:, :],
                                 wqkv[:, wq_off + c * D + oc * 128: wq_off + c * D + oc * 128 + 128],
                                 xna[:, c * 512:(c + 1) * 512],
                                 start=(c == 0), stop=(c == DC - 1),
                                 inc="pe" if c == DC - 1 else None)
                    if chalf in (None, 1):
                        em.wait("scalar", "pe", qmt[1])
                        if (st * H + oc) >= 2:
                            prev = ntqs.get((st * H + oc) - 2)
                            if prev is not None:
                                em.wait("scalar", "dve", prev[1])
                        et = em.op("scalar", lambda e, p=pidx, oc=oc: e.activation(
                            hq[:, (oc % 2) * 512:(oc % 2) * 512 + 512], pw[p][:, :], AF.Exp),
                            inc="act")
                        bank_done("pw%d" % pidx, ("act", et[1]))
                        etoks[oc] = et

                def emit_zq(oc):
                    em.wait("tensor", "act", etoks[oc][1])
                    pe_use("pw4")
                    zt = mm(pw[4][0:1, 0:512], ccol[:, 1:2],
                            hq[:, (oc % 2) * 512:(oc % 2) * 512 + 512],
                            start=True, stop=True, inc="pe")
                    em.wait("vector", "pe", zt[1])
                    rt = em.op("vector", lambda e, oc=oc: e.reciprocal(
                        rows[0:1, (oc % 2) * 512:(oc % 2) * 512 + 512], pw[4][0:1, 0:512]),
                        inc="dve")
                    bank_done("pw4", ("dve", rt[1]))
                    rts[oc] = rt

                def emit_zrep_ntq(oc):
                    em.wait("tensor", "dve", rts[oc][1])
                    pe_use("pw4")
                    zr = mm(pw[4][:, :], orow[:, :],
                            rows[0:1, (oc % 2) * 512:(oc % 2) * 512 + 512],
                            start=True, stop=True, inc="pe")
                    em.wait("vector", "pe", zr[1])
                    if oc == 0:
                        em.wait_tok("vector", qbuf_free[qb])
                    ntq = em.op("vector", lambda e, oc=oc, qb=qb: e.tensor_tensor(
                        qbuf[qb][:, oc * 512:(oc + 1) * 512],
                        hq[:, (oc % 2) * 512:(oc % 2) * 512 + 512],
                        pw[4][:, :], ALU.mult), inc="dve")
                    bank_done("pw4", ("dve", ntq[1]))
                    ntqs[st * H + oc] = ntq
                    return ntq

                emit_qmm(0)
                emit_qmm(1)
                ntq = None
                for oc in range(H):
                    emit_zq(oc)
                    if oc + 2 < H:
                        emit_qmm(oc + 2, chalf=0)
                    ntq = emit_zrep_ntq(oc)
                    if oc + 2 < H:
                        emit_qmm(oc + 2, chalf=1)
                qs = dma("sync", "dq", qT[st], qbuf[qb][:, :],
                         waits=[("dve", ntq[1]), qslot[st]])
                qslot[st] = ("dq", qs[1])
                qbuf_free[qb] = [("dq", qs[1])]
                for t4 in range(4):
                    first = (st == 0 and t4 == 0)
                    last = (st == NST - 1 and t4 == 3)
                    ktoks = []
                    for f in range(2):
                        pe_use("pw%d" % (2 + f))
                        kt = None
                        for c in range(DC):
                            kt = mm(pw[2 + f][:, :],
                                    xna[:, c * 512 + t4 * 128: c * 512 + t4 * 128 + 128],
                                    wqkv[:, wk_off + c * D + f * 512: wk_off + c * D + (f + 1) * 512],
                                    start=(c == 0), stop=(c == DC - 1),
                                    inc="pe" if c == DC - 1 else None)
                        ktoks.append(kt)
                        em.wait("scalar", "pe", kt[1])
                        et = em.op("scalar", lambda e, f=f: e.activation(
                            ekb[:, f * 512:(f + 1) * 512], pw[2 + f][:, :], AF.Exp), inc="act")
                        bank_done("pw%d" % (2 + f), ("act", et[1]))
                        ktoks[f] = ("act", et[1])
                    vcts = []
                    for f in range(2):
                        pe_use("pw%d" % f)
                        vt = None
                        for c in range(DC):
                            vt = mm(pw[f][:, :],
                                    xna[:, c * 512 + t4 * 128: c * 512 + t4 * 128 + 128],
                                    wqkv[:, wv_off + c * D + f * 512: wv_off + c * D + (f + 1) * 512],
                                    start=(c == 0), stop=(c == DC - 1),
                                    inc="pe" if c == DC - 1 else None)
                        em.wait("vector", "pe", vt[1])
                        vct = em.op("vector", lambda e, f=f: e.tensor_copy(
                            vbuf[:, f * 512:(f + 1) * 512], pw[f][:, :]), inc="dve")
                        bank_done("pw%d" % f, ("dve", vct[1]))
                        vcts.append(vct)
                    for f in range(2):
                        em.wait("tensor", "act", ktoks[f][1])
                        em.wait("tensor", "dve", vcts[f][1])
                        for h in range(4 * f, 4 * f + 4):
                            b, coff, zoff = ctx_pos(h)
                            mm(pc[b][:, coff - b * 387:coff - b * 387 + 128],
                               ekb[:, h * 128:(h + 1) * 128],
                               vbuf[:, h * 128:(h + 1) * 128],
                               start=(first and h in (0, 3, 6)), stop=last)
                            mm(pc[b][:, zoff - b * 387: zoff - b * 387 + 1],
                               ekb[:, h * 128:(h + 1) * 128],
                               ccol[:, 1:2],
                               start=False, stop=last,
                               inc="pe" if (last and h == H - 1) else None)
                xin_free[xb] = [("pe", em.n("pe"))]
            em.wait("vector", "pe", em.n("pe"))
            em.op("vector", lambda e: e.tensor_copy(ctxCC[:, 0:387], pc[0][:, 0:387]))
            em.op("vector", lambda e: e.tensor_copy(ctxCC[:, 387:774], pc[1][:, 0:387]))
            ev = em.op("vector", lambda e: e.tensor_copy(ctxCC[:, 774:1032], pc[2][:, 0:258]), inc="dve")
            for b in range(3):
                bank_done("pc%d" % b, ("dve", ev[1]))
            ci = dma("gpsimd", "dcc", ccin[:, :], ctxCC, waits=[("dve", ev[1])])
            em.wait("gpsimd", "dcc", ci[1])
            cct = em.op("gpsimd", lambda e: e.collective_compute(
                "AllReduce", ALU.add,
                replica_groups=[[0, 1, 2, 3], [4, 5, 6, 7]],
                ins=[ccin.ap().opt()], outs=[ccout.ap().opt()]), inc="cc")
            em.wait("gpsimd", "cc", cct[1])
            co = dma("gpsimd", "dcc", ctxCC, ccout[:, :])
            return co

        def sweep_B(li, co, pre=None):
            lnb = li % 2
            em.wait_tok("tensor", wo_rdy[li])
            em.wait("vector", "dcc", co[1])
            rc = None
            for h in range(H):
                b, coff, zoff = ctx_pos(h)
                rc = em.op("vector", lambda e, z=zoff, h=h: e.reciprocal(
                    tmpA[:, h:h + 1], ctxCC[:, z:z + 1]),
                    inc="dve" if h == H - 1 else None)
            em.wait("vector", "dve", rc[1])
            nt = None
            for h in range(H):
                b, coff, zoff = ctx_pos(h)
                nt = em.op("vector", lambda e, h=h, c0=coff: e.tensor_scalar(
                    ctxbf[:, h * 128:(h + 1) * 128], ctxCC[:, c0:c0 + 128],
                    tmpA[:, h:h + 1], None, ALU.mult), inc="dve")
            em.wait("tensor", "dve", nt[1])
            if ("x", 0) not in pend:
                ld_x(0)
                ld_q(0)
            for st in range(NST):
                xb = st % 2
                qb = st % 2
                if st + 1 < NST:
                    ld_x(st + 1)
                    ld_q(st + 1)
                elif pre is not None:
                    pre()
                xsb = xin[xb]
                xtok = pend.pop(("x", st))
                em.wait_tok("tensor", pend.pop(("q", st)))
                if li == 0 and st == 0:
                    pass
                aot = None
                ao_banks = [("pw4", pw[4]), ("pc0", pc[0]), ("pc1", pc[1]), ("pc2", pc[2])]
                for h in range(H):
                    bank, pt = ao_banks[h % 4]
                    pe_use(bank)
                    t = mm(pt[:, :], ctxbf[:, h * 128:(h + 1) * 128],
                           qbuf[qb][:, h * 512:(h + 1) * 512], start=True, stop=True, inc="pe")
                    em.wait("scalar", "pe", t[1])
                    aot = em.op("scalar", lambda e, h=h, pt=pt: e.activation(
                        aoT[:, h * 512:(h + 1) * 512], pt[:, :], AF.Copy), inc="act")
                    bank_done(bank, ("act", aot[1]))
                ao_done = ("pe", em.n("pe"))
                em.wait("tensor", "act", aot[1])
                em.wait_tok("vector", xtok)
                for half in range(2):
                    wtoks = []
                    for ds in range(4):
                        d_idx = half * 4 + ds
                        pe_use("pw%d" % ds)
                        t = None
                        for ec in range(DC):
                            t = mm(pw[ds][:, :],
                                   wosb[:, ec * D + d_idx * 128: ec * D + d_idx * 128 + 128],
                                   aoT[:, ec * 512:(ec + 1) * 512],
                                   start=(ec == 0), stop=(ec == DC - 1),
                                   inc="pe" if ec == DC - 1 else None)
                        wtoks.append(t)
                    for ds in range(4):
                        d_idx = half * 4 + ds
                        em.wait("vector", "pe", wtoks[ds][1])
                        em.op("vector", lambda e, ds=ds, d=d_idx, b=lnb: e.tensor_scalar(
                            tmpA[:, :], pw[ds][:, :], lncol(b, d, 4), None, ALU.add))
                        rt = em.op("vector", lambda e, d=d_idx, xsb=xsb: e.tensor_tensor(
                            xsb[:, d * 512:(d + 1) * 512], xsb[:, d * 512:(d + 1) * 512],
                            tmpA[:, :], ALU.add), inc="dve")
                        bank_done("pw%d" % ds, ("dve", rt[1]))
                ln_stats(xsb, xn1T, hq)
                em.wait_tok("vector", [ao_done])
                at = None
                for c in range(DC):
                    at = ln_apply(xsb, c, qbuf[qb][:, c * 512:(c + 1) * 512],
                                  wcol=lncol(lnb, c, 2), bcol=lncol(lnb, c, 3))
                qs = dma("sync", "dq", qT[st], qbuf[qb][:, :], waits=[("dve", at[1])])
                qslot[st] = ("dq", qs[1])
                qbuf_free[qb] = [("dq", qs[1])]
                so = dma("sync", "dxo", xT[st], xsb[:, :],
                         waits=[("dve", at[1])])
                xslot[st] = ("dxo", so[1])
                xin_free[xb] = [("dxo", so[1])]

        def sweep_F(li, half, last=False, stash=False, pre=None):
            em.wait_tok("tensor", w12_rdy[(li, half)])
            if stash:
                em.wait_tok("vector", ln_rdy[li + 1])
            if ("x", 0) not in pend:
                ld_x(0)
                ld_q(0)
            for st in range(NST):
                xb = st % 2
                qb = st % 2
                if last:
                    if st > 0 and ("x", st) not in pend:
                        ld_x(st)
                        ld_q(st)
                elif st + 1 < NST:
                    ld_x(st + 1)
                    ld_q(st + 1)
                elif pre is not None:
                    pre()
                xsb = xin[xb]
                em.wait_tok("tensor", pend.pop(("q", st)))
                em.wait_tok("vector", pend.pop(("x", st)))
                yt = None
                for Q in range(2):
                    hts = []
                    for oc8 in range(DC):
                        bank = "pw4" if oc8 % 2 == 0 else "pc0"
                        pt = pw[4] if oc8 % 2 == 0 else pc[0]
                        pe_use(bank)
                        t = None
                        for c in range(DC):
                            t = mm(pt[:, :],
                                   w12h[:, c * 2048 + Q * 1024 + oc8 * 128: c * 2048 + Q * 1024 + oc8 * 128 + 128],
                                   qbuf[qb][:, c * 512:(c + 1) * 512],
                                   start=(c == 0), stop=(c == DC - 1),
                                   inc="pe" if c == DC - 1 else None)
                        em.wait("scalar", "pe", t[1])
                        em.wait_tok("scalar", hq_free[0])
                        ht = em.op("scalar", lambda e, oc8=oc8, pt=pt: e.activation(
                            hq[:, oc8 * 512:(oc8 + 1) * 512], pt[:, :], AF.Lrelu), inc="act")
                        bank_done(bank, ("act", ht[1]))
                        hts.append(ht)
                    em.wait("tensor", "act", hts[-1][1])
                    for dh in range(2):
                        ytoks = []
                        for ds in range(4):
                            d_idx = dh * 4 + ds
                            pe_use("pw%d" % ds)
                            t = None
                            for oc8 in range(DC):
                                woff = 2 * DC * D + (Q * DC + oc8) * D + d_idx * 128
                                t = mm(pw[ds][:, :],
                                       w12h[:, woff: woff + 128],
                                       hq[:, oc8 * 512:(oc8 + 1) * 512],
                                       start=(oc8 == 0), stop=(oc8 == DC - 1),
                                       inc="pe" if oc8 == DC - 1 else None)
                            ytoks.append(t)
                        for ds in range(4):
                            d_idx = dh * 4 + ds
                            em.wait("vector", "pe", ytoks[ds][1])
                            yt = em.op("vector", lambda e, ds=ds, d=d_idx, xsb=xsb: e.tensor_tensor(
                                xsb[:, d * 512:(d + 1) * 512], xsb[:, d * 512:(d + 1) * 512],
                                pw[ds][:, :], ALU.add), inc="dve")
                            bank_done("pw%d" % ds, ("dve", yt[1]))
                qbuf_free[qb] = [("pe", em.n("pe"))]
                if not last:
                    so = dma("sync", "dxo", xT[st], xsb[:, :],
                             waits=[("dve", yt[1])])
                    xslot[st] = ("dxo", so[1])
                    if stash:
                        em.wait("scalar", "pe", em.n("pe"))
                        ln_stats(xsb, xn1T, hq)
                        hq_free[0] = ("pe", em.n("pe"))
                        em.wait_tok("vector", qbuf_free[qb])
                        at2 = None
                        for c in range(DC):
                            at2 = ln_apply(xsb, c, qbuf[qb][:, c * 512:(c + 1) * 512],
                                           wcol=lncol((li + 1) % 2, c, 0),
                                           bcol=lncol((li + 1) % 2, c, 1))
                        s2 = dma("sync", "dq", xnT[st], qbuf[qb][:, :], waits=[("dve", at2[1])])
                        xnslot[st] = ("dq", s2[1])
                        qbuf_free[qb] = [("dq", s2[1])]
                        xin_free[xb] = [("dxo", so[1]), ("dve", at2[1])]
                    else:
                        xin_free[xb] = [("dxo", so[1])]
                else:
                    ln_stats(xsb, xn1T, hq)
                    ob = (st + 1) % 2
                    l0 = dma("sync", "dx", xin[ob][:, :], x0T[st],
                             waits=[xin_free[ob]])
                    em.wait("vector", "dx", l0[1])
                    at = None
                    for c in range(DC):
                        def mk(c=c, ob=ob):
                            def sink(tB):
                                return em.op("vector", lambda e: e.tensor_tensor(
                                    xin[ob][:, c * 512:(c + 1) * 512], tB,
                                    xin[ob][:, c * 512:(c + 1) * 512], ALU.add), inc="dve")
                            return sink
                        at = ln_apply(xsb, c, mk())
                    so = dma("sync", "dxo", xT[st], xin[ob][:, :],
                             waits=[("dve", at[1])])
                    xslot[st] = ("dxo", so[1])
                    xin_free[xb] = [("dve", at[1])]
                    xin_free[ob] = [("dxo", so[1])]

        if isinstance(debug, tuple):
            n_layers, stop_stage = debug[1] + 1, debug[0]
        elif debug == "entry":
            n_layers, stop_stage = 0, None
        else:
            n_layers, stop_stage = DEPTH, None

        def pre_xq():
            ld_x(0)
            ld_q(0)

        def pre_xn():
            ld_xn(0)

        for li in range(n_layers):
            co = sweep_A(li, pre=pre_xq)
            if stop_stage == "A" and li == n_layers - 1:
                break
            sweep_B(li, co, pre=pre_xq)
            if stop_stage == "B" and li == n_layers - 1:
                break
            if li + 1 < DEPTH:
                wqkv_rdy[li + 1] = load_wqkv(li + 1)
                ln_rdy[li + 1] = load_lncols(li + 1, (li + 1) % 2)
            sweep_F(li, 0, pre=pre_xq)
            if li + 1 < DEPTH:
                wo_rdy[li + 1] = load_wo(li + 1)
            w12_rdy[(li, 1)] = load_w12h(li, 1)
            sweep_F(li, 1, last=(li == DEPTH - 1), stash=(li + 1 < DEPTH),
                    pre=(pre_xn if li + 1 < DEPTH else None))
            if li + 1 < DEPTH:
                w12_rdy[(li + 1, 0)] = load_w12h(li + 1, 0)

        if debug is not None:
            em.wait("sync", "dxo", em.n("dxo"))
            em.wait("sync", "dq", em.n("dq"))
            dma("sync", "dout", dbg["x"], xT)
            dma("sync", "dout", dbg["q"], qT)
            em.wait("gpsimd", "dcc", em.n("dcc"))
            dma("gpsimd", "dout", dbg["ctx"], ccout[:, :])
            em.wait("sync", "dout", em.n("dout"))
            em.wait("gpsimd", "dout", em.n("dout"))
            em.flush(block, sems)
            inner_es.close()
        else:
            em.flush(block, sems)
            inner_es.close()
            emit_outer(nc, em, sems, dict(
                dma=dma, mm=mm, pe_use=pe_use, bank_done=bank_done,
                ln_stats=ln_stats, ln_apply=ln_apply,
                xT=xT, pw=pw, pc=pc, rows=rows, id_f=id_f, ccol=ccol,
                orow=orow, epsr=epsr, tmpA=tmpA, mreps=mreps, out=out,
                xslot=xslot, w1t=w1t, w2t=w2t, h1t=h1t, h2t=h2t,
                w3t=w3t, scp=scp), E, em)
            em.wait("sync", "dout", em.n("dout"))
            em.flush(block, sems)

    return nc


def emit_outer(nc, em, sems, L, E, _em):
    dma, mm, pe_use, bank_done = L["dma"], L["mm"], L["pe_use"], L["bank_done"]
    ln_stats, ln_apply = L["ln_stats"], L["ln_apply"]
    xT, pw, pc = L["xT"], L["pw"], L["pc"]
    id_f, ccol = L["id_f"], L["ccol"]
    out = L["out"]
    xslot = L["xslot"]
    w1t, w2t, h1t, h2t, w3t, scp = (L[k] for k in ("w1t", "w2t", "h1t", "h2t", "w3t", "scp"))

    ox = E(nc.sbuf_tensor("ox", [128, DC * STW], F32))
    hgrp = E(nc.sbuf_tensor("hgrp", [128, 32 * 1024], BF16))
    ggrp = E(nc.sbuf_tensor("ggrp", [128, 32 * 1024], BF16))
    wsl = [E(nc.sbuf_tensor("wsl0", [128, 4096], BF16)),
           E(nc.sbuf_tensor("wsl1", [128, 4096], BF16))]
    zT = [E(nc.sbuf_tensor("zT0", [128, 4096], BF16)),
          E(nc.sbuf_tensor("zT1", [128, 4096], BF16))]
    fsb = E(nc.sbuf_tensor("fsb", [128, 2048], F32))
    outtm = E(nc.sbuf_tensor("outtm", [128, 1024], F32))
    scols = E(nc.sbuf_tensor("scols", [128, 160], F32))
    stmp = E(nc.sbuf_tensor("stmp", [128, 1024], BF16))

    sct = dma("sync", "dmisc", scols[:, :], scp)
    for eng in ("tensor", "vector", "scalar"):
        em.wait(eng, "dmisc", sct[1])

    def scol(oc, vec):
        return scols[:, oc * 5 + vec: oc * 5 + vec + 1]

    wsl_free = [None, None]
    outtm_free = [None]

    def wq_of(buf):
        return ("gpsimd", "dw")

    def load_slice(src, oc, buf, width):
        eng, sem = wq_of(buf)
        return dma(eng, sem, wsl[buf][:, 0:width], src[oc],
                   waits=[wsl_free[buf]])

    ox_free = None
    for g in range(4):
        sts = [2 * g, 2 * g + 1]
        for k, st in enumerate(sts):
            lt = dma("sync", "dx", ox[:, :], xT[st],
                     waits=[ox_free, xslot[st]])
            em.wait("vector", "dx", lt[1])
            ln_stats(ox, hgrp[:, 0:4096], hgrp[:, 4096:8192])
            at = None
            for c in range(DC):
                at = ln_apply(ox, c, zT[k][:, c * 512:(c + 1) * 512])
            ox_free = [("dve", at[1])]
        em.wait("tensor", "dve", em.n("dve"))
        ht = None
        for oc in range(32):
            buf = oc % 2
            eng, sem = wq_of(buf)
            dma(eng, sem, wsl[buf][:, 0:1024], w1t[oc],
                waits=[wsl_free[buf]])
            l2 = dma(eng, sem, wsl[buf][:, 1024:2048], w2t[oc])
            em.wait_tok("tensor", l2)
            for k in range(2):
                pe_use("pw2")
                for c in range(DC):
                    mm(pw[2][:, :], wsl[buf][:, c * 128:(c + 1) * 128],
                       zT[k][:, c * 512:(c + 1) * 512], start=(c == 0), stop=(c == DC - 1))
                pe_use("pw3")
                tb = None
                for c in range(DC):
                    tb = mm(pw[3][:, :], wsl[buf][:, 1024 + c * 128: 1024 + (c + 1) * 128],
                            zT[k][:, c * 512:(c + 1) * 512], start=(c == 0), stop=(c == DC - 1),
                            inc="pe" if c == DC - 1 else None)
                em.wait("scalar", "pe", tb[1])
                if ht is not None:
                    em.wait("scalar", "dve", ht[1])
                sa = em.op("scalar", lambda e, oc=oc: e.activation(
                    stmp[:, 0:512], pw[2][:, :], AF.Silu, bias=0.0, scale=scol(oc, 0)), inc="act")
                sb2 = em.op("scalar", lambda e, oc=oc: e.activation(
                    stmp[:, 512:1024], pw[3][:, :], AF.Copy, bias=0.0, scale=scol(oc, 1)), inc="act")
                bank_done("pw2", ("act", sa[1]))
                bank_done("pw3", ("act", sb2[1]))
                em.wait("vector", "act", sb2[1])
                ht = em.op("vector", lambda e, oc=oc, k=k: e.tensor_tensor(
                    hgrp[:, oc * 1024 + k * 512: oc * 1024 + (k + 1) * 512],
                    stmp[:, 0:512], stmp[:, 512:1024], ALU.mult), inc="dve")
            wsl_free[buf] = [("dve", ht[1])]

        def hstage(src_grp, dst_grp, wmat, svec_idx, src_sem):
            em.wait("tensor", src_sem, em.n(src_sem))
            for oc in range(32):
                buf = oc % 2
                ls = load_slice(wmat, oc, buf, 4096)
                em.wait_tok("tensor", ls)
                last = None
                for k in range(2):
                    bank = "pw2" if k == 0 else "pc0"
                    pt = pw[2] if k == 0 else pc[0]
                    pe_use(bank)
                    t = None
                    for ic in range(32):
                        t = mm(pt[:, :], wsl[buf][:, ic * 128:(ic + 1) * 128],
                               src_grp[:, ic * 1024 + k * 512: ic * 1024 + (k + 1) * 512],
                               start=(ic == 0), stop=(ic == 31),
                               inc="pe" if ic == 31 else None)
                    em.wait("scalar", "pe", t[1])
                    gt = em.op("scalar", lambda e, oc=oc, k=k, pt=pt, dst=dst_grp: e.activation(
                        dst[:, oc * 1024 + k * 512: oc * 1024 + (k + 1) * 512],
                        pt[:, :], AF.Gelu, bias=0.0, scale=scol(oc, svec_idx)), inc="act")
                    bank_done(bank, ("act", gt[1]))
                    last = gt
                wsl_free[buf] = [("act", last[1])]

        hstage(hgrp, ggrp, h1t, 2, "dve")
        hstage(ggrp, hgrp, h2t, 3, "act")
        em.wait("tensor", "act", em.n("act"))
        for k, st in enumerate(sts):
            lt = dma("sync", "dx", ox[:, :], xT[st],
                     waits=[ox_free, xslot[st]])
            for hf in range(2):
                ft = None
                for oc8 in range(DC):
                    buf = oc8 % 2
                    ls = load_slice(w3t, oc8, buf, 4096)
                    em.wait_tok("tensor", ls)
                    bank = "pw2" if oc8 % 2 == 0 else "pc0"
                    pt = pw[2] if oc8 % 2 == 0 else pc[0]
                    pe_use(bank)
                    t = None
                    for ic in range(32):
                        t = mm(pt[:, 0:256], wsl[buf][:, ic * 128:(ic + 1) * 128],
                               hgrp[:, ic * 1024 + k * 512 + hf * 256: ic * 1024 + k * 512 + hf * 256 + 256],
                               start=(ic == 0), stop=(ic == 31),
                               inc="pe" if ic == 31 else None)
                    wsl_free[buf] = [("pe", t[1])]
                    em.wait("scalar", "pe", t[1])
                    ft = em.op("scalar", lambda e, oc8=oc8, pt=pt: e.activation(
                        fsb[:, oc8 * 256:(oc8 + 1) * 256], pt[:, 0:256], AF.Copy, bias=0.0,
                        scale=scol(oc8, 4)), inc="act")
                    bank_done(bank, ("act", ft[1]))
                em.wait("vector", "act", ft[1])
                ln_stats(fsb, ggrp[:, 0:2048], ggrp[:, 2048:4096], W=256)
                em.wait("vector", "dx", lt[1])
                ot = None
                for c in range(DC):
                    def mk(c=c, hf=hf):
                        def sink(tB):
                            return em.op("vector", lambda e: e.tensor_tensor(
                                fsb[:, c * 256:(c + 1) * 256], tB,
                                ox[:, c * 512 + hf * 256: c * 512 + hf * 256 + 256],
                                ALU.add), inc="dve")
                        return sink
                    ot = ln_apply(fsb, c, mk(), W=256)
                em.wait("tensor", "dve", ot[1])
                for tt2 in range(2):
                    em.wait_tok("vector", outtm_free[0])
                    dt = None
                    for c in range(DC):
                        pidx = 3 + (c % 2)
                        pe_use("pw%d" % pidx)
                        tt = em.op("tensor", lambda e, c=c, tt2=tt2, p=pidx: e.transpose(
                            pw[p][:, 0:128],
                            fsb[:, c * 256 + tt2 * 128: c * 256 + (tt2 + 1) * 128],
                            id_f[:, :]), inc="pe")
                        em.wait("vector", "pe", tt[1])
                        dt = em.op("vector", lambda e, c=c, p=pidx: e.tensor_copy(
                            outtm[:, c * 128:(c + 1) * 128], pw[p][:, 0:128]), inc="dve")
                        bank_done("pw%d" % pidx, ("dve", dt[1]))
                    tile_idx = st * 4 + hf * 2 + tt2
                    od = dma("sync", "dout", out[tile_idx * 128:(tile_idx + 1) * 128, :], outtm[:, :],
                             waits=[("dve", dt[1])])
                    outtm_free[0] = ("dout", od[1])
            ox_free = [("dve", em.n("dve"))]


def _pcw(a, p=128):
    cp, w = a.shape
    c = cp // p
    return np.ascontiguousarray(a.reshape(c, p, w).transpose(1, 0, 2).reshape(p, c * w))


def prep_inputs(inputs):
    x = np.asarray(inputs["L0_attn_x"], np.float32).reshape(T * NCORES, D)
    shards = [np.ascontiguousarray(
        x[c * T:(c + 1) * T].reshape(NST, 4, 128, D).transpose(0, 2, 1, 3).reshape(NST, 128, 4 * D))
        for c in range(NCORES)]

    lnp = np.stack([
        np.asarray(inputs["L0_attn_ln1_w"], np.float32),
        np.asarray(inputs["L0_attn_ln1_b"], np.float32),
        np.asarray(inputs["L0_attn_ln2_w"], np.float32),
        np.asarray(inputs["L0_attn_ln2_b"], np.float32),
        np.asarray(inputs["L0_attn_bo"], np.float32),
    ], axis=1)
    lncp = np.ascontiguousarray(
        lnp.reshape(DEPTH, 5, DC, 128).transpose(0, 3, 2, 1).reshape(DEPTH, 128, 40))

    sp = np.zeros((5, FF), np.float32)
    sp[0] = np.asarray(inputs["L0_ffn_w1_s"], np.float32)
    sp[1] = np.asarray(inputs["L0_ffn_w2_s"], np.float32)
    sp[2] = np.asarray(inputs["L0_ffn_h1_s"], np.float32)
    sp[3] = np.asarray(inputs["L0_ffn_h2_s"], np.float32)
    sp[4, 0:D] = np.asarray(inputs["L0_ffn_w3_s"], np.float32)
    scp = np.ascontiguousarray(sp.reshape(5, 32, 128).transpose(2, 1, 0).reshape(128, 160))

    def wslices(wT, nslice, width):
        I, O = wT.shape
        c = I // 128
        a = wT.reshape(c, 128, nslice, width).transpose(2, 1, 0, 3)
        return np.ascontiguousarray(a.reshape(nslice, 128, c * width))

    w1f = np.asarray(inputs["L0_attn_w1"], np.float32)
    w2f = np.asarray(inputs["L0_attn_w2"], np.float32)
    w1h = np.stack([np.stack([_pcw(w1f[li][:, h * 2048:(h + 1) * 2048]) for h in range(2)])
                    for li in range(DEPTH)])
    w2h = np.stack([np.stack([_pcw(w2f[li][h * 2048:(h + 1) * 2048, :]) for h in range(2)])
                    for li in range(DEPTH)])
    common = {
        "wq": np.stack([_pcw(np.asarray(inputs["L0_attn_wq"], np.float32)[li]) for li in range(DEPTH)]),
        "wk": np.stack([_pcw(np.asarray(inputs["L0_attn_wk"], np.float32)[li]) for li in range(DEPTH)]),
        "wv": np.stack([_pcw(np.asarray(inputs["L0_attn_wv"], np.float32)[li]) for li in range(DEPTH)]),
        "wo": np.stack([_pcw(np.asarray(inputs["L0_attn_wo"], np.float32)[li]) for li in range(DEPTH)]),
        "w1": w1h,
        "w2": w2h,
        "lncp": lncp,
        "w1t": wslices(np.asarray(inputs["L0_ffn_w1_w"], np.float32).T, 32, 128),
        "w2t": wslices(np.asarray(inputs["L0_ffn_w2_w"], np.float32).T, 32, 128),
        "h1t": wslices(np.asarray(inputs["L0_ffn_h1_w"], np.float32).T, 32, 128),
        "h2t": wslices(np.asarray(inputs["L0_ffn_h2_w"], np.float32).T, 32, 128),
        "w3t": wslices(np.asarray(inputs["L0_ffn_w3_w"], np.float32).T, 8, 128),
        "scp": scp,
        "idf": np.eye(128, dtype=np.float32),
        "cbf": np.ascontiguousarray(np.stack(
            [np.full(128, 1.0 / D), np.ones(128)], axis=1).astype(ml_dtypes.bfloat16)),
        "orf": np.ones((1, 128), np.float32),
    }
    return shards, common


def kernel(**inputs):
    shards, common = prep_inputs(inputs)
    nc = build_nc()
    in_maps = [dict(common, x0=shards[c]) for c in range(NCORES)]
    res = run_bass_kernel_spmd(nc, in_maps, core_ids=list(range(NCORES)))
    outp = np.concatenate([np.asarray(res.results[c]["out"]) for c in range(NCORES)], axis=0)
    return outp.reshape(2, 16384, D)
